# version 33
# baseline (speedup 1.0000x reference)
import math
from collections import deque
from contextlib import ExitStack

import numpy as np
import concourse.bass as bass
import concourse.mybir as mybir
from concourse.bass_utils import run_bass_kernel_spmd
from concourse.alu_op_type import AluOpType as Op

F32 = mybir.dt.float32
BF16 = mybir.dt.bfloat16
I16 = mybir.dt.int16
I32 = mybir.dt.int32
FP8 = mybir.dt.float8e4
AF = mybir.ActivationFunctionType
AX = mybir.AxisListType

S = 2048
D = 1024
H = 16
HD = 64
I = 4096
T = S // 128
NC = D // 128
IC = I // 128
QB = 4
NB = T // QB
FB = 256
NFB = S // FB
TB = 512
NB3 = S // TB
CH = 8
NCH = 4
EPS = 1e-5
LN1000 = math.log(1000.0)
TWO_PI = 2.0 * math.pi
ENGINES = ("sync", "tensor", "vector", "scalar", "gpsimd")


class Sched:

    def __init__(self, nc, stack, name):
        self.nc = nc
        self.ops = {e: [] for e in ENGINES}
        self.cnt = {e: 0 for e in ENGINES}
        self.sems = {e: stack.enter_context(nc.semaphore(f"{name}_{e[:2]}"))
                     for e in ENGINES}

    def emit(self, eng, fn, deps=()):
        self.ops[eng].append((fn, tuple(d for d in deps if d is not None)))
        self.cnt[eng] += 1
        return (self, eng, self.cnt[eng])

    def build(self):
        with self.nc.Block() as blk:
            for e in ENGINES:
                lst = self.ops[e]
                if not lst:
                    continue

                def lane(E, e=e, lst=lst):
                    for fn, deps in lst:
                        for d in deps:
                            if d[0] == "SEM":
                                E.wait_ge(d[1], d[2])
                            elif d[0] is not self:
                                pass
                            elif d[1] == e and e != "gpsimd":
                                E.wait_ge(self.sems[e], d[2])
                            else:
                                E.wait_ge(self.sems[d[1]], d[2])
                        last = fn(E)
                        if e in ("vector", "scalar"):
                            if last is not None:
                                last.then_inc(self.sems[e], 1)
                            else:
                                E.drain().then_inc(self.sems[e], 1)
                        elif e in ("gpsimd", "tensor"):
                            assert last is not None, f"{e} group must return last inst"
                            last.then_inc(self.sems[e], 1)
                        else:
                            E.nop().then_inc(self.sems[e], 1)

                getattr(blk, e)(lane)


def build_nc(debug=False):
    nc = bass.Bass("TRN2", target_bir_lowering=False)
    env = {"debug": debug, "dbg": {}}
    env["x_h"] = nc.dram_tensor("x", [S, D], F32, kind="ExternalInput").ap()
    env["xT_h"] = nc.dram_tensor("xT", [D, S], F32, kind="ExternalInput").ap()
    env["wall_h"] = nc.dram_tensor("wall", [D, D + 2 * HD], F32, kind="ExternalInput").ap()
    env["woT_h"] = nc.dram_tensor("woT", [D, D], F32, kind="ExternalInput").ap()
    env["wsrc_h"] = {
        "g": nc.dram_tensor("wgT", [IC, 128, NC, 128], F32, kind="ExternalInput").ap(),
        "u": nc.dram_tensor("wuT", [IC, 128, NC, 128], F32, kind="ExternalInput").ap(),
        "d": nc.dram_tensor("wdT", [I, D], F32, kind="ExternalInput").ap(),
    }
    env["y_h"] = nc.dram_tensor("y", [S, D], F32, kind="ExternalOutput").ap()
    env["xsp_h"] = nc.dram_tensor("x_spill", [S, D], F32, kind="Internal").ap()
    env["qsp_h"] = nc.dram_tensor("q_spill", [T, 128, NC * 128], BF16, kind="Internal").ap()
    env["tern_h"] = {
        "g": nc.dram_tensor("tern_g", [IC * D * 128], FP8, kind="Internal").ap(),
        "u": nc.dram_tensor("tern_u", [IC * D * 128], FP8, kind="Internal").ap(),
        "d": nc.dram_tensor("tern_d", [I * D], FP8, kind="Internal").ap(),
    }
    if debug:
        for name, shape in [
            ("dbg_qfm", [128, NC * S]), ("dbg_kdup", [128, S]),
            ("dbg_v1", [128, T * 66]), ("dbg_x1", [S, D]),
            ("dbg_negb", [128, H]), ("dbg_qint", [128, NC * S]),
            ("dbg_pt", [128, T * 512]), ("dbg_otm", [128, QB * D]),
            ("dbg_tern", [IC * D * 128]),
        ]:
            env["dbg"][name] = nc.dram_tensor(name, shape, F32, kind="ExternalOutput").ap()

    with ExitStack() as top:
        def alloc(n, shape, dt, stk=top):
            t = stk.enter_context(nc.sbuf_tensor(n, shape, dt))
            env[n] = t
            return t

        alloc("identb", [128, 128], BF16)
        alloc("identf", [128, 128], F32)
        alloc("ones_col_b", [128, 1], BF16)
        alloc("ones_row_f", [1, 128], F32)
        alloc("onerow16", [1, 16], F32)
        alloc("delta", [128, 1], F32)
        alloc("sin_t", [128, T * 32], F32)
        alloc("cos_t", [128, T * 32], F32)
        alloc("negBtab", [128, H], F32)
        alloc("mxq", [128, T], F32)
        for w in "gud":
            alloc(f"sw_{w}", [128, 1], F32)
        alloc("beta_gu2", [128, 1], F32)
        alloc("beta_d", [128, 1], F32)
        alloc("wsum", [128, 48], F32)
        alloc("t1r", [1, 32], F32)
        alloc("t16r", [16, 4], F32)
        alloc("qn2acc", [128, H], F32)
        alloc("kn2acc", [128, 1], F32)
        for j in range(2):
            alloc(f"st6_{j}", [128, 12], F32)
            alloc(f"mv_{j}", [128, 2], F32)
            alloc(f"rstd_{j}", [128, 1], F32)
            alloc(f"negmr_{j}", [128, 1], F32)
            alloc(f"tny_{j}", [128, 16], F32)

        sX = ExitStack()
        alloc("x_tm", [128, T * D], F32, sX)

        with ExitStack() as sB:
            alloc("q_fm", [128, NC * S], BF16, sB)
            for j in range(2):
                alloc(f"wch_{j}", [128, 2048], F32, sB)
            alloc("kdup_fm", [128, S], BF16, sB)
            alloc("v1_tm", [128, T * 66], BF16, sB)

            with ExitStack() as sA:
                alloc("x_fm", [128, NC * S], BF16, sA)
                alloc("wall_sb", [128, NC * 1152], BF16, sA)
                build_p0(nc, sA, env)
                build_p1(nc, sA, env)

            with ExitStack() as sC:
                alloc("woT_sb", [128, NC * D], BF16, sC)
                alloc("pt_sb", [128, T * 512], BF16, sC)
                for j in range(2):
                    alloc(f"o_tm{j}", [128, QB * D], BF16, sC)
                alloc("o_fmt", [128, NC * 128], BF16, sC)
                alloc("tch_0", [128, 2048], FP8, sC)
                for j in range(2):
                    alloc(f"osum_{j}", [128, 4], F32, sC)
                alloc("zch2", [128, 2048], F32, sC)
                alloc("pch", [128, 2048], FP8, sC)
                alloc("ybuf", [128, D], F32, sC)
                alloc("oraw", [128, QB * 65], F32, sC)
                alloc("qst_0", [128, NC * 128], BF16, sC)
                for j in range(2):
                    alloc(f"ln2_{j}", [128, D], F32, sC)
                    alloc(f"sgq_{j}", [128, D], FP8, sC)
                alloc("i16_0", [128, D], I16, sC)
                alloc("qtm_0", [128, D], BF16, sC)
                build_p2(nc, sC, env)

        sX.close()

        with ExitStack() as sD:
            for j in range(2):
                alloc(f"qb_{j}", [128, 4 * 1024], BF16, sD)
                alloc(f"tg_{j}", [128, NC * 128], FP8, sD)
                alloc(f"tg_{2 + j}", [128, NC * 128], FP8, sD)
                alloc(f"tu_{j}", [128, NC * 128], FP8, sD)
                alloc(f"tu_{2 + j}", [128, NC * 128], FP8, sD)
                alloc(f"rb_{j}", [128, TB], F32, sD)
                alloc(f"ub_{j}", [128, TB], F32, sD)
                alloc(f"xrl_{j}", [128, D], F32, sD)
            for j in range(3):
                alloc(f"tdc_{j}", [128, CH * 512], FP8, sD)
            alloc("h_sb", [128, IC * TB], F32, sD)
            alloc("hq_sb", [128, IC * TB], BF16, sD)
            alloc("mxacc", [128, TB], F32, sD)
            for j in range(2):
                alloc(f"habs_{j}", [128, TB], F32, sD)
            alloc("csrow", [1, TB], F32, sD)
            alloc("csb", [128, TB], F32, sD)
            for j in range(2):
                alloc(f"m1b_{j}", [128, TB], F32, sD)
                alloc(f"sgb_{j}", [128, TB], FP8, sD)
                alloc(f"i16f_{j}", [128, TB], I16, sD)
            alloc("s4", [128, 32], F32, sD)
            alloc("d_tm", [128, 4 * D], F32, sD)
            alloc("ybuf3", [128, D], F32, sD)
            build_p3b(nc, sD, env)

    return nc, env["dbg"]


def build_p0(nc, stack, env):
    st = Sched(nc, stack, "p0")
    Ez = stack.enter_context
    s_dg = Ez(nc.semaphore("p0dg"))
    s_ds = Ez(nc.semaphore("p0ds"))
    ps_a = Ez(nc.psum_tensor("p0psa", [128, 512], F32))
    trows = Ez(nc.sbuf_tensor("trows", [1, T * 128], F32))
    jrow = Ez(nc.sbuf_tensor("jrow", [1, 32], F32))
    freqs = Ez(nc.sbuf_tensor("freqs", [1, 32], F32))
    rr = Ez(nc.sbuf_tensor("rr", [128, 32], F32))
    rk = Ez(nc.sbuf_tensor("rk", [128, 32], I32))
    rf = Ez(nc.sbuf_tensor("rf", [128, 32], F32))
    rf2 = Ez(nc.sbuf_tensor("rf2", [128, 32], F32))
    rs1 = Ez(nc.sbuf_tensor("rs1", [128, 32], F32))
    rs2 = Ez(nc.sbuf_tensor("rs2", [128, 32], F32))
    d15 = Ez(nc.sbuf_tensor("d15", [128, 1], F32))
    d15i = Ez(nc.sbuf_tensor("d15i", [128, 1], I16))
    g = env.__getitem__

    st.emit("sync", lambda E: E.dma_start(
        g("x_tm")[:], g("x_h").rearrange("(t p) d -> p t d", p=128)).then_inc(s_ds, 16))

    def gdmas(E):
        E.dma_start(g("x_fm")[:],
                    g("xT_h").rearrange("(c p) t -> p c t", p=128)).then_inc(s_dg, 16)
        E.dma_start(g("wall_sb")[:],
                    g("wall_h").rearrange("(c p) n -> p c n", p=128)).then_inc(s_dg, 16)
        return E.nop()

    st.emit("gpsimd", gdmas)

    def memsets(E):
        for n in ("identb", "identf", "ones_col_b", "ones_row_f", "onerow16"):
            E.memset(g(n)[:], 1.0)
        for n in ("qn2acc", "kn2acc", "wsum"):
            E.memset(g(n)[:], 0.0)
        E.memset(g("v1_tm")[:], 1.0)
        return E.memset(d15[:], 1.5)

    t_ms = st.emit("vector", memsets)

    def mkdelta(E):
        E.tensor_scalar(out=d15i[:], in0=d15[:], scalar1=1.0, scalar2=None, op0=Op.mult)
        E.drain()
        return E.tensor_scalar(out=g("delta")[:], in0=d15i[:], scalar1=-0.5,
                               scalar2=1.0, op0=Op.mult, op1=Op.add)

    st.emit("vector", mkdelta, deps=[t_ms])

    t_idb = st.emit("gpsimd", lambda E: E.affine_select(
        g("identb")[:], g("identb")[:], [[1, 128]], Op.is_equal, 0.0,
        base=0, channel_multiplier=-1), deps=[t_ms])
    t_idf = st.emit("gpsimd", lambda E: E.affine_select(
        g("identf")[:], g("identf")[:], [[1, 128]], Op.is_equal, 0.0,
        base=0, channel_multiplier=-1), deps=[t_ms])
    env["tok_ident"] = t_idf
    env["tok_identb"] = t_idb

    t_io1 = st.emit("gpsimd", lambda E: E.iota(
        trows[:], [[128, T], [1, 128]], base=0, channel_multiplier=0,
        allow_small_or_imprecise_dtypes=True))
    t_io2 = st.emit("gpsimd", lambda E: E.iota(
        jrow[:], [[1, 32]], base=0, channel_multiplier=0,
        allow_small_or_imprecise_dtypes=True))
    t_fr = st.emit("scalar", lambda E: E.activation(
        out=freqs[:], in_=jrow[:], func=AF.Exp, scale=np.float32(-LN1000 / 32.0)), deps=[t_io2])

    t_prev = None
    for i in range(T):
        t_an = st.emit("tensor", lambda E, i=i: E.matmul(
            ps_a[:, 0:32], trows[:, i * 128:(i + 1) * 128], freqs[:],
            start=True, stop=True), deps=[t_fr, t_io1, t_prev])

        def frac(E):
            E.tensor_scalar(out=rr[:], in0=ps_a[:, 0:32], scalar1=1.0 / TWO_PI,
                            scalar2=None, op0=Op.mult)
            E.drain()
            E.tensor_scalar(out=rk[:], in0=rr[:], scalar1=1.0, scalar2=None, op0=Op.mult)
            E.drain()
            E.tensor_tensor(out=rf[:], in0=rr[:], in1=rk[:], op=Op.subtract)
            E.drain()
            E.scalar_tensor_tensor(out=rf[:], in0=rf[:], scalar=0.0, in1=rf[:],
                                   op0=Op.is_lt, op1=Op.add)
            E.drain()
            E.tensor_scalar(out=rf2[:], in0=rf[:], scalar1=0.25, scalar2=None, op0=Op.add)
            E.drain()
            return E.scalar_tensor_tensor(out=rf2[:], in0=rf2[:], scalar=1.0,
                                          in1=rf2[:], op0=Op.is_lt, op1=Op.add)

        t_vf = st.emit("vector", frac, deps=[t_an])

        def sins(E):
            E.activation(out=rs1[:], in_=rf[:], func=AF.Sin, scale=np.float32(TWO_PI), bias=np.float32(-math.pi))
            E.drain()
            return E.activation(out=rs2[:], in_=rf2[:], func=AF.Sin,
                                scale=np.float32(TWO_PI),
                                bias=np.float32(-3.0 * math.pi))

        t_sn = st.emit("scalar", sins, deps=[t_vf])

        def neg(E, i=i):
            E.tensor_scalar(out=g("sin_t")[:, i * 32:(i + 1) * 32], in0=rs1[:],
                            scalar1=-1.0, scalar2=None, op0=Op.mult)
            return E.tensor_scalar(out=g("cos_t")[:, i * 32:(i + 1) * 32],
                                   in0=rs2[:], scalar1=-1.0, scalar2=None,
                                   op0=Op.mult)

        t_prev = st.emit("vector", neg, deps=[t_sn])

    def waitall(E):
        E.wait_ge(s_ds, 16)
        E.wait_ge(s_dg, 32)

    st.emit("sync", waitall, deps=[t_prev, env["tok_ident"]])
    st.build()


def build_p1(nc, stack, env):
    st = Sched(nc, stack, "p1")
    Ez = stack.enter_context
    ps_qkv = Ez(nc.psum_tensor("p1qkv", [128, 1536], F32))
    ps_tr = Ez(nc.psum_tensor("p1tr", [128, 1152], BF16))
    ps_nm = Ez(nc.psum_tensor("p1nm", [128, 512], F32))
    call = Ez(nc.sbuf_tensor("call", [1, 1152], BF16))
    negmrow_all = Ez(nc.sbuf_tensor("negmrow_all", [1, T * 128], BF16))
    rstd_all = Ez(nc.sbuf_tensor("rstd_all", [128, T], F32))
    qtq = [Ez(nc.sbuf_tensor(f"qtq{j}", [128, 1024], BF16)) for j in range(2)]
    qtkv = [Ez(nc.sbuf_tensor(f"qtkv{j}", [128, 128], F32)) for j in range(2)]
    qro = [Ez(nc.sbuf_tensor(f"qro{j}", [128, 1024], BF16)) for j in range(2)]
    kro = [Ez(nc.sbuf_tensor(f"kro{j}", [128, 128], BF16)) for j in range(2)]
    sc1 = Ez(nc.sbuf_tensor("sc1", [128, 1024], F32))
    g = env.__getitem__
    x_fm, wall_sb = g("x_fm"), g("wall_sb")
    st6 = [g("st6_0"), g("st6_1")]
    mvs = [g("mv_0"), g("mv_1")]
    tny = [g("tny_0"), g("tny_1")]

    def colsum(E):
        for c in range(NC):
            wc = wall_sb[:, c * 1152:(c + 1) * 1152]
            for lo, hi in ((0, 512), (512, 1024), (1024, 1152)):
                last = E.matmul(ps_qkv[:1, lo:hi], g("ones_col_b")[:], wc[:, lo:hi],
                                start=(c == 0), stop=(c == NC - 1))
        return last

    t_cs = st.emit("tensor", colsum)
    t_cse = st.emit("vector", lambda E: E.tensor_copy(call[:], ps_qkv[:1, 0:1152]),
                    deps=[t_cs])

    s_dwA = [Ez(nc.semaphore(f"p1wA{j}")) for j in range(2)]
    ps_fa = Ez(nc.psum_tensor("p1fa", [128, 512], F32))
    wsrc = env["wsrc_h"]
    passA = [(w, c) for w in "gud" for c in range(16)]
    tokA = {}
    t_abs = {}
    t_finred = {}
    t_findone = {}

    def wflat(w):
        a = wsrc[w]
        if len(a.shape) == 4:
            return a.rearrange("a b c d -> (a b c d)")
        return a.rearrange("a b -> (a b)")

    def emit_dmaA(k):
        if k < 0 or k >= len(passA):
            return
        w, c = passA[k]
        j = k % 2
        view = wflat(w).rearrange("(c p n) -> p c n", p=128, n=2048)[:, c, :]
        st.emit("sync", lambda E: E.dma_start(
            g(f"wch_{j}")[:], view).then_inc(s_dwA[j], 16),
            deps=[tokA.get(k - 2)])

    def emit_passA(k):
        j = k % 2
        col = k

        def red(E, j=j, col=col, k=k):
            E.wait_ge(s_dwA[j], 16 * (k // 2 + 1))
            return E.tensor_reduce(out=g("wsum")[:, col:col + 1],
                                   in_=g(f"wch_{j}")[:], axis=AX.X, op=Op.add,
                                   apply_absolute_value=True)

        tokA[k] = st.emit("vector", red)
        emit_dmaA(k + 2)

    def emit_finishA(w):
        wi = "gud".index(w)
        f1 = st.emit("vector", lambda E: E.tensor_reduce(
            out=g("tny_0")[:, 14:15], in_=g("wsum")[:, 16 * wi:16 * wi + 16],
            axis=AX.X, op=Op.add), deps=[tokA[16 * wi + c] for c in range(16)])
        f2 = st.emit("tensor", lambda E: E.transpose(
            ps_fa[0:1, 0:128], g("tny_0")[:, 14:15], g("identf")[:]),
            deps=[f1, env["tok_ident"]])

        def finv(E):
            E.tensor_reduce(out=g("t1r")[:, 0:1], in_=ps_fa[0:1, 0:128], axis=AX.X,
                            op=Op.add)
            E.drain()
            E.tensor_scalar(out=g("t1r")[:, 1:2], in0=g("t1r")[:, 0:1],
                            scalar1=1.0 / (I * D), scalar2=EPS, op0=Op.mult,
                            op1=Op.max)
            E.drain()
            return E.reciprocal(g("t1r")[:, 2:3], g("t1r")[:, 1:2])

        t_finred[w] = f3 = st.emit("vector", finv, deps=[f2])
        f4 = st.emit("tensor", lambda E: (
            E.matmul(ps_fa[:, 300:301], g("ones_row_f")[:], g("t1r")[:, 2:3],
                     start=True, stop=True),
            E.matmul(ps_fa[:, 301:302], g("ones_row_f")[:], g("t1r")[:, 1:2],
                     start=True, stop=True))[-1], deps=[f3])

        def fin5(E, w=w, wi=wi):
            E.tensor_copy(g(f"sw_{w}")[:], ps_fa[:, 300:301])
            E.drain()
            if w == "d":
                return E.tensor_copy(g("beta_d")[:], ps_fa[:, 301:302])
            return E.tensor_copy(g("tny_1")[:, 5 + wi:6 + wi], ps_fa[:, 301:302])

        t_findone[w] = st.emit("vector", fin5, deps=[f4])

    def emit_comb():
        def comb(E):
            E.tensor_tensor(out=g("beta_gu2")[:], in0=g("tny_1")[:, 5:6],
                            in1=g("tny_1")[:, 5:6], op=Op.mult)
            E.drain()
            return E.tensor_tensor(out=g("beta_gu2")[:], in0=g("beta_gu2")[:],
                                   in1=g("tny_1")[:, 6:7], op=Op.mult)

        st.emit("vector", comb, deps=[t_findone["g"], t_findone["u"]])

    emit_dmaA(0)
    emit_dmaA(1)
    pA_items = ([("A", k) for k in range(16)] + [("F", "g")]
                + [("A", k) for k in range(16, 32)] + [("F", "u")]
                + [("A", k) for k in range(32, 48)] + [("F", "d"), ("C", None)])
    pA_next = [0]

    def pA_quantum(n):
        for _ in range(n):
            if pA_next[0] < len(pA_items):
                kind, a = pA_items[pA_next[0]]
                pA_next[0] += 1
                if kind == "A":
                    emit_passA(a)
                elif kind == "F":
                    emit_finishA(a)
                else:
                    emit_comb()

    t_stats = {}
    t_nm = {}
    t_sq = {}
    t_nei = {}
    ne_chain = {"t": None}

    def emit_stats(i):
        j = i % 2

        def stats(E, i=i, j=j):
            xx = g("x_tm")[:, i * D:(i + 1) * D]
            E.bn_stats(st6[j][:, 0:6], xx[:, 0:512])
            E.bn_stats(st6[j][:, 6:12], xx[:, 512:1024])
            E.drain()
            E.bn_aggr(mvs[j][:], st6[j][:])
            E.drain()
            E.tensor_scalar(out=tny[j][:, 0:1], in0=mvs[j][:, 1:2], scalar1=EPS,
                            scalar2=None, op0=Op.add)
            E.drain()
            E.reciprocal(tny[j][:, 1:2], tny[j][:, 0:1])
            E.drain()
            return E.tensor_scalar(out=tny[j][:, 2:3], in0=mvs[j][:, 0:1],
                                   scalar1=-1.0, scalar2=None, op0=Op.mult)

        t_stats[i] = st.emit("vector", stats, deps=[t_nm.get(i - 2),
                                                    t_sq.get(i - 2)])
        t_sq[i] = st.emit("scalar", lambda E, i=i, j=j: E.activation(
            out=rstd_all[:, i:i + 1], in_=tny[j][:, 1:2], func=AF.Sqrt,
            scale=np.float32(1.0)), deps=[t_stats[i]])
        t_nm[i] = st.emit("tensor", lambda E, j=j: E.transpose(
            ps_nm[:1, 0:128], tny[j][:, 2:3], g("identf")[:]),
            deps=[t_stats[i], ne_chain["t"], env["tok_ident"]])
        t_nei[i] = ne_chain["t"] = st.emit("vector", lambda E, i=i: E.tensor_copy(
            negmrow_all[:, i * 128:(i + 1) * 128], ps_nm[:1, 0:128]),
            deps=[t_nm[i]])

    t_ev = {}
    t_rope = {}
    tre_chain = {"t": None}

    def emit_qkv_tile(i):
        j = i % 2
        t_tre = tre_chain["t"]

        def qkv(E, i=i):
            for c in range(NC):
                stat = x_fm[:, c * S + i * 128: c * S + (i + 1) * 128]
                wc = wall_sb[:, c * 1152:(c + 1) * 1152]
                for lo, hi in ((0, 512), (512, 1024), (1024, 1152)):
                    E.matmul(ps_qkv[:, lo:hi], stat, wc[:, lo:hi],
                             start=(c == 0), stop=False)
            nrow = negmrow_all[:, i * 128:(i + 1) * 128]
            for lo, hi in ((0, 512), (512, 1024), (1024, 1152)):
                last = E.matmul(ps_qkv[:, lo:hi], nrow, call[:, lo:hi],
                                start=False, stop=True)
            return last

        dq = [t_cse, t_nei[i], t_ev.get(i - 1)]
        t_mm = st.emit("tensor", qkv, deps=dq)

        def evac(E, i=i, j=j):
            for lo, hi in ((0, 512), (512, 1024)):
                E.activation(out=qtq[j][:, lo:hi], in_=ps_qkv[:, lo:hi],
                             func=AF.Identity, scale=rstd_all[:, i:i + 1])
            E.activation(out=qtkv[j][:], in_=ps_qkv[:, 1024:1152],
                         func=AF.Identity, scale=rstd_all[:, i:i + 1])

        t_ev[i] = st.emit("scalar", evac,
                          deps=[t_mm, t_sq[i], t_rope.get(i - 2)])

        def rope(E, i=i, j=j):
            kv = qtkv[j]
            sin_b = g("sin_t")[:, i * 32:(i + 1) * 32].rearrange(
                "p (o k) -> p o k", o=1).broadcast_to((128, H, 32))
            cos_b = g("cos_t")[:, i * 32:(i + 1) * 32].rearrange(
                "p (o k) -> p o k", o=1).broadcast_to((128, H, 32))
            qq = qtq[j][:, 0:1024].rearrange("p (h k) -> p h k", k=64)
            qa, qb = qq[:, :, 0:32], qq[:, :, 32:64]
            ro = qro[j][:, 0:1024].rearrange("p (h k) -> p h k", k=64)
            ra, rb2 = ro[:, :, 0:32], ro[:, :, 32:64]
            w1 = sc1[:, 0:512].rearrange("p (h k) -> p h k", k=32)
            w2 = sc1[:, 512:1024].rearrange("p (h k) -> p h k", k=32)
            E.tensor_tensor(out=w1, in0=qa, in1=cos_b, op=Op.mult)
            E.tensor_tensor(out=w2, in0=qb, in1=sin_b, op=Op.mult)
            E.drain()
            E.tensor_tensor(out=ra, in0=w1, in1=w2, op=Op.subtract)
            E.drain()
            E.tensor_tensor(out=w1, in0=qb, in1=cos_b, op=Op.mult)
            E.tensor_tensor(out=w2, in0=qa, in1=sin_b, op=Op.mult)
            E.drain()
            E.tensor_tensor(out=rb2, in0=w1, in1=w2, op=Op.add)
            E.drain()
            ka, kb = kv[:, 0:32], kv[:, 32:64]
            s1 = g("sin_t")[:, i * 32:(i + 1) * 32]
            c1 = g("cos_t")[:, i * 32:(i + 1) * 32]
            E.tensor_tensor(out=sc1[:, 0:32], in0=ka, in1=c1, op=Op.mult)
            E.tensor_tensor(out=sc1[:, 32:64], in0=kb, in1=s1, op=Op.mult)
            E.tensor_tensor(out=sc1[:, 64:96], in0=kb, in1=c1, op=Op.mult)
            E.tensor_tensor(out=sc1[:, 96:128], in0=ka, in1=s1, op=Op.mult)
            E.drain()
            E.tensor_tensor(out=kro[j][:, 0:32], in0=sc1[:, 0:32], in1=sc1[:, 32:64],
                            op=Op.subtract)
            E.tensor_tensor(out=kro[j][:, 32:64], in0=sc1[:, 64:96],
                            in1=sc1[:, 96:128], op=Op.add)
            E.drain()
            E.tensor_copy(kro[j][:, 64:128], kro[j][:, 0:64])
            E.tensor_copy(g("v1_tm")[:, i * 66:i * 66 + 64], kv[:, 64:128])
            E.drain()
            E.tensor_tensor(out=sc1[:], in0=qro[j][:], in1=qro[j][:], op=Op.mult)
            E.tensor_tensor(out=kv[:, 0:64], in0=kro[j][:, 0:64],
                            in1=kro[j][:, 0:64], op=Op.mult)
            E.drain()
            E.tensor_reduce(out=kv[:, 64:65], in_=kv[:, 0:64], axis=AX.X, op=Op.add)
            E.tensor_reduce(out=kv[:, 80:88],
                            in_=sc1[:, 0:512].rearrange("p (h k) -> p h k", k=64),
                            axis=AX.X, op=Op.add)
            E.tensor_reduce(out=kv[:, 88:96],
                            in_=sc1[:, 512:1024].rearrange("p (h k) -> p h k", k=64),
                            axis=AX.X, op=Op.add)
            E.drain()
            E.tensor_tensor(out=g("qn2acc")[:], in0=g("qn2acc")[:],
                            in1=kv[:, 80:96], op=Op.max)
            return E.tensor_tensor(out=g("kn2acc")[:], in0=g("kn2acc")[:],
                                   in1=kv[:, 64:65], op=Op.max)

        t_rope[i] = st.emit("vector", rope, deps=[t_ev[i]])

        def tposes(E, j=j):
            for c in range(NC):
                E.transpose(ps_tr[:, c * 128:(c + 1) * 128],
                            qro[j][:, c * 128:(c + 1) * 128], g("identb")[:])
            return E.transpose(ps_tr[:, 1024:1152], kro[j][:], g("identb")[:])

        t_tp = st.emit("tensor", tposes, deps=[t_rope[i], env["tok_identb"], t_tre])

        def trevac(E, i=i):
            for c in range(NC):
                E.tensor_copy(g("q_fm")[:, c * S + i * 128: c * S + (i + 1) * 128],
                              ps_tr[:, c * 128:(c + 1) * 128])
            return E.tensor_copy(g("kdup_fm")[:, i * 128:(i + 1) * 128],
                                 ps_tr[:, 1024:1152])

        tre_chain["t"] = st.emit("vector", trevac, deps=[t_tp])
        pA_quantum(4)

    for k in range(T + 2):
        if k < T:
            emit_stats(k)
        if k >= 2:
            emit_qkv_tile(k - 2)

    pA_quantum(len(pA_items))

    if env["debug"]:
        dbg = env["dbg"]
        s_dbg = Ez(nc.semaphore("p1dbg"))

        def dbgdma(E):
            E.dma_start(dbg["dbg_qfm"], g("q_fm")[:]).then_inc(s_dbg, 16)
            E.dma_start(dbg["dbg_kdup"], g("kdup_fm")[:]).then_inc(s_dbg, 16)
            E.dma_start(dbg["dbg_v1"], g("v1_tm")[:]).then_inc(s_dbg, 16)
            E.wait_ge(s_dbg, 48)
            return E.nop()

        st.emit("gpsimd", dbgdma,
                deps=[tre_chain["t"], t_rope[T - 1], t_rope[T - 2]])
    st.build()


def build_p2(nc, stack, env):
    st = Sched(nc, stack, "p2")
    Ez = stack.enter_context
    s_dwo = Ez(nc.semaphore("p2dwo"))
    s_dq = [Ez(nc.semaphore(f"p2q{j}")) for j in range(2)]
    s_dxs = Ez(nc.semaphore("p2xs"))
    s_dwA = [Ez(nc.semaphore(f"p2dA{j}")) for j in range(2)]
    s_dwB = [Ez(nc.semaphore(f"p2dB{j}")) for j in range(2)]
    ps_st = [Ez(nc.psum_tensor(f"p2st{j}", [128, 512], F32)) for j in range(2)]
    ps_o = [Ez(nc.psum_tensor(f"p2o{j}", [128, 512], F32)) for j in range(2)]
    ps_wo = Ez(nc.psum_tensor("p2wo", [128, 1024], F32))
    ps_tr = Ez(nc.psum_tensor("p2tr", [128, 1024], BF16))
    ps_bc = Ez(nc.psum_tensor("p2bc", [128, 512], F32))
    g = env.__getitem__
    st6 = [g("st6_0"), g("st6_1")]
    mvs = [g("mv_0"), g("mv_1")]
    rstd = [g("rstd_0"), g("rstd_1")]
    negmr = [g("negmr_0"), g("negmr_1")]
    tny = [g("tny_0"), g("tny_1")]
    t1r, t16r = g("t1r"), g("t16r")
    osum = [g("osum_0"), g("osum_1")]

    def _woT(E):
        E.dma_start(g("woT_sb")[:],
                    g("woT_h").rearrange("(c p) n -> p c n", p=128)).then_inc(s_dwo, 16)
        return E.nop()

    st.emit("gpsimd", _woT)
    t_ptms = st.emit("vector", lambda E: E.memset(g("pt_sb")[:], 0.0))

    tk = st.emit("tensor", lambda E: (
        E.transpose(ps_bc[0:16, 0:128], g("qn2acc")[:], g("identf")[:]),
        E.transpose(ps_bc[0:1, 128:256], g("kn2acc")[:], g("identf")[:]))[-1],
        deps=[env["tok_ident"]])

    def nb_v2(E):
        E.tensor_reduce(out=t16r[:, 0:1], in_=ps_bc[0:16, 0:128], axis=AX.X, op=Op.max)
        E.drain()
        return E.tensor_reduce(out=t1r[:, 0:1], in_=ps_bc[0:1, 128:256], axis=AX.X,
                               op=Op.max)

    tk = st.emit("vector", nb_v2, deps=[tk])
    tk2 = st.emit("tensor", lambda E: E.matmul(
        ps_bc[0:16, 256:257], g("onerow16")[:], t1r[:, 0:1], start=True, stop=True),
        deps=[tk])
    tk3 = st.emit("vector", lambda E: E.tensor_tensor(
        out=t16r[:, 1:2], in0=t16r[:, 0:1], in1=ps_bc[0:16, 256:257], op=Op.mult),
        deps=[tk2])
    tk4 = st.emit("scalar", lambda E: E.activation(
        out=t16r[:, 2:3], in_=t16r[:, 1:2], func=AF.Sqrt, scale=np.float32(1.0 / 64.0), bias=np.float32(1e-30)),
        deps=[tk3])
    tk5 = st.emit("vector", lambda E: E.tensor_scalar(
        out=t16r[:, 3:4], in0=t16r[:, 2:3], scalar1=-1.0, scalar2=None, op0=Op.mult),
        deps=[tk4])
    tk6 = st.emit("tensor", lambda E: E.transpose(
        ps_bc[0:1, 256:272], t16r[:, 3:4], g("identf")[0:16, 0:16]), deps=[tk5])
    tk7 = st.emit("vector", lambda E: E.tensor_copy(t1r[:, 8:24], ps_bc[0:1, 256:272]),
                  deps=[tk6])
    tk8 = st.emit("tensor", lambda E: E.matmul(
        ps_bc[:, 256:272], g("ones_row_f")[:], t1r[:, 8:24], start=True, stop=True),
        deps=[tk7])
    t_negb = st.emit("vector", lambda E: E.tensor_copy(g("negBtab")[:], ps_bc[:, 256:272]),
                     deps=[tk8])

    wsrc, tern = env["wsrc_h"], env["tern_h"]
    passA = [(w, c) for w in "gud" for c in range(16)]
    tokB = {}
    tokBv = {}

    def wflat(w):
        a = wsrc[w]
        if len(a.shape) == 4:
            return a.rearrange("a b c d -> (a b c d)")
        return a.rearrange("a b -> (a b)")

    def emit_passB(k):
        w, c = passA[k]
        j = k % 2
        view = wflat(w).rearrange("(c p n) -> p c n", p=128, n=2048)[:, c, :]
        tB = st.emit("sync", lambda E: E.dma_start(
            g(f"wch_{j}")[:], view).then_inc(s_dwA[j], 16),
            deps=[tokBv.get(k - 2)])

        def tzg(E, w=w, j=j, k=k):
            E.wait_ge(s_dwA[j], 16 * (k // 2 + 1))
            return E.tensor_scalar(out=g("zch2")[:], in0=g(f"wch_{j}")[:],
                                   scalar1=g(f"sw_{w}")[:], scalar2=1.0,
                                   op0=Op.mult, op1=Op.mult)

        tzg_tok = st.emit("gpsimd", tzg, deps=[tokBv.get(k - 2)])

        def tz(E, j=j, k=k):
            E.wait_ge(s_dwB[0], 16 * k)
            E.tensor_scalar(out=g("pch")[:], in0=g("zch2")[:], scalar1=0.5,
                            scalar2=None, op0=Op.is_ge)
            E.drain()
            return E.scalar_tensor_tensor(out=g("tch_0")[:], in0=g("zch2")[:],
                                          scalar=-0.5, in1=g("pch")[:], op0=Op.is_le,
                                          op1=Op.subtract)

        tokBv[k] = st.emit("vector", tz, deps=[tzg_tok])
        tern_view = tern[w].rearrange("(c p n) -> p c n", p=128, n=2048)[:, c, :]
        tokB[k] = st.emit("sync", lambda E: E.dma_start(
            tern_view, g("tch_0")[:]).then_inc(s_dwB[0], 16), deps=[tokBv[k]])

    woT_sb, pt_sb, o_fmt = g("woT_sb"), g("pt_sb"), g("o_fmt")
    o_tm = [g("o_tm0"), g("o_tm1")]
    q_fm, kdup_fm, v1_tm = g("q_fm"), g("kdup_fm"), g("v1_tm")
    x_tm, ybuf, oraw = g("x_tm"), g("ybuf"), g("oraw")

    prepseq = [("B", k) for k in range(len(passA))]
    prep_state = {"B": 0}

    def prep_quantum():
        if prep_state["B"] < len(prepseq):
            kind, a = prepseq[prep_state["B"]]
            prep_state["B"] += 1
            emit_passB(a)

    pend2 = deque()

    def make_wo_quanta(B):
        qs = []
        for q in range(QB):
            i = B * QB + q
            j = i % 2

            def qA(E=None, B=B, q=q, i=i, pb=B % 2):
                def otp(E, q=q, pb=pb):
                    for c in range(NC):
                        last = E.transpose(ps_tr[:, c * 128:(c + 1) * 128],
                                           o_tm[pb][:, q * D + c * 128:
                                                     q * D + (c + 1) * 128],
                                           g("identb")[:])
                    return last

                t_ot = st.emit("tensor", otp,
                               deps=[t_oev[(B, h, q)] for h in range(H)]
                                    + [env["tok_identb"], chain2["ps_tr_r"]])
                t_otp[(B, q)] = t_ot

                def otrev(E):
                    for c in range(NC):
                        last = E.tensor_copy(o_fmt[:, c * 128:(c + 1) * 128],
                                             ps_tr[:, c * 128:(c + 1) * 128])
                    return last

                chain2["trev"] = chain2["ps_tr_r"] = st.emit(
                    "vector", otrev, deps=[t_ot, chain2["wo"]])

            def qB_(E=None, B=B, q=q, i=i, j=j):
                def womm(E):
                    E.wait_ge(s_dwo, 16)
                    for c in range(NC):
                        stat = o_fmt[:, c * 128:(c + 1) * 128]
                        wc = woT_sb[:, c * D:(c + 1) * D]
                        E.matmul(ps_wo[:, 0:512], stat, wc[:, 0:512],
                                 start=(c == 0), stop=(c == NC - 1))
                        last = E.matmul(ps_wo[:, 512:1024], stat, wc[:, 512:1024],
                                        start=(c == 0), stop=(c == NC - 1))
                    return last

                t_wo = st.emit("tensor", womm, deps=[chain2["trev"], chain2["ye"]])
                chain2["wo"] = t_wo

                def lnstats(E, j=j):
                    E.bn_stats(st6[j][:, 0:6], ps_wo[:, 0:512])
                    E.bn_stats(st6[j][:, 6:12], ps_wo[:, 512:1024])
                    E.drain()
                    E.bn_aggr(mvs[j][:], st6[j][:])
                    E.drain()
                    E.tensor_scalar(out=tny[j][:, 0:1], in0=mvs[j][:, 1:2],
                                    scalar1=EPS, scalar2=None, op0=Op.add)
                    E.drain()
                    return E.reciprocal(tny[j][:, 1:2], tny[j][:, 0:1])

                t_ls = st.emit("vector", lnstats, deps=[t_wo, t_as3.get(i - 2)])
                chain2["rs"] = st.emit("scalar", lambda E, j=j: E.activation(
                    out=rstd[j][:], in_=tny[j][:, 1:2], func=AF.Sqrt,
                    scale=np.float32(1.0)), deps=[t_ls])

            def qC_(E=None, B=B, q=q, i=i, j=j):
                t_n2 = st.emit("vector", lambda E, j=j: E.tensor_scalar(
                    out=negmr[j][:], in0=mvs[j][:, 0:1], scalar1=rstd[j][:],
                    scalar2=-1.0, op0=Op.mult, op1=Op.mult), deps=[chain2["rs"]])

                def yev(E, j=j):
                    return E.activation(out=ybuf[:], in_=ps_wo[:, 0:1024],
                                        func=AF.Identity,
                                        scale=rstd[j][:], bias=negmr[j][:])

                chain2["ye"] = t_ye = st.emit("scalar", yev,
                                              deps=[t_n2, chain2["resid"]])

                def resid(E, i=i):
                    return E.tensor_tensor(out=x_tm[:, i * D:(i + 1) * D],
                                           in0=x_tm[:, i * D:(i + 1) * D],
                                           in1=ybuf[:], op=Op.add)

                t_resid[i] = st.emit("vector", resid, deps=[t_ye])
                chain2["resid"] = t_resid[i]

                def xspill(E, i=i):
                    E.dma_start(g("xsp_h").rearrange("(t p) d -> p t d", p=128)[:, i, :],
                                x_tm[:, i * D:(i + 1) * D]).then_inc(s_dxs, 16)

                st.emit("sync", xspill, deps=[t_resid[i]])

            def qD_(E=None, B=B, q=q, i=i, j=j):
                def lnq_stats(E, i=i, j=j):
                    xx = x_tm[:, i * D:(i + 1) * D]
                    E.bn_stats(st6[j][:, 0:6], xx[:, 0:512])
                    E.bn_stats(st6[j][:, 6:12], xx[:, 512:1024])
                    E.drain()
                    E.bn_aggr(mvs[j][:], st6[j][:])
                    E.drain()
                    E.tensor_scalar(out=tny[j][:, 0:1], in0=mvs[j][:, 1:2],
                                    scalar1=EPS, scalar2=None, op0=Op.add)
                    E.drain()
                    return E.reciprocal(tny[j][:, 1:2], tny[j][:, 0:1])

                t_q_ls = st.emit("vector", lnq_stats,
                                 deps=[t_resid[i], t_as3.get(i - 2)])
                t_q_rs = st.emit("scalar", lambda E, j=j: E.activation(
                    out=rstd[j][:], in_=tny[j][:, 1:2], func=AF.Sqrt,
                    scale=np.float32(1.0)), deps=[t_q_ls])
                chain2["qnm"] = st.emit("vector", lambda E, j=j: E.tensor_scalar(
                    out=negmr[j][:], in0=mvs[j][:, 0:1], scalar1=rstd[j][:],
                    scalar2=-1.0, op0=Op.mult, op1=Op.mult), deps=[t_q_rs])

            def qE_(E=None, B=B, q=q, i=i, j=j):
                t_q_l2 = st.emit("scalar", lambda E, i=i, j=j: E.activation(
                    out=g(f"ln2_{j}")[:], in_=x_tm[:, i * D:(i + 1) * D],
                    func=AF.Identity, scale=rstd[j][:], bias=negmr[j][:]),
                    deps=[chain2["qnm"]])

                def lnq_scale(E, i=i, j=j):
                    E.tensor_reduce(out=tny[j][:, 2:3], in_=g(f"ln2_{j}")[:],
                                    axis=AX.X, op=Op.max,
                                    apply_absolute_value=True)
                    E.drain()
                    E.tensor_scalar(out=g("mxq")[:, i:i + 1], in0=tny[j][:, 2:3],
                                    scalar1=EPS, scalar2=None, op0=Op.max)
                    E.drain()
                    E.reciprocal(tny[j][:, 3:4], g("mxq")[:, i:i + 1])
                    E.drain()
                    return E.tensor_scalar(out=tny[j][:, 4:5], in0=tny[j][:, 3:4],
                                           scalar1=127.0, scalar2=None,
                                           op0=Op.mult)

                chain2["qqs"] = st.emit("vector", lnq_scale, deps=[t_q_l2])

            def qF_(E=None, B=B, q=q, i=i, j=j):
                def lnq_absign(E, j=j):
                    E.activation(out=g(f"sgq_{j}")[:], in_=g(f"ln2_{j}")[:],
                                 func=AF.Sign)
                    E.drain()
                    return E.activation(out=g(f"ln2_{j}")[:], in_=g(f"ln2_{j}")[:],
                                        func=AF.Abs, scale=tny[j][:, 4:5])

                t_as3[i] = st.emit("scalar", lnq_absign,
                                   deps=[chain2["qqs"], t_rq3.get(i - 2)])

                def lnq_round(E, j=j):
                    E.tensor_scalar(out=g("i16_0")[:], in0=g(f"ln2_{j}")[:],
                                    scalar1=g("delta")[:], scalar2=None, op0=Op.add)
                    E.drain()
                    return E.tensor_tensor(out=g("qtm_0")[:], in0=g("i16_0")[:],
                                           in1=g(f"sgq_{j}")[:], op=Op.mult)

                t_rq3[i] = st.emit("vector", lnq_round,
                                   deps=[t_as3[i], t_qtp3.get(i - 1)])

            def qG_(E=None, B=B, q=q, i=i, j=j):
                def lnq_tp(E):
                    for c in range(NC):
                        last = E.transpose(ps_tr[:, c * 128:(c + 1) * 128],
                                           g("qtm_0")[:, c * 128:(c + 1) * 128],
                                           g("identb")[:])
                    return last

                t_qtp3[i] = st.emit("tensor", lnq_tp,
                                    deps=[t_rq3[i], chain2["ps_tr_r"],
                                          env["tok_identb"]])

                def lnq_trev(E, i=i):
                    if i >= 1:
                        E.wait_ge(s_dq[0], 16 * i)
                    for c in range(NC):
                        last = E.tensor_copy(g("qst_0")[:, c * 128:(c + 1) * 128],
                                             ps_tr[:, c * 128:(c + 1) * 128])
                    return last

                t_qtrev3[i] = st.emit("vector", lnq_trev, deps=[t_qtp3[i]])
                chain2["ps_tr_r"] = t_qtrev3[i]

                st.emit("sync", lambda E, i=i: E.dma_start(
                    g("qsp_h")[i], g("qst_0")[:]).then_inc(s_dq[0], 16),
                    deps=[t_qtrev3[i]])

            qs.append([qA, qB_, qC_, qD_, qE_, qF_, qG_])
        pat = [(0, 0), (0, 1), (0, 2), (1, 0), (0, 3), (1, 1), (0, 4), (1, 2),
               (0, 5), (1, 3), (0, 6), (1, 4), (2, 0), (1, 5), (2, 1), (1, 6),
               (2, 2), (3, 0), (2, 3), (3, 1), (2, 4), (3, 2), (2, 5), (3, 3),
               (2, 6), (3, 4), (3, 5), (3, 6)]
        return [qs[q][k] for q, k in pat]

    t_exp = {}
    t_pv = {}
    t_oev = {}
    t_cpd = {}
    t_as3 = {}
    t_rq3 = {}
    t_qtp3 = {}
    t_qtrev3 = {}
    t_otp = {}
    t_resid = {}
    last_pv_block = {}
    chain2 = {"trev": None, "ye": None, "resid": None, "rs": None,
              "qnm": None, "qqs": None, "wo": None, "ps_tr_r": None}

    for B in range(NB):
        nkt = QB * (B + 1)
        for h in range(H):
            hb = 64 * (h % 2)
            chunk = h // 2
            for kt in range(nkt):
                lo = max(kt * 128, B * 512)
                hi = (B + 1) * 512
                segw = hi - lo
                par = kt % 3

                def strip(E, kt=kt, lo=lo, hi=hi, segw=segw, par=par, hb=hb, chunk=chunk):
                    dst = ps_bc if par == 2 else ps_st[par]
                    return E.matmul(dst[:, 0:segw],
                                    kdup_fm[hb:hb + 64, kt * 128:(kt + 1) * 128],
                                    q_fm[hb:hb + 64, chunk * S + lo: chunk * S + hi],
                                    start=True, stop=True)

                dstrip = []
                if kt >= 3:
                    dstrip.append(t_exp[(B, h, kt - 3)])
                elif h > 0:
                    dstrip.append(t_exp[(B, h - 1, nkt - 3 + kt)])
                elif B > 0:
                    dstrip.append(t_exp[(B - 1, H - 1, QB * B - 3 + kt)])
                elif par == 2:
                    dstrip.append(t_negb)
                t_s = st.emit("tensor", strip, deps=dstrip)

                def expf(E, h=h, kt=kt, lo=lo, segw=segw, par=par, B=B):
                    off = kt * 512 + (lo - B * 512)
                    src_ps = ps_bc if par == 2 else ps_st[par]
                    return E.activation(out=pt_sb[:, off:off + segw],
                                        in_=src_ps[:, 0:segw],
                                        func=AF.Exp, scale=np.float32(0.125),
                                        bias=g("negBtab")[:, h:h + 1])

                dexp = [t_s]
                if B == 0 and h == 0 and kt == 0:
                    dexp.append(t_negb)
                    dexp.append(t_ptms)
                if h > 0:
                    dexp.append(t_pv[(B, h - 1, QB - 1)])
                elif B > 0:
                    dexp.append(t_pv[(B - 1, H - 1, QB - 1)])
                t_exp[(B, h, kt)] = st.emit("scalar", expf, deps=dexp)

            t_masks = []
            for qd in range(QB):
                def mask(E, B=B, qd=qd):
                    off0 = (B * QB) * 512 + (B * QB * 128 - B * 512) + 640 * qd
                    ap = bass.AP(pt_sb.tensor if hasattr(pt_sb, "tensor") else pt_sb,
                                 off0, [[T * 512, 128], [1, 128]])
                    return E.affine_select(ap, ap, [[1, 128]], Op.is_ge, 0.0,
                                           base=0, channel_multiplier=-1)

                t_masks.append(st.emit("gpsimd", mask,
                                       deps=[t_exp[(B, h, B * QB + qd)]]))

            for q in range(QB):
                qt = B * QB + q
                par = q % 2

                def pv(E, B=B, qt=qt, par=par):
                    for kt in range(qt + 1):
                        off = kt * 512 + (qt * 128 - B * 512)
                        last = E.matmul(ps_o[par][:, 0:65], pt_sb[:, off:off + 128],
                                        v1_tm[:, kt * 66:kt * 66 + 65],
                                        start=(kt == 0), stop=(kt == qt))
                    return last

                dpv = t_masks[:q + 1] + [t_exp[(B, h, qt)]]
                if q >= 2:
                    dpv.append(t_cpd.get((B, h, q - 2)))
                elif h > 0:
                    dpv.append(t_cpd.get((B, h - 1, q + 2)))
                elif B > 0:
                    dpv.append(t_cpd.get((B - 1, H - 1, q + 2)))
                t_pv[(B, h, q)] = st.emit("tensor", pv, deps=dpv)

                dcp = [t_pv[(B, h, q)]]
                if h > 0:
                    dcp.append(t_oev.get((B, h - 1, q)))
                elif B > 0:
                    dcp.append(t_oev.get((B - 1, H - 1, q)))
                t_cpd[(B, h, q)] = st.emit("vector", lambda E, q=q, par=par:
                    E.tensor_copy(oraw[:, q * 65:(q + 1) * 65], ps_o[par][:, 0:65]),
                    deps=dcp)

            t_rc = st.emit("vector", lambda E: E.reciprocal(
                osum[1][:, 0:4], oraw[:].rearrange("p (q c) -> p q c", c=65)[:, :, 64]),
                deps=[t_cpd[(B, h, q)] for q in range(QB)])

            for q in range(QB):
                def oev(E, h=h, q=q, pb=B % 2):
                    return E.tensor_scalar(
                        out=o_tm[pb][:, q * D + h * 64: q * D + (h + 1) * 64],
                        in0=oraw[:, q * 65:q * 65 + 64],
                        scalar1=osum[1][:, q:q + 1], scalar2=None, op0=Op.mult)

                doev = [t_rc]
                if B > 1 and h == 0:
                    doev.append(t_otp.get((B - 2, q)))
                t_oev[(B, h, q)] = st.emit("vector", oev, deps=doev)

            prep_quantum()
            for _ in range(2 if h < 13 else 1):
                if pend2:
                    pend2.popleft()()

        last_pv_block[B] = t_pv[(B, H - 1, QB - 1)]

        pend2.extend(make_wo_quanta(B))

    while pend2:
        pend2.popleft()()
    while prep_state["B"] < len(prepseq):
        prep_quantum()

    def waitB(E):
        E.wait_ge(s_dwB[0], 16 * 48)
        E.wait_ge(s_dxs, 16 * T)
        E.wait_ge(s_dq[0], 16 * T)

    st.emit("sync", waitB, deps=[tokB[len(passA) - 1], tokB[len(passA) - 2]])

    if env["debug"]:
        dbg = env["dbg"]
        s_dbg = Ez(nc.semaphore("p2dbg"))

        def dbgdma(E):
            E.dma_start(dbg["dbg_x1"].rearrange("(t p) d -> p t d", p=128),
                        x_tm[:].rearrange("p (t d) -> p t d", d=D)).then_inc(s_dbg, 16)
            E.dma_start(dbg["dbg_negb"], g("negBtab")[:]).then_inc(s_dbg, 16)
            E.dma_start(dbg["dbg_pt"], pt_sb[:]).then_inc(s_dbg, 16)
            E.dma_start(dbg["dbg_otm"], o_tm[1][:]).then_inc(s_dbg, 16)
            E.dma_start(dbg["dbg_tern"], tern["g"]).then_inc(s_dbg, 16)
            E.wait_ge(s_dbg, 80)
            return E.nop()

        st.emit("gpsimd", dbgdma,
                deps=[t_resid[T - 1], t_negb, tokB[len(passA) - 1]])

    env["tok_resid_p2"] = dict(t_resid)
    st.build()


def build_p3b(nc, stack, env):
    st = Sched(nc, stack, "p3b")
    Ez = stack.enter_context
    s_qb = [Ez(nc.semaphore(f"p3bq{j}")) for j in range(2)]
    s_dt = [Ez(nc.semaphore(f"p3bt{j}")) for j in range(4)]
    s_tdc = [Ez(nc.semaphore(f"p3bd{j}")) for j in range(3)]
    s_dx = [Ez(nc.semaphore(f"p3bx{j}")) for j in range(2)]
    s_dy = Ez(nc.semaphore("p3by"))
    ps_gu = [Ez(nc.psum_tensor(f"p3gu{j}", [128, 512], F32)) for j in range(4)]
    ps_d = [Ez(nc.psum_tensor(f"p3d{j}", [128, 512], F32)) for j in range(2)]
    ps_sc = [Ez(nc.psum_tensor(f"p3sc{j}", [128, 512], F32)) for j in range(2)]
    g = env.__getitem__
    st6 = [g("st6_0"), g("st6_1")]
    mvs = [g("mv_0"), g("mv_1")]
    rstd = [g("rstd_0"), g("rstd_1")]
    negmr = [g("negmr_0"), g("negmr_1")]
    tny = [g("tny_0"), g("tny_1")]
    tern = env["tern_h"]
    s4 = g("s4")

    t_qdma = {}
    t_tdma = {}
    t_mm_gu = {}
    t_ev = {}
    t_h1 = {}
    t_h = {}
    t_mx = {}
    t_habs = {}
    t_mt = {}
    t_cs4 = {}
    t_csrow = {}
    t_bcm = {}
    t_bce = {}
    t_m1 = {}
    t_sg = {}
    t_rds = {}
    t_fill = {}
    t_fev = {}
    t_fev2 = {}
    t_y3 = {}
    t_store = {}
    t_xr = {}
    chain = {"r3b": None}
    nqb = [0, 0]
    ndt = [0] * 4
    ndc = [0] * 3
    pend = deque()

    def emit_gu(b3, ic):
        par = ic % 4
        par_b = b3 % 2
        jj = ic % 2

        def dma_gu(E, ic=ic, par=par):
            gv = tern["g"].rearrange("(a p n) -> a p n", p=128, n=NC * 128)[ic]
            uv = tern["u"].rearrange("(a p n) -> a p n", p=128, n=NC * 128)[ic]
            E.dma_start(g(f"tg_{par}")[:], gv).then_inc(s_dt[par], 16)
            E.dma_start(g(f"tu_{par}")[:], uv).then_inc(s_dt[par], 16)

        t_tdma[(b3, ic)] = st.emit("sync", dma_gu,
                                   deps=[t_mm_gu.get((b3, ic - 4)),
                                         t_mm_gu.get((b3 - 1, IC - 4 + ic))
                                         if ic < 4 else None])
        ndt[par] += 2

        def mm_gu(E, par=par, par_b=par_b, dtt=16 * ndt[par], qbt=16 * nqb[par_b]):
            pp = par % 2
            E.wait_ge(s_dt[par], dtt)
            E.wait_ge(s_qb[par_b], qbt)
            qb = g(f"qb_{par_b}")[:].rearrange("p (t n) -> p t n", n=1024)
            for c2 in range(NC):
                mov = qb[:, :, c2 * 128:(c2 + 1) * 128]
                E.matmul(ps_gu[pp][:, 0:TB], g(f"tg_{par}")[:, c2 * 128:(c2 + 1) * 128],
                         mov, start=(c2 == 0), stop=(c2 == NC - 1))
                last = E.matmul(ps_gu[2 + pp][:, 0:TB],
                                g(f"tu_{par}")[:, c2 * 128:(c2 + 1) * 128],
                                mov, start=(c2 == 0), stop=(c2 == NC - 1))
            return last

        dmm = list(t_ev.get((b3, ic - 2)) or ())
        if ic < 2:
            dmm += list(t_ev.get((b3 - 1, IC - 2 + ic)) or ())
        t_mm_gu[(b3, ic)] = st.emit("tensor", mm_gu, deps=dmm)

        dev = [t_mm_gu[(b3, ic)], t_h.get((b3, ic - 2))]
        if ic < 2:
            dev.append(t_h.get((b3 - 1, IC - 2 + ic)))
        t_eg = st.emit("scalar", lambda E, pp=jj: E.activation(
            out=g(f"rb_{pp}")[:], in_=ps_gu[pp][:, 0:TB], func=AF.Relu,
            scale=np.float32(-1.0)), deps=dev)
        t_eu = st.emit("scalar", lambda E, pp=jj: E.activation(
            out=g(f"ub_{pp}")[:], in_=ps_gu[2 + pp][:, 0:TB], func=AF.Identity),
            deps=dev)
        t_ev[(b3, ic)] = (t_eg, t_eu)

        t_h1[(b3, ic)] = st.emit("gpsimd", lambda E, pp=jj: E.tensor_tensor(
            out=g(f"ub_{pp}")[:], in0=g(f"rb_{pp}")[:], in1=g(f"ub_{pp}")[:],
            op=Op.mult), deps=[t_eg, t_eu, t_m1.get((b3 - 1, ic))])
        t_h[(b3, ic)] = st.emit("gpsimd", lambda E, ic=ic, pp=jj: E.tensor_tensor(
            out=g("h_sb")[:, ic * TB:(ic + 1) * TB], in0=g(f"ub_{pp}")[:],
            in1=g(f"rb_{pp}")[:], op=Op.mult), deps=[t_h1[(b3, ic)]])

        dab = [t_h[(b3, ic)], t_mx.get((b3, ic - 2))]
        if ic < 2:
            dab.append(t_mx.get((b3 - 1, IC - 2 + ic)))
        t_habs[(b3, ic)] = st.emit("scalar", lambda E, ic=ic: E.activation(
            out=g(f"habs_{ic % 2}")[:], in_=g("h_sb")[:, ic * TB:(ic + 1) * TB],
            func=AF.Abs), deps=dab)

    def emit_mxup(b3, ic):
        dmx = [t_habs[(b3, ic)], t_mx.get((b3, ic - 1))]
        pr = ic % 2
        if ic == 0:
            dmx.append(t_mt.get(b3 - 1))
            fn = lambda E, pr=pr: E.tensor_copy(g("mxacc")[:], g(f"habs_{pr}")[:])
        else:
            fn = lambda E, pr=pr: E.tensor_tensor(out=g("mxacc")[:], in0=g("mxacc")[:],
                                                  in1=g(f"habs_{pr}")[:], op=Op.max)
        t_mx[(b3, ic)] = st.emit("vector", fn, deps=dmx)

    def emit_scales(b3):
        def mxtp(E):
            for q4 in range(4):
                last = E.transpose(ps_sc[0][:, q4 * 128:(q4 + 1) * 128],
                                   g("mxacc")[:, q4 * 128:(q4 + 1) * 128],
                                   g("identf")[:])
            return last

        t_mt[b3] = st.emit("tensor", mxtp,
                           deps=[t_mx[(b3, IC - 1)], env["tok_ident"],
                                 t_bce.get(b3 - 1)] + list(t_ev[(b3, IC - 1)]))

        def mx4(E):
            for tt in range(4):
                last = E.tensor_reduce(out=s4[:, tt:tt + 1],
                                       in_=ps_sc[0][:, tt * 128:(tt + 1) * 128],
                                       axis=AX.X, op=Op.max)
            return last

        t_m4 = st.emit("vector", mx4, deps=[t_mt[b3], t_cs4.get(b3 - 1)])

        def csc4(E, b3=b3):
            mq = g("mxq")[:, 4 * b3:4 * b3 + 4]
            E.tensor_scalar(out=s4[:, 4:8], in0=mq, scalar1=1.0 / 127.0,
                            scalar2=None, op0=Op.mult)
            E.drain()
            E.tensor_tensor(out=s4[:, 8:12], in0=s4[:, 4:8], in1=s4[:, 4:8], op=Op.mult)
            E.drain()
            E.tensor_tensor(out=s4[:, 8:12], in0=s4[:, 8:12], in1=s4[:, 4:8], op=Op.mult)
            E.drain()
            E.tensor_scalar(out=s4[:, 8:12], in0=s4[:, 8:12], scalar1=g("beta_gu2")[:],
                            scalar2=None, op0=Op.mult)
            E.drain()
            E.tensor_tensor(out=s4[:, 4:8], in0=s4[:, 8:12], in1=s4[:, 0:4], op=Op.mult)
            E.drain()
            E.tensor_scalar(out=s4[:, 4:8], in0=s4[:, 4:8], scalar1=EPS,
                            scalar2=None, op0=Op.max)
            E.drain()
            E.reciprocal(s4[:, 0:4], s4[:, 4:8])
            E.drain()
            E.tensor_scalar(out=s4[:, 8:12], in0=s4[:, 8:12], scalar1=127.0,
                            scalar2=None, op0=Op.mult)
            E.drain()
            E.tensor_tensor(out=s4[:, 12:16], in0=s4[:, 8:12], in1=s4[:, 0:4],
                            op=Op.mult)
            E.drain()
            PA = 16 + 8 * (b3 % 2)
            E.tensor_scalar(out=s4[:, PA:PA + 4], in0=s4[:, 4:8],
                            scalar1=g("beta_d")[:], scalar2=1.0 / 127.0,
                            op0=Op.mult, op1=Op.mult)
            E.drain()
            return E.tensor_tensor(out=s4[:, PA + 4:PA + 8], in0=s4[:, PA:PA + 4],
                                   in1=s4[:, PA:PA + 4], op=Op.mult)

        dcs = [t_m4, t_csrow.get(b3 - 1), t_y3.get((b3 - 2, 3))]
        t_cs4[b3] = st.emit("vector", csc4, deps=dcs)

        def cstp(E):
            for tt in range(4):
                last = E.transpose(ps_sc[1][:1, tt * 128:(tt + 1) * 128],
                                   s4[:, 12 + tt:13 + tt], g("identf")[:])
            return last

        t_ctp = st.emit("tensor", cstp, deps=[t_cs4[b3], t_csrow.get(b3 - 1)])
        t_csrow[b3] = st.emit("vector", lambda E: E.tensor_copy(
            g("csrow")[:], ps_sc[1][:1, 0:512]), deps=[t_ctp, t_bcm.get(b3 - 1)])
        t_bcm[b3] = st.emit("tensor", lambda E: E.matmul(
            ps_sc[0][:, 0:TB], g("ones_row_f")[:], g("csrow")[:],
            start=True, stop=True), deps=[t_csrow[b3], t_m4])
        t_bce[b3] = st.emit("vector", lambda E: E.tensor_copy(
            g("csb")[:], ps_sc[0][:, 0:TB]),
            deps=[t_bcm[b3], t_m1.get((b3 - 1, IC - 1))])

    def emit_m1_sg(b3, ic):
        pr = ic % 2
        dm1 = [t_bce[b3], t_rds.get((b3, ic - 2))]
        if ic < 2:
            dm1.append(t_rds.get((b3 - 1, IC - 2 + ic)))
        t_m1[(b3, ic)] = st.emit("vector", lambda E, ic=ic, pr=pr: E.tensor_tensor(
            out=g(f"m1b_{pr}")[:], in0=g("h_sb")[:, ic * TB:(ic + 1) * TB],
            in1=g("csb")[:], op=Op.mult), deps=dm1)

        def sgabs(E, pr=pr):
            E.activation(out=g(f"sgb_{pr}")[:], in_=g(f"m1b_{pr}")[:], func=AF.Sign)
            E.drain()
            return E.activation(out=g(f"m1b_{pr}")[:], in_=g(f"m1b_{pr}")[:],
                                func=AF.Abs)

        t_sg[(b3, ic)] = st.emit("scalar", sgabs, deps=[t_m1[(b3, ic)]])

    def emit_rnd2(b3, ic):
        pr = ic % 2

        def rnd2(E, ic=ic, pr=pr):
            E.tensor_scalar(out=g(f"i16f_{pr}")[:], in0=g(f"m1b_{pr}")[:],
                            scalar1=g("delta")[:], scalar2=None, op0=Op.add)
            E.drain()
            return E.tensor_tensor(out=g("hq_sb")[:, ic * TB:(ic + 1) * TB],
                                   in0=g(f"i16f_{pr}")[:], in1=g(f"sgb_{pr}")[:],
                                   op=Op.mult)

        t_rds[(b3, ic)] = st.emit("vector", rnd2,
                                  deps=[t_m1[(b3, ic)], t_sg[(b3, ic)],
                                        t_fill.get((b3 - 1, ic // CH, 3, 1))])

    def emit_tdc_dma(j):
        sl = j % 3
        dprev = None
        if j >= 3:
            bp, rp = divmod(j - 3, 8)
            cp, dhp = divmod(rp, 2)
            dprev = t_fill.get((bp, cp, 3, dhp))
        bj, rj = divmod(j, 8)
        c, dh = divmod(rj, 2)

        def dmaf(E, c=c, dh=dh, sl=sl):
            dv = tern["d"].rearrange("(h c p n) -> h c p n",
                                     c=NCH, p=128, n=CH * 512)[dh, c]
            E.dma_start(g(f"tdc_{sl}")[:], dv).then_inc(s_tdc[sl], 16)

        st.emit("sync", dmaf, deps=[dprev])
        ndc[sl] += 1

    def emit_fill(b3, c, tg, dh):
        j = b3 * 8 + c * 2 + dh
        sl = j % 3
        f = c * 8 + tg * 2 + dh
        bank = f % 2

        def mm(E, c=c, tg=tg, sl=sl, bank=bank, dct=16 * ndc[sl]):
            E.wait_ge(s_tdc[sl], dct)
            for icl in range(CH):
                ic = c * CH + icl
                last = E.matmul(
                    ps_d[bank][:, 0:TB],
                    g("hq_sb")[:, ic * TB + tg * 128: ic * TB + (tg + 1) * 128],
                    g(f"tdc_{sl}")[:, icl * 512:(icl + 1) * 512],
                    start=(icl == 0), stop=(icl == CH - 1))
            return last

        dmm = [t_rds[(b3, c * CH + CH - 1)], t_fev.get((b3, f - 2))]
        if f < 2:
            dmm.append(t_fev.get((b3 - 1, 30 + f)))
        t_fill[(b3, c, tg, dh)] = tok = st.emit("tensor", mm, deps=dmm)

        def ev(E, c=c, tg=tg, dh=dh, bank=bank):
            dst = g("d_tm")[:, tg * D + dh * 512: tg * D + dh * 512 + 512]
            if c == 0:
                return E.tensor_copy(dst, ps_d[bank][:, 0:TB])
            return E.tensor_tensor(out=dst, in0=dst, in1=ps_d[bank][:, 0:TB],
                                   op=Op.add)

        dev = [tok]
        if c == 0:
            dev.append(t_y3.get((b3 - 1, tg)))
        else:
            dev.append(t_fev2.get((b3, c - 1, tg, dh)))
        t_fev[(b3, f)] = t_fev2[(b3, c, tg, dh)] = st.emit("vector", ev, deps=dev)

        if tg == 3 and j + 3 < NB3 * 8:
            emit_tdc_dma(j + 3)

    def make_tail_quanta(b3):
        qs = []
        for tt in range(4):
            i = 4 * b3 + tt
            j = i % 2
            hold = {}

            def qa(b3=b3, tt=tt, i=i, j=j, hold=hold):
                def xdma(E, i=i, j=j):
                    if i >= 2:
                        E.wait_ge(s_dy, 16 * i)
                    E.dma_start(g(f"xrl_{j}")[:],
                                g("xsp_h").rearrange("(t p) d -> p t d", p=128)[:, i, :]
                                ).then_inc(s_dx[j], 16)

                t_xr[(b3, tt)] = st.emit("sync", xdma, deps=[t_store.get(i - 2)])

                def lnst3(E, tt=tt, j=j, PA=16 + 8 * (b3 % 2)):
                    dd = g("d_tm")[:, tt * D:(tt + 1) * D]
                    E.bn_stats(st6[j][:, 0:6], dd[:, 0:512])
                    E.bn_stats(st6[j][:, 6:12], dd[:, 512:1024])
                    E.drain()
                    E.bn_aggr(mvs[j][:], st6[j][:])
                    E.drain()
                    E.tensor_scalar(out=tny[j][:, 0:1], in0=mvs[j][:, 1:2],
                                    scalar1=s4[:, PA + 4 + tt:PA + 5 + tt],
                                    scalar2=EPS, op0=Op.mult, op1=Op.add)
                    E.drain()
                    return E.reciprocal(tny[j][:, 1:2], tny[j][:, 0:1])

                t_l3 = st.emit("vector", lnst3,
                               deps=[t_fev2[(b3, NCH - 1, tt, 0)],
                                     t_fev2[(b3, NCH - 1, tt, 1)], t_cs4[b3]])
                t_r3 = st.emit("scalar", lambda E, j=j: E.activation(
                    out=rstd[j][:], in_=tny[j][:, 1:2], func=AF.Sqrt,
                    scale=np.float32(1.0)), deps=[t_l3])
                hold["toks"] = (t_l3, t_r3)

            def qb_(b3=b3, tt=tt, i=i, j=j, hold=hold):
                t_l3, t_r3 = hold["toks"]

                def n3f(E, j=j, tt=tt, PA=16 + 8 * (b3 % 2)):
                    E.tensor_tensor(out=tny[j][:, 2:3], in0=rstd[j][:],
                                    in1=s4[:, PA + tt:PA + tt + 1], op=Op.mult)
                    E.drain()
                    return E.tensor_scalar(out=negmr[j][:], in0=mvs[j][:, 0:1],
                                           scalar1=tny[j][:, 2:3], scalar2=-1.0,
                                           op0=Op.mult, op1=Op.mult)

                t_n3 = st.emit("vector", n3f, deps=[t_r3])
                t_y3[(b3, tt)] = st.emit("scalar", lambda E, tt=tt, j=j: E.activation(
                    out=g("ybuf3")[:], in_=g("d_tm")[:, tt * D:(tt + 1) * D],
                    func=AF.Identity, scale=tny[j][:, 2:3], bias=negmr[j][:]),
                    deps=[t_n3, chain["r3b"]])

                def resid3(E, j=j, xrt=16 * ((i // 2) + 1)):
                    E.wait_ge(s_dx[j], xrt)
                    return E.tensor_tensor(out=g(f"xrl_{j}")[:], in0=g(f"xrl_{j}")[:],
                                           in1=g("ybuf3")[:], op=Op.add)

                t_r3b = st.emit("vector", resid3, deps=[t_y3[(b3, tt)], t_xr[(b3, tt)]])
                chain["r3b"] = t_r3b

                def store(E, i=i, j=j):
                    E.dma_start(env["y_h"].rearrange("(t p) d -> p t d", p=128)[:, i, :],
                                g(f"xrl_{j}")[:]).then_inc(s_dy, 16)

                t_store[i] = st.emit("sync", store, deps=[t_r3b])

            qs.append(qa)
            qs.append(qb_)
        return qs

    for j in range(3):
        emit_tdc_dma(j)

    for b3 in range(NB3):
        par_b = b3 % 2

        def qdma(E, b3=b3, par_b=par_b):
            for tt in range(4):
                E.dma_start(g(f"qb_{par_b}")[:, tt * 1024:(tt + 1) * 1024],
                            g("qsp_h")[4 * b3 + tt]).then_inc(s_qb[par_b], 16)

        t_qdma[b3] = st.emit("sync", qdma, deps=[t_mm_gu.get((b3 - 2, IC - 1))])
        nqb[par_b] += 4

        fq = deque((b3 - 1, c, tg, dh) for c in range(NCH - 1)
                   for tg in range(4) for dh in range(2)) if b3 > 0 else deque()
        for ic in range(IC):
            if b3 > 0:
                emit_m1_sg(b3 - 1, ic)
            emit_gu(b3, ic)
            if b3 > 0 and ic > 0:
                emit_rnd2(b3 - 1, ic - 1)
            if ic > 0:
                emit_mxup(b3, ic - 1)
            if pend:
                pend.popleft()()
            if fq and fq[0][1] * CH + CH <= ic:
                emit_fill(*fq.popleft())
        emit_mxup(b3, IC - 1)
        while pend:
            pend.popleft()()
        if b3 > 0:
            emit_rnd2(b3 - 1, IC - 1)
            while fq:
                emit_fill(*fq.popleft())
            for tg in range(2):
                for dh in range(2):
                    emit_fill(b3 - 1, NCH - 1, tg, dh)
            emit_scales(b3)
            for tg in range(2, 4):
                for dh in range(2):
                    emit_fill(b3 - 1, NCH - 1, tg, dh)
            pend.extend(make_tail_quanta(b3 - 1))
        else:
            emit_scales(b3)

    b3 = NB3 - 1
    fq = deque((b3, c, tg, dh) for c in range(NCH - 1)
               for tg in range(4) for dh in range(2))
    for ic in range(IC):
        emit_m1_sg(b3, ic)
        if ic > 0:
            emit_rnd2(b3, ic - 1)
        if pend:
            pend.popleft()()
        if fq and fq[0][1] * CH + CH <= ic:
            emit_fill(*fq.popleft())
    emit_rnd2(b3, IC - 1)
    while fq:
        emit_fill(*fq.popleft())
    while pend:
        pend.popleft()()
    for tg in range(4):
        for dh in range(2):
            emit_fill(b3, NCH - 1, tg, dh)
    for q in make_tail_quanta(b3):
        q()

    st.emit("sync", lambda E: E.wait_ge(s_dy, 16 * T))
    st.build()


_CACHE = {}


def _build_cached(debug=False):
    key = ("nc", debug)
    if key not in _CACHE:
        _CACHE[key] = build_nc(debug=debug)
    return _CACHE[key]


def _host_layout(wq, wk, wv, wo, wg, wu, wd):
    perm = np.concatenate([np.arange(0, 64, 2), np.arange(1, 64, 2)])
    rows = np.concatenate([h * 64 + perm for h in range(H)])
    wall = np.empty((D, D + 2 * HD), np.float32)
    wall[:, 0:D] = wq[rows, :].T
    wall[:, D:D + HD] = wk[:, perm]
    wall[:, D + HD:] = wv
    woT = np.ascontiguousarray(wo.T)
    wgT = np.ascontiguousarray(
        wg.T.reshape(NC, 128, IC, 128).transpose(2, 1, 0, 3))
    wuT = np.ascontiguousarray(
        wu.T.reshape(NC, 128, IC, 128).transpose(2, 1, 0, 3))
    wdT = np.ascontiguousarray(
        wd.T.reshape(NCH, CH, 128, 2, 512).transpose(3, 0, 2, 1, 4)).reshape(I, D)
    return wall, woT, wgT, wuT, wdT


def make_in_maps(x, wq, wk, wv, wo, wg, wu, wd):
    wall, woT, wgT, wuT, wdT = _host_layout(wq, wk, wv, wo, wg, wu, wd)
    in_maps = []
    for c in range(x.shape[0]):
        in_maps.append({
            "x": np.ascontiguousarray(x[c]),
            "xT": np.ascontiguousarray(x[c].T),
            "wall": wall, "woT": woT, "wgT": wgT, "wuT": wuT, "wdT": wdT,
        })
    return in_maps


def kernel(L0_attn_x, L0_attn_wq, L0_attn_wk, L0_attn_wv, L0_attn_wo,
           L0_ffn_w_gate, L0_ffn_w_up, L0_ffn_w_down):
    x = np.ascontiguousarray(np.asarray(L0_attn_x, np.float32))
    in_maps = make_in_maps(
        x, np.asarray(L0_attn_wq, np.float32), np.asarray(L0_attn_wk, np.float32),
        np.asarray(L0_attn_wv, np.float32), np.asarray(L0_attn_wo, np.float32),
        np.asarray(L0_ffn_w_gate, np.float32), np.asarray(L0_ffn_w_up, np.float32),
        np.asarray(L0_ffn_w_down, np.float32))
    nc, _ = _build_cached(debug=False)
    res = run_bass_kernel_spmd(nc, in_maps, core_ids=list(range(8)))
    return np.stack([res.results[c]["y"] for c in range(8)], axis=0)


# version 34
# speedup vs baseline: 1.0095x; 1.0095x over previous
import math
from collections import deque
from contextlib import ExitStack

import numpy as np
import concourse.bass as bass
import concourse.mybir as mybir
from concourse.bass_utils import run_bass_kernel_spmd
from concourse.alu_op_type import AluOpType as Op

F32 = mybir.dt.float32
BF16 = mybir.dt.bfloat16
I16 = mybir.dt.int16
I32 = mybir.dt.int32
FP8 = mybir.dt.float8e4
AF = mybir.ActivationFunctionType
AX = mybir.AxisListType

S = 2048
D = 1024
H = 16
HD = 64
I = 4096
T = S // 128
NC = D // 128
IC = I // 128
QB = 4
NB = T // QB
FB = 256
NFB = S // FB
TB = 512
NB3 = S // TB
CH = 8
NCH = 4
EPS = 1e-5
LN1000 = math.log(1000.0)
TWO_PI = 2.0 * math.pi
ENGINES = ("sync", "tensor", "vector", "scalar", "gpsimd")


class Sched:

    def __init__(self, nc, stack, name):
        self.nc = nc
        self.ops = {e: [] for e in ENGINES}
        self.cnt = {e: 0 for e in ENGINES}
        self.sems = {e: stack.enter_context(nc.semaphore(f"{name}_{e[:2]}"))
                     for e in ENGINES}

    def emit(self, eng, fn, deps=()):
        self.ops[eng].append((fn, tuple(d for d in deps if d is not None)))
        self.cnt[eng] += 1
        return (self, eng, self.cnt[eng])

    def build(self):
        with self.nc.Block() as blk:
            for e in ENGINES:
                lst = self.ops[e]
                if not lst:
                    continue

                def lane(E, e=e, lst=lst):
                    for fn, deps in lst:
                        for d in deps:
                            if d[0] == "SEM":
                                E.wait_ge(d[1], d[2])
                            elif d[0] is not self:
                                pass
                            elif d[1] == e and e != "gpsimd":
                                E.wait_ge(self.sems[e], d[2])
                            else:
                                E.wait_ge(self.sems[d[1]], d[2])
                        last = fn(E)
                        if e in ("vector", "scalar"):
                            if last is not None:
                                last.then_inc(self.sems[e], 1)
                            else:
                                E.drain().then_inc(self.sems[e], 1)
                        elif e in ("gpsimd", "tensor"):
                            assert last is not None, f"{e} group must return last inst"
                            last.then_inc(self.sems[e], 1)
                        else:
                            E.nop().then_inc(self.sems[e], 1)

                getattr(blk, e)(lane)


def build_nc(debug=False):
    nc = bass.Bass("TRN2", target_bir_lowering=False)
    env = {"debug": debug, "dbg": {}}
    env["x_h"] = nc.dram_tensor("x", [S, D], F32, kind="ExternalInput").ap()
    env["xT_h"] = nc.dram_tensor("xT", [D, S], F32, kind="ExternalInput").ap()
    env["wall_h"] = nc.dram_tensor("wall", [D, D + 2 * HD], F32, kind="ExternalInput").ap()
    env["woT_h"] = nc.dram_tensor("woT", [D, D], F32, kind="ExternalInput").ap()
    env["wsrc_h"] = {
        "g": nc.dram_tensor("wgT", [IC, 128, NC, 128], F32, kind="ExternalInput").ap(),
        "u": nc.dram_tensor("wuT", [IC, 128, NC, 128], F32, kind="ExternalInput").ap(),
        "d": nc.dram_tensor("wdT", [I, D], F32, kind="ExternalInput").ap(),
    }
    env["y_h"] = nc.dram_tensor("y", [S, D], F32, kind="ExternalOutput").ap()
    env["xsp_h"] = nc.dram_tensor("x_spill", [S, D], F32, kind="Internal").ap()
    env["qsp_h"] = nc.dram_tensor("q_spill", [T, 128, NC * 128], BF16, kind="Internal").ap()
    env["tern_h"] = {
        "g": nc.dram_tensor("tern_g", [IC * D * 128], FP8, kind="Internal").ap(),
        "u": nc.dram_tensor("tern_u", [IC * D * 128], FP8, kind="Internal").ap(),
        "d": nc.dram_tensor("tern_d", [I * D], FP8, kind="Internal").ap(),
    }
    if debug:
        for name, shape in [
            ("dbg_qfm", [128, NC * S]), ("dbg_kdup", [128, S]),
            ("dbg_v1", [128, T * 66]), ("dbg_x1", [S, D]),
            ("dbg_negb", [128, H]), ("dbg_qint", [128, NC * S]),
            ("dbg_pt", [128, T * 512]), ("dbg_otm", [128, QB * D]),
            ("dbg_tern", [IC * D * 128]),
        ]:
            env["dbg"][name] = nc.dram_tensor(name, shape, F32, kind="ExternalOutput").ap()

    with ExitStack() as top:
        def alloc(n, shape, dt, stk=top):
            t = stk.enter_context(nc.sbuf_tensor(n, shape, dt))
            env[n] = t
            return t

        alloc("identb", [128, 128], BF16)
        alloc("identf", [128, 128], F32)
        alloc("ones_col_b", [128, 1], BF16)
        alloc("ones_row_f", [1, 128], F32)
        alloc("onerow16", [1, 16], F32)
        alloc("delta", [128, 1], F32)
        alloc("sin_t", [128, T * 32], F32)
        alloc("cos_t", [128, T * 32], F32)
        alloc("negBtab", [128, H], F32)
        alloc("mxq", [128, T], F32)
        for w in "gud":
            alloc(f"sw_{w}", [128, 1], F32)
        alloc("beta_gu2", [128, 1], F32)
        alloc("beta_d", [128, 1], F32)
        alloc("wsum", [128, 48], F32)
        alloc("t1r", [1, 32], F32)
        alloc("t16r", [16, 4], F32)
        alloc("qn2acc", [128, H], F32)
        alloc("kn2acc", [128, 1], F32)
        for j in range(2):
            alloc(f"st6_{j}", [128, 12], F32)
            alloc(f"mv_{j}", [128, 2], F32)
            alloc(f"rstd_{j}", [128, 1], F32)
            alloc(f"negmr_{j}", [128, 1], F32)
            alloc(f"tny_{j}", [128, 16], F32)

        sX = ExitStack()
        alloc("x_tm", [128, T * D], F32, sX)

        with ExitStack() as sB:
            alloc("q_fm", [128, NC * S], BF16, sB)
            for j in range(2):
                alloc(f"wch_{j}", [128, 2048], F32, sB)
            alloc("kdup_fm", [128, S], BF16, sB)
            alloc("v1_tm", [128, T * 66], BF16, sB)

            with ExitStack() as sA:
                alloc("x_fm", [128, NC * S], BF16, sA)
                alloc("wall_sb", [128, NC * 1152], BF16, sA)
                build_p0(nc, sA, env)
                build_p1(nc, sA, env)

            with ExitStack() as sC:
                alloc("woT_sb", [128, NC * D], BF16, sC)
                alloc("pt_sb", [128, T * 512], BF16, sC)
                for j in range(2):
                    alloc(f"o_tm{j}", [128, QB * D], BF16, sC)
                alloc("o_fmt", [128, NC * 128], BF16, sC)
                alloc("tch_0", [128, 2048], FP8, sC)
                for j in range(2):
                    alloc(f"osum_{j}", [128, 4], F32, sC)
                alloc("zch2", [128, 2048], F32, sC)
                alloc("pch", [128, 2048], FP8, sC)
                alloc("ybuf", [128, D], F32, sC)
                alloc("oraw", [128, QB * 65], F32, sC)
                alloc("qst_0", [128, NC * 128], BF16, sC)
                for j in range(2):
                    alloc(f"ln2_{j}", [128, D], F32, sC)
                    alloc(f"sgq_{j}", [128, D], FP8, sC)
                alloc("i16_0", [128, D], I16, sC)
                alloc("qtm_0", [128, D], BF16, sC)
                build_p2(nc, sC, env)

        sX.close()

        with ExitStack() as sD:
            for j in range(2):
                alloc(f"qb_{j}", [128, 4 * 1024], BF16, sD)
                alloc(f"tg_{j}", [128, NC * 128], FP8, sD)
                alloc(f"tg_{2 + j}", [128, NC * 128], FP8, sD)
                alloc(f"tu_{j}", [128, NC * 128], FP8, sD)
                alloc(f"tu_{2 + j}", [128, NC * 128], FP8, sD)
                alloc(f"rb_{j}", [128, TB], F32, sD)
                alloc(f"ub_{j}", [128, TB], F32, sD)
                alloc(f"xrl_{j}", [128, D], F32, sD)
            for j in range(3):
                alloc(f"tdc_{j}", [128, CH * 512], FP8, sD)
            alloc("h_sb", [128, IC * TB], F32, sD)
            alloc("hq_sb", [128, IC * TB], BF16, sD)
            alloc("mxacc", [128, TB], F32, sD)
            for j in range(2):
                alloc(f"habs_{j}", [128, TB], F32, sD)
            alloc("csrow", [1, TB], F32, sD)
            alloc("csb", [128, TB], F32, sD)
            for j in range(2):
                alloc(f"m1b_{j}", [128, TB], F32, sD)
                alloc(f"sgb_{j}", [128, TB], FP8, sD)
                alloc(f"i16f_{j}", [128, TB], I16, sD)
            alloc("s4", [128, 32], F32, sD)
            alloc("d_tm", [128, 4 * D], F32, sD)
            alloc("ybuf3", [128, D], F32, sD)
            build_p3b(nc, sD, env)

    return nc, env["dbg"]


def build_p0(nc, stack, env):
    st = Sched(nc, stack, "p0")
    Ez = stack.enter_context
    s_dg = Ez(nc.semaphore("p0dg"))
    s_ds = Ez(nc.semaphore("p0ds"))
    ps_a = Ez(nc.psum_tensor("p0psa", [128, 512], F32))
    trows = Ez(nc.sbuf_tensor("trows", [1, T * 128], F32))
    jrow = Ez(nc.sbuf_tensor("jrow", [1, 32], F32))
    freqs = Ez(nc.sbuf_tensor("freqs", [1, 32], F32))
    rr = Ez(nc.sbuf_tensor("rr", [128, 32], F32))
    rk = Ez(nc.sbuf_tensor("rk", [128, 32], I32))
    rf = Ez(nc.sbuf_tensor("rf", [128, 32], F32))
    rf2 = Ez(nc.sbuf_tensor("rf2", [128, 32], F32))
    rs1 = Ez(nc.sbuf_tensor("rs1", [128, 32], F32))
    rs2 = Ez(nc.sbuf_tensor("rs2", [128, 32], F32))
    d15 = Ez(nc.sbuf_tensor("d15", [128, 1], F32))
    d15i = Ez(nc.sbuf_tensor("d15i", [128, 1], I16))
    g = env.__getitem__

    st.emit("sync", lambda E: E.dma_start(
        g("x_tm")[:], g("x_h").rearrange("(t p) d -> p t d", p=128)).then_inc(s_ds, 16))

    def gdmas(E):
        E.dma_start(g("x_fm")[:],
                    g("xT_h").rearrange("(c p) t -> p c t", p=128)).then_inc(s_dg, 16)
        E.dma_start(g("wall_sb")[:],
                    g("wall_h").rearrange("(c p) n -> p c n", p=128)).then_inc(s_dg, 16)
        return E.nop()

    st.emit("gpsimd", gdmas)

    def memsets(E):
        for n in ("identb", "identf", "ones_col_b", "ones_row_f", "onerow16"):
            E.memset(g(n)[:], 1.0)
        for n in ("qn2acc", "kn2acc", "wsum"):
            E.memset(g(n)[:], 0.0)
        E.memset(g("v1_tm")[:], 1.0)
        return E.memset(d15[:], 1.5)

    t_ms = st.emit("vector", memsets)

    def mkdelta(E):
        E.tensor_scalar(out=d15i[:], in0=d15[:], scalar1=1.0, scalar2=None, op0=Op.mult)
        E.drain()
        return E.tensor_scalar(out=g("delta")[:], in0=d15i[:], scalar1=-0.5,
                               scalar2=1.0, op0=Op.mult, op1=Op.add)

    st.emit("vector", mkdelta, deps=[t_ms])

    t_idb = st.emit("gpsimd", lambda E: E.affine_select(
        g("identb")[:], g("identb")[:], [[1, 128]], Op.is_equal, 0.0,
        base=0, channel_multiplier=-1), deps=[t_ms])
    t_idf = st.emit("gpsimd", lambda E: E.affine_select(
        g("identf")[:], g("identf")[:], [[1, 128]], Op.is_equal, 0.0,
        base=0, channel_multiplier=-1), deps=[t_ms])
    env["tok_ident"] = t_idf
    env["tok_identb"] = t_idb

    t_io1 = st.emit("gpsimd", lambda E: E.iota(
        trows[:], [[128, T], [1, 128]], base=0, channel_multiplier=0,
        allow_small_or_imprecise_dtypes=True))
    t_io2 = st.emit("gpsimd", lambda E: E.iota(
        jrow[:], [[1, 32]], base=0, channel_multiplier=0,
        allow_small_or_imprecise_dtypes=True))
    t_fr = st.emit("scalar", lambda E: E.activation(
        out=freqs[:], in_=jrow[:], func=AF.Exp, scale=np.float32(-LN1000 / 32.0)), deps=[t_io2])

    t_prev = None
    for i in range(T):
        t_an = st.emit("tensor", lambda E, i=i: E.matmul(
            ps_a[:, 0:32], trows[:, i * 128:(i + 1) * 128], freqs[:],
            start=True, stop=True), deps=[t_fr, t_io1, t_prev])

        def frac(E):
            E.tensor_scalar(out=rr[:], in0=ps_a[:, 0:32], scalar1=1.0 / TWO_PI,
                            scalar2=None, op0=Op.mult)
            E.drain()
            E.tensor_scalar(out=rk[:], in0=rr[:], scalar1=1.0, scalar2=None, op0=Op.mult)
            E.drain()
            E.tensor_tensor(out=rf[:], in0=rr[:], in1=rk[:], op=Op.subtract)
            E.drain()
            E.scalar_tensor_tensor(out=rf[:], in0=rf[:], scalar=0.0, in1=rf[:],
                                   op0=Op.is_lt, op1=Op.add)
            E.drain()
            E.tensor_scalar(out=rf2[:], in0=rf[:], scalar1=0.25, scalar2=None, op0=Op.add)
            E.drain()
            return E.scalar_tensor_tensor(out=rf2[:], in0=rf2[:], scalar=1.0,
                                          in1=rf2[:], op0=Op.is_lt, op1=Op.add)

        t_vf = st.emit("vector", frac, deps=[t_an])

        def sins(E):
            E.activation(out=rs1[:], in_=rf[:], func=AF.Sin, scale=np.float32(TWO_PI), bias=np.float32(-math.pi))
            E.drain()
            return E.activation(out=rs2[:], in_=rf2[:], func=AF.Sin,
                                scale=np.float32(TWO_PI),
                                bias=np.float32(-3.0 * math.pi))

        t_sn = st.emit("scalar", sins, deps=[t_vf])

        def neg(E, i=i):
            E.tensor_scalar(out=g("sin_t")[:, i * 32:(i + 1) * 32], in0=rs1[:],
                            scalar1=-1.0, scalar2=None, op0=Op.mult)
            return E.tensor_scalar(out=g("cos_t")[:, i * 32:(i + 1) * 32],
                                   in0=rs2[:], scalar1=-1.0, scalar2=None,
                                   op0=Op.mult)

        t_prev = st.emit("vector", neg, deps=[t_sn])

    def waitall(E):
        E.wait_ge(s_ds, 16)
        E.wait_ge(s_dg, 32)

    st.emit("sync", waitall, deps=[t_prev, env["tok_ident"]])
    st.build()


def build_p1(nc, stack, env):
    st = Sched(nc, stack, "p1")
    Ez = stack.enter_context
    ps_qkv = Ez(nc.psum_tensor("p1qkv", [128, 1536], F32))
    ps_tr = Ez(nc.psum_tensor("p1tr", [128, 1152], BF16))
    ps_nm = Ez(nc.psum_tensor("p1nm", [128, 512], F32))
    call = Ez(nc.sbuf_tensor("call", [1, 1152], BF16))
    negmrow_all = Ez(nc.sbuf_tensor("negmrow_all", [1, T * 128], BF16))
    rstd_all = Ez(nc.sbuf_tensor("rstd_all", [128, T], F32))
    qtq = [Ez(nc.sbuf_tensor(f"qtq{j}", [128, 1024], BF16)) for j in range(2)]
    qtkv = [Ez(nc.sbuf_tensor(f"qtkv{j}", [128, 128], F32)) for j in range(2)]
    qro = [Ez(nc.sbuf_tensor(f"qro{j}", [128, 1024], BF16)) for j in range(2)]
    kro = [Ez(nc.sbuf_tensor(f"kro{j}", [128, 128], BF16)) for j in range(2)]
    sc1 = Ez(nc.sbuf_tensor("sc1", [128, 1024], F32))
    g = env.__getitem__
    x_fm, wall_sb = g("x_fm"), g("wall_sb")
    st6 = [g("st6_0"), g("st6_1")]
    mvs = [g("mv_0"), g("mv_1")]
    tny = [g("tny_0"), g("tny_1")]

    def colsum(E):
        for c in range(NC):
            wc = wall_sb[:, c * 1152:(c + 1) * 1152]
            for lo, hi in ((0, 512), (512, 1024), (1024, 1152)):
                last = E.matmul(ps_qkv[:1, lo:hi], g("ones_col_b")[:], wc[:, lo:hi],
                                start=(c == 0), stop=(c == NC - 1))
        return last

    t_cs = st.emit("tensor", colsum)
    t_cse = st.emit("vector", lambda E: E.tensor_copy(call[:], ps_qkv[:1, 0:1152]),
                    deps=[t_cs])

    s_dwA = [Ez(nc.semaphore(f"p1wA{j}")) for j in range(2)]
    ps_fa = Ez(nc.psum_tensor("p1fa", [128, 512], F32))
    wsrc = env["wsrc_h"]
    passA = [(w, c) for w in "gud" for c in range(16)]
    tokA = {}
    t_abs = {}
    t_finred = {}
    t_findone = {}

    def wflat(w):
        a = wsrc[w]
        if len(a.shape) == 4:
            return a.rearrange("a b c d -> (a b c d)")
        return a.rearrange("a b -> (a b)")

    def emit_dmaA(k):
        if k < 0 or k >= len(passA):
            return
        w, c = passA[k]
        j = k % 2
        view = wflat(w).rearrange("(c p n) -> p c n", p=128, n=2048)[:, c, :]
        st.emit("sync", lambda E: E.dma_start(
            g(f"wch_{j}")[:], view).then_inc(s_dwA[j], 16),
            deps=[tokA.get(k - 2)])

    def emit_passA(k):
        j = k % 2
        col = k

        def red(E, j=j, col=col, k=k):
            E.wait_ge(s_dwA[j], 16 * (k // 2 + 1))
            return E.tensor_reduce(out=g("wsum")[:, col:col + 1],
                                   in_=g(f"wch_{j}")[:], axis=AX.X, op=Op.add,
                                   apply_absolute_value=True)

        tokA[k] = st.emit("vector", red)
        emit_dmaA(k + 2)

    def emit_finishA(w):
        wi = "gud".index(w)
        f1 = st.emit("vector", lambda E: E.tensor_reduce(
            out=g("tny_0")[:, 14:15], in_=g("wsum")[:, 16 * wi:16 * wi + 16],
            axis=AX.X, op=Op.add), deps=[tokA[16 * wi + c] for c in range(16)])
        f2 = st.emit("tensor", lambda E: E.transpose(
            ps_fa[0:1, 0:128], g("tny_0")[:, 14:15], g("identf")[:]),
            deps=[f1, env["tok_ident"]])

        def finv(E):
            E.tensor_reduce(out=g("t1r")[:, 0:1], in_=ps_fa[0:1, 0:128], axis=AX.X,
                            op=Op.add)
            E.drain()
            E.tensor_scalar(out=g("t1r")[:, 1:2], in0=g("t1r")[:, 0:1],
                            scalar1=1.0 / (I * D), scalar2=EPS, op0=Op.mult,
                            op1=Op.max)
            E.drain()
            return E.reciprocal(g("t1r")[:, 2:3], g("t1r")[:, 1:2])

        t_finred[w] = f3 = st.emit("vector", finv, deps=[f2])
        f4 = st.emit("tensor", lambda E: (
            E.matmul(ps_fa[:, 300:301], g("ones_row_f")[:], g("t1r")[:, 2:3],
                     start=True, stop=True),
            E.matmul(ps_fa[:, 301:302], g("ones_row_f")[:], g("t1r")[:, 1:2],
                     start=True, stop=True))[-1], deps=[f3])

        def fin5(E, w=w, wi=wi):
            E.tensor_copy(g(f"sw_{w}")[:], ps_fa[:, 300:301])
            E.drain()
            if w == "d":
                return E.tensor_copy(g("beta_d")[:], ps_fa[:, 301:302])
            return E.tensor_copy(g("tny_1")[:, 5 + wi:6 + wi], ps_fa[:, 301:302])

        t_findone[w] = st.emit("vector", fin5, deps=[f4])

    def emit_comb():
        def comb(E):
            E.tensor_tensor(out=g("beta_gu2")[:], in0=g("tny_1")[:, 5:6],
                            in1=g("tny_1")[:, 5:6], op=Op.mult)
            E.drain()
            return E.tensor_tensor(out=g("beta_gu2")[:], in0=g("beta_gu2")[:],
                                   in1=g("tny_1")[:, 6:7], op=Op.mult)

        st.emit("vector", comb, deps=[t_findone["g"], t_findone["u"]])

    emit_dmaA(0)
    emit_dmaA(1)
    pA_items = ([("A", k) for k in range(16)] + [("F", "g")]
                + [("A", k) for k in range(16, 32)] + [("F", "u")]
                + [("A", k) for k in range(32, 48)] + [("F", "d"), ("C", None)])
    pA_next = [0]

    def pA_quantum(n):
        for _ in range(n):
            if pA_next[0] < len(pA_items):
                kind, a = pA_items[pA_next[0]]
                pA_next[0] += 1
                if kind == "A":
                    emit_passA(a)
                elif kind == "F":
                    emit_finishA(a)
                else:
                    emit_comb()

    t_stats = {}
    t_nm = {}
    t_sq = {}
    t_nei = {}
    ne_chain = {"t": None}

    def emit_stats(i):
        j = i % 2

        def stats(E, i=i, j=j):
            xx = g("x_tm")[:, i * D:(i + 1) * D]
            E.bn_stats(st6[j][:, 0:6], xx[:, 0:512])
            E.bn_stats(st6[j][:, 6:12], xx[:, 512:1024])
            E.drain()
            E.bn_aggr(mvs[j][:], st6[j][:])
            E.drain()
            E.tensor_scalar(out=tny[j][:, 0:1], in0=mvs[j][:, 1:2], scalar1=EPS,
                            scalar2=None, op0=Op.add)
            E.drain()
            E.reciprocal(tny[j][:, 1:2], tny[j][:, 0:1])
            E.drain()
            return E.tensor_scalar(out=tny[j][:, 2:3], in0=mvs[j][:, 0:1],
                                   scalar1=-1.0, scalar2=None, op0=Op.mult)

        t_stats[i] = st.emit("vector", stats, deps=[t_nm.get(i - 2),
                                                    t_sq.get(i - 2)])
        t_sq[i] = st.emit("scalar", lambda E, i=i, j=j: E.activation(
            out=rstd_all[:, i:i + 1], in_=tny[j][:, 1:2], func=AF.Sqrt,
            scale=np.float32(1.0)), deps=[t_stats[i]])
        t_nm[i] = st.emit("tensor", lambda E, j=j: E.transpose(
            ps_nm[:1, 0:128], tny[j][:, 2:3], g("identf")[:]),
            deps=[t_stats[i], ne_chain["t"], env["tok_ident"]])
        t_nei[i] = ne_chain["t"] = st.emit("vector", lambda E, i=i: E.tensor_copy(
            negmrow_all[:, i * 128:(i + 1) * 128], ps_nm[:1, 0:128]),
            deps=[t_nm[i]])

    t_ev = {}
    t_rope = {}
    tre_chain = {"t": None}

    def emit_qkv_tile(i):
        j = i % 2
        t_tre = tre_chain["t"]

        def qkv(E, i=i):
            for c in range(NC):
                stat = x_fm[:, c * S + i * 128: c * S + (i + 1) * 128]
                wc = wall_sb[:, c * 1152:(c + 1) * 1152]
                for lo, hi in ((0, 512), (512, 1024), (1024, 1152)):
                    E.matmul(ps_qkv[:, lo:hi], stat, wc[:, lo:hi],
                             start=(c == 0), stop=False)
            nrow = negmrow_all[:, i * 128:(i + 1) * 128]
            for lo, hi in ((0, 512), (512, 1024), (1024, 1152)):
                last = E.matmul(ps_qkv[:, lo:hi], nrow, call[:, lo:hi],
                                start=False, stop=True)
            return last

        dq = [t_cse, t_nei[i], t_ev.get(i - 1)]
        t_mm = st.emit("tensor", qkv, deps=dq)

        def evac(E, i=i, j=j):
            for lo, hi in ((0, 512), (512, 1024)):
                E.activation(out=qtq[j][:, lo:hi], in_=ps_qkv[:, lo:hi],
                             func=AF.Identity, scale=rstd_all[:, i:i + 1])
            E.activation(out=qtkv[j][:], in_=ps_qkv[:, 1024:1152],
                         func=AF.Identity, scale=rstd_all[:, i:i + 1])

        t_ev[i] = st.emit("scalar", evac,
                          deps=[t_mm, t_sq[i], t_rope.get(i - 2)])

        def rope(E, i=i, j=j):
            kv = qtkv[j]
            sin_b = g("sin_t")[:, i * 32:(i + 1) * 32].rearrange(
                "p (o k) -> p o k", o=1).broadcast_to((128, H, 32))
            cos_b = g("cos_t")[:, i * 32:(i + 1) * 32].rearrange(
                "p (o k) -> p o k", o=1).broadcast_to((128, H, 32))
            qq = qtq[j][:, 0:1024].rearrange("p (h k) -> p h k", k=64)
            qa, qb = qq[:, :, 0:32], qq[:, :, 32:64]
            ro = qro[j][:, 0:1024].rearrange("p (h k) -> p h k", k=64)
            ra, rb2 = ro[:, :, 0:32], ro[:, :, 32:64]
            w1 = sc1[:, 0:512].rearrange("p (h k) -> p h k", k=32)
            w2 = sc1[:, 512:1024].rearrange("p (h k) -> p h k", k=32)
            E.tensor_tensor(out=w1, in0=qa, in1=cos_b, op=Op.mult)
            E.tensor_tensor(out=w2, in0=qb, in1=sin_b, op=Op.mult)
            E.drain()
            E.tensor_tensor(out=ra, in0=w1, in1=w2, op=Op.subtract)
            E.drain()
            E.tensor_tensor(out=w1, in0=qb, in1=cos_b, op=Op.mult)
            E.tensor_tensor(out=w2, in0=qa, in1=sin_b, op=Op.mult)
            E.drain()
            E.tensor_tensor(out=rb2, in0=w1, in1=w2, op=Op.add)
            E.drain()
            ka, kb = kv[:, 0:32], kv[:, 32:64]
            s1 = g("sin_t")[:, i * 32:(i + 1) * 32]
            c1 = g("cos_t")[:, i * 32:(i + 1) * 32]
            E.tensor_tensor(out=sc1[:, 0:32], in0=ka, in1=c1, op=Op.mult)
            E.tensor_tensor(out=sc1[:, 32:64], in0=kb, in1=s1, op=Op.mult)
            E.tensor_tensor(out=sc1[:, 64:96], in0=kb, in1=c1, op=Op.mult)
            E.tensor_tensor(out=sc1[:, 96:128], in0=ka, in1=s1, op=Op.mult)
            E.drain()
            E.tensor_tensor(out=kro[j][:, 0:32], in0=sc1[:, 0:32], in1=sc1[:, 32:64],
                            op=Op.subtract)
            E.tensor_tensor(out=kro[j][:, 32:64], in0=sc1[:, 64:96],
                            in1=sc1[:, 96:128], op=Op.add)
            E.drain()
            E.tensor_copy(kro[j][:, 64:128], kro[j][:, 0:64])
            E.tensor_copy(g("v1_tm")[:, i * 66:i * 66 + 64], kv[:, 64:128])
            E.drain()
            E.tensor_tensor(out=sc1[:], in0=qro[j][:], in1=qro[j][:], op=Op.mult)
            E.tensor_tensor(out=kv[:, 0:64], in0=kro[j][:, 0:64],
                            in1=kro[j][:, 0:64], op=Op.mult)
            E.drain()
            E.tensor_reduce(out=kv[:, 64:65], in_=kv[:, 0:64], axis=AX.X, op=Op.add)
            E.tensor_reduce(out=kv[:, 80:88],
                            in_=sc1[:, 0:512].rearrange("p (h k) -> p h k", k=64),
                            axis=AX.X, op=Op.add)
            E.tensor_reduce(out=kv[:, 88:96],
                            in_=sc1[:, 512:1024].rearrange("p (h k) -> p h k", k=64),
                            axis=AX.X, op=Op.add)
            E.drain()
            E.tensor_tensor(out=g("qn2acc")[:], in0=g("qn2acc")[:],
                            in1=kv[:, 80:96], op=Op.max)
            return E.tensor_tensor(out=g("kn2acc")[:], in0=g("kn2acc")[:],
                                   in1=kv[:, 64:65], op=Op.max)

        t_rope[i] = st.emit("vector", rope, deps=[t_ev[i]])

        def tposes(E, j=j):
            for c in range(NC):
                E.transpose(ps_tr[:, c * 128:(c + 1) * 128],
                            qro[j][:, c * 128:(c + 1) * 128], g("identb")[:])
            return E.transpose(ps_tr[:, 1024:1152], kro[j][:], g("identb")[:])

        t_tp = st.emit("tensor", tposes, deps=[t_rope[i], env["tok_identb"], t_tre])

        def trevac(E, i=i):
            for c in range(NC):
                E.tensor_copy(g("q_fm")[:, c * S + i * 128: c * S + (i + 1) * 128],
                              ps_tr[:, c * 128:(c + 1) * 128])
            return E.tensor_copy(g("kdup_fm")[:, i * 128:(i + 1) * 128],
                                 ps_tr[:, 1024:1152])

        tre_chain["t"] = st.emit("vector", trevac, deps=[t_tp])
        pA_quantum(4)

    for k in range(T + 1):
        if 2 * k < T:
            emit_stats(2 * k)
        if 2 * k + 1 < T:
            emit_stats(2 * k + 1)
        if k >= 1:
            emit_qkv_tile(k - 1)

    pA_quantum(len(pA_items))

    if env["debug"]:
        dbg = env["dbg"]
        s_dbg = Ez(nc.semaphore("p1dbg"))

        def dbgdma(E):
            E.dma_start(dbg["dbg_qfm"], g("q_fm")[:]).then_inc(s_dbg, 16)
            E.dma_start(dbg["dbg_kdup"], g("kdup_fm")[:]).then_inc(s_dbg, 16)
            E.dma_start(dbg["dbg_v1"], g("v1_tm")[:]).then_inc(s_dbg, 16)
            E.wait_ge(s_dbg, 48)
            return E.nop()

        st.emit("gpsimd", dbgdma,
                deps=[tre_chain["t"], t_rope[T - 1], t_rope[T - 2]])
    st.build()


def build_p2(nc, stack, env):
    st = Sched(nc, stack, "p2")
    Ez = stack.enter_context
    s_dwo = Ez(nc.semaphore("p2dwo"))
    s_dq = [Ez(nc.semaphore(f"p2q{j}")) for j in range(2)]
    s_dxs = Ez(nc.semaphore("p2xs"))
    s_dwA = [Ez(nc.semaphore(f"p2dA{j}")) for j in range(2)]
    s_dwB = [Ez(nc.semaphore(f"p2dB{j}")) for j in range(2)]
    ps_st = [Ez(nc.psum_tensor(f"p2st{j}", [128, 512], F32)) for j in range(2)]
    ps_o = [Ez(nc.psum_tensor(f"p2o{j}", [128, 512], F32)) for j in range(2)]
    ps_wo = Ez(nc.psum_tensor("p2wo", [128, 1024], F32))
    ps_tr = Ez(nc.psum_tensor("p2tr", [128, 1024], BF16))
    ps_bc = Ez(nc.psum_tensor("p2bc", [128, 512], F32))
    g = env.__getitem__
    st6 = [g("st6_0"), g("st6_1")]
    mvs = [g("mv_0"), g("mv_1")]
    rstd = [g("rstd_0"), g("rstd_1")]
    negmr = [g("negmr_0"), g("negmr_1")]
    tny = [g("tny_0"), g("tny_1")]
    t1r, t16r = g("t1r"), g("t16r")
    osum = [g("osum_0"), g("osum_1")]

    def _woT(E):
        E.dma_start(g("woT_sb")[:],
                    g("woT_h").rearrange("(c p) n -> p c n", p=128)).then_inc(s_dwo, 16)
        return E.nop()

    st.emit("gpsimd", _woT)
    t_ptms = st.emit("vector", lambda E: E.memset(g("pt_sb")[:], 0.0))

    tk = st.emit("tensor", lambda E: (
        E.transpose(ps_bc[0:16, 0:128], g("qn2acc")[:], g("identf")[:]),
        E.transpose(ps_bc[0:1, 128:256], g("kn2acc")[:], g("identf")[:]))[-1],
        deps=[env["tok_ident"]])

    def nb_v2(E):
        E.tensor_reduce(out=t16r[:, 0:1], in_=ps_bc[0:16, 0:128], axis=AX.X, op=Op.max)
        E.drain()
        return E.tensor_reduce(out=t1r[:, 0:1], in_=ps_bc[0:1, 128:256], axis=AX.X,
                               op=Op.max)

    tk = st.emit("vector", nb_v2, deps=[tk])
    tk2 = st.emit("tensor", lambda E: E.matmul(
        ps_bc[0:16, 256:257], g("onerow16")[:], t1r[:, 0:1], start=True, stop=True),
        deps=[tk])
    tk3 = st.emit("vector", lambda E: E.tensor_tensor(
        out=t16r[:, 1:2], in0=t16r[:, 0:1], in1=ps_bc[0:16, 256:257], op=Op.mult),
        deps=[tk2])
    tk4 = st.emit("scalar", lambda E: E.activation(
        out=t16r[:, 2:3], in_=t16r[:, 1:2], func=AF.Sqrt, scale=np.float32(1.0 / 64.0), bias=np.float32(1e-30)),
        deps=[tk3])
    tk5 = st.emit("vector", lambda E: E.tensor_scalar(
        out=t16r[:, 3:4], in0=t16r[:, 2:3], scalar1=-1.0, scalar2=None, op0=Op.mult),
        deps=[tk4])
    tk6 = st.emit("tensor", lambda E: E.transpose(
        ps_bc[0:1, 256:272], t16r[:, 3:4], g("identf")[0:16, 0:16]), deps=[tk5])
    tk7 = st.emit("vector", lambda E: E.tensor_copy(t1r[:, 8:24], ps_bc[0:1, 256:272]),
                  deps=[tk6])
    tk8 = st.emit("tensor", lambda E: E.matmul(
        ps_bc[:, 256:272], g("ones_row_f")[:], t1r[:, 8:24], start=True, stop=True),
        deps=[tk7])
    t_negb = st.emit("vector", lambda E: E.tensor_copy(g("negBtab")[:], ps_bc[:, 256:272]),
                     deps=[tk8])

    wsrc, tern = env["wsrc_h"], env["tern_h"]
    passA = [(w, c) for w in "gud" for c in range(16)]
    tokB = {}
    tokBv = {}

    def wflat(w):
        a = wsrc[w]
        if len(a.shape) == 4:
            return a.rearrange("a b c d -> (a b c d)")
        return a.rearrange("a b -> (a b)")

    def emit_passB(k):
        w, c = passA[k]
        j = k % 2
        view = wflat(w).rearrange("(c p n) -> p c n", p=128, n=2048)[:, c, :]
        tB = st.emit("sync", lambda E: E.dma_start(
            g(f"wch_{j}")[:], view).then_inc(s_dwA[j], 16),
            deps=[tokBv.get(k - 2)])

        def tzg(E, w=w, j=j, k=k):
            E.wait_ge(s_dwA[j], 16 * (k // 2 + 1))
            return E.tensor_scalar(out=g("zch2")[:], in0=g(f"wch_{j}")[:],
                                   scalar1=g(f"sw_{w}")[:], scalar2=1.0,
                                   op0=Op.mult, op1=Op.mult)

        tzg_tok = st.emit("gpsimd", tzg, deps=[tokBv.get(k - 2)])

        def tz(E, j=j, k=k):
            E.wait_ge(s_dwB[0], 16 * k)
            E.tensor_scalar(out=g("pch")[:], in0=g("zch2")[:], scalar1=0.5,
                            scalar2=None, op0=Op.is_ge)
            E.drain()
            return E.scalar_tensor_tensor(out=g("tch_0")[:], in0=g("zch2")[:],
                                          scalar=-0.5, in1=g("pch")[:], op0=Op.is_le,
                                          op1=Op.subtract)

        tokBv[k] = st.emit("vector", tz, deps=[tzg_tok])
        tern_view = tern[w].rearrange("(c p n) -> p c n", p=128, n=2048)[:, c, :]
        tokB[k] = st.emit("sync", lambda E: E.dma_start(
            tern_view, g("tch_0")[:]).then_inc(s_dwB[0], 16), deps=[tokBv[k]])

    woT_sb, pt_sb, o_fmt = g("woT_sb"), g("pt_sb"), g("o_fmt")
    o_tm = [g("o_tm0"), g("o_tm1")]
    q_fm, kdup_fm, v1_tm = g("q_fm"), g("kdup_fm"), g("v1_tm")
    x_tm, ybuf, oraw = g("x_tm"), g("ybuf"), g("oraw")

    prepseq = [("B", k) for k in range(len(passA))]
    prep_state = {"B": 0}

    def prep_quantum():
        if prep_state["B"] < len(prepseq):
            kind, a = prepseq[prep_state["B"]]
            prep_state["B"] += 1
            emit_passB(a)

    pend2 = deque()

    def make_wo_quanta(B):
        qs = []
        for q in range(QB):
            i = B * QB + q
            j = i % 2

            def qA(E=None, B=B, q=q, i=i, pb=B % 2):
                def otp(E, q=q, pb=pb):
                    for c in range(NC):
                        last = E.transpose(ps_tr[:, c * 128:(c + 1) * 128],
                                           o_tm[pb][:, q * D + c * 128:
                                                     q * D + (c + 1) * 128],
                                           g("identb")[:])
                    return last

                t_ot = st.emit("tensor", otp,
                               deps=[t_oev[(B, h, q)] for h in range(H)]
                                    + [env["tok_identb"], chain2["ps_tr_r"]])
                t_otp[(B, q)] = t_ot

                def otrev(E):
                    for c in range(NC):
                        last = E.tensor_copy(o_fmt[:, c * 128:(c + 1) * 128],
                                             ps_tr[:, c * 128:(c + 1) * 128])
                    return last

                chain2["trev"] = chain2["ps_tr_r"] = st.emit(
                    "vector", otrev, deps=[t_ot, chain2["wo"]])

            def qB_(E=None, B=B, q=q, i=i, j=j):
                def womm(E):
                    E.wait_ge(s_dwo, 16)
                    for c in range(NC):
                        stat = o_fmt[:, c * 128:(c + 1) * 128]
                        wc = woT_sb[:, c * D:(c + 1) * D]
                        E.matmul(ps_wo[:, 0:512], stat, wc[:, 0:512],
                                 start=(c == 0), stop=(c == NC - 1))
                        last = E.matmul(ps_wo[:, 512:1024], stat, wc[:, 512:1024],
                                        start=(c == 0), stop=(c == NC - 1))
                    return last

                t_wo = st.emit("tensor", womm, deps=[chain2["trev"], chain2["ye"]])
                chain2["wo"] = t_wo

                def lnstats(E, j=j):
                    E.bn_stats(st6[j][:, 0:6], ps_wo[:, 0:512])
                    E.bn_stats(st6[j][:, 6:12], ps_wo[:, 512:1024])
                    E.drain()
                    E.bn_aggr(mvs[j][:], st6[j][:])
                    E.drain()
                    E.tensor_scalar(out=tny[j][:, 0:1], in0=mvs[j][:, 1:2],
                                    scalar1=EPS, scalar2=None, op0=Op.add)
                    E.drain()
                    return E.reciprocal(tny[j][:, 1:2], tny[j][:, 0:1])

                t_ls = st.emit("vector", lnstats, deps=[t_wo, t_as3.get(i - 2)])
                chain2["rs"] = st.emit("scalar", lambda E, j=j: E.activation(
                    out=rstd[j][:], in_=tny[j][:, 1:2], func=AF.Sqrt,
                    scale=np.float32(1.0)), deps=[t_ls])

            def qC_(E=None, B=B, q=q, i=i, j=j):
                t_n2 = st.emit("vector", lambda E, j=j: E.tensor_scalar(
                    out=negmr[j][:], in0=mvs[j][:, 0:1], scalar1=rstd[j][:],
                    scalar2=-1.0, op0=Op.mult, op1=Op.mult), deps=[chain2["rs"]])

                def yev(E, j=j):
                    return E.activation(out=ybuf[:], in_=ps_wo[:, 0:1024],
                                        func=AF.Identity,
                                        scale=rstd[j][:], bias=negmr[j][:])

                chain2["ye"] = t_ye = st.emit("scalar", yev,
                                              deps=[t_n2, chain2["resid"]])

                def resid(E, i=i):
                    return E.tensor_tensor(out=x_tm[:, i * D:(i + 1) * D],
                                           in0=x_tm[:, i * D:(i + 1) * D],
                                           in1=ybuf[:], op=Op.add)

                t_resid[i] = st.emit("vector", resid, deps=[t_ye])
                chain2["resid"] = t_resid[i]

                def xspill(E, i=i):
                    E.dma_start(g("xsp_h").rearrange("(t p) d -> p t d", p=128)[:, i, :],
                                x_tm[:, i * D:(i + 1) * D]).then_inc(s_dxs, 16)

                st.emit("sync", xspill, deps=[t_resid[i]])

            def qD_(E=None, B=B, q=q, i=i, j=j):
                def lnq_stats(E, i=i, j=j):
                    xx = x_tm[:, i * D:(i + 1) * D]
                    E.bn_stats(st6[j][:, 0:6], xx[:, 0:512])
                    E.bn_stats(st6[j][:, 6:12], xx[:, 512:1024])
                    E.drain()
                    E.bn_aggr(mvs[j][:], st6[j][:])
                    E.drain()
                    E.tensor_scalar(out=tny[j][:, 0:1], in0=mvs[j][:, 1:2],
                                    scalar1=EPS, scalar2=None, op0=Op.add)
                    E.drain()
                    return E.reciprocal(tny[j][:, 1:2], tny[j][:, 0:1])

                t_q_ls = st.emit("vector", lnq_stats,
                                 deps=[t_resid[i], t_as3.get(i - 2)])
                t_q_rs = st.emit("scalar", lambda E, j=j: E.activation(
                    out=rstd[j][:], in_=tny[j][:, 1:2], func=AF.Sqrt,
                    scale=np.float32(1.0)), deps=[t_q_ls])
                chain2["qnm"] = st.emit("vector", lambda E, j=j: E.tensor_scalar(
                    out=negmr[j][:], in0=mvs[j][:, 0:1], scalar1=rstd[j][:],
                    scalar2=-1.0, op0=Op.mult, op1=Op.mult), deps=[t_q_rs])

            def qE_(E=None, B=B, q=q, i=i, j=j):
                t_q_l2 = st.emit("scalar", lambda E, i=i, j=j: E.activation(
                    out=g(f"ln2_{j}")[:], in_=x_tm[:, i * D:(i + 1) * D],
                    func=AF.Identity, scale=rstd[j][:], bias=negmr[j][:]),
                    deps=[chain2["qnm"]])

                def lnq_scale(E, i=i, j=j):
                    E.tensor_reduce(out=tny[j][:, 2:3], in_=g(f"ln2_{j}")[:],
                                    axis=AX.X, op=Op.max,
                                    apply_absolute_value=True)
                    E.drain()
                    E.tensor_scalar(out=g("mxq")[:, i:i + 1], in0=tny[j][:, 2:3],
                                    scalar1=EPS, scalar2=None, op0=Op.max)
                    E.drain()
                    E.reciprocal(tny[j][:, 3:4], g("mxq")[:, i:i + 1])
                    E.drain()
                    return E.tensor_scalar(out=tny[j][:, 4:5], in0=tny[j][:, 3:4],
                                           scalar1=127.0, scalar2=None,
                                           op0=Op.mult)

                chain2["qqs"] = st.emit("vector", lnq_scale, deps=[t_q_l2])

            def qF_(E=None, B=B, q=q, i=i, j=j):
                def lnq_absign(E, j=j):
                    E.activation(out=g(f"sgq_{j}")[:], in_=g(f"ln2_{j}")[:],
                                 func=AF.Sign)
                    E.drain()
                    return E.activation(out=g(f"ln2_{j}")[:], in_=g(f"ln2_{j}")[:],
                                        func=AF.Abs, scale=tny[j][:, 4:5])

                t_as3[i] = st.emit("scalar", lnq_absign,
                                   deps=[chain2["qqs"], t_rq3.get(i - 2)])

                def lnq_round(E, j=j):
                    E.tensor_scalar(out=g("i16_0")[:], in0=g(f"ln2_{j}")[:],
                                    scalar1=g("delta")[:], scalar2=None, op0=Op.add)
                    E.drain()
                    return E.tensor_tensor(out=g("qtm_0")[:], in0=g("i16_0")[:],
                                           in1=g(f"sgq_{j}")[:], op=Op.mult)

                t_rq3[i] = st.emit("vector", lnq_round,
                                   deps=[t_as3[i], t_qtp3.get(i - 1)])

            def qG_(E=None, B=B, q=q, i=i, j=j):
                def lnq_tp(E):
                    for c in range(NC):
                        last = E.transpose(ps_tr[:, c * 128:(c + 1) * 128],
                                           g("qtm_0")[:, c * 128:(c + 1) * 128],
                                           g("identb")[:])
                    return last

                t_qtp3[i] = st.emit("tensor", lnq_tp,
                                    deps=[t_rq3[i], chain2["ps_tr_r"],
                                          env["tok_identb"]])

                def lnq_trev(E, i=i):
                    if i >= 1:
                        E.wait_ge(s_dq[0], 16 * i)
                    for c in range(NC):
                        last = E.tensor_copy(g("qst_0")[:, c * 128:(c + 1) * 128],
                                             ps_tr[:, c * 128:(c + 1) * 128])
                    return last

                t_qtrev3[i] = st.emit("vector", lnq_trev, deps=[t_qtp3[i]])
                chain2["ps_tr_r"] = t_qtrev3[i]

                st.emit("sync", lambda E, i=i: E.dma_start(
                    g("qsp_h")[i], g("qst_0")[:]).then_inc(s_dq[0], 16),
                    deps=[t_qtrev3[i]])

            qs.append([qA, qB_, qC_, qD_, qE_, qF_, qG_])
        pat = [(0, 0), (0, 1), (0, 2), (1, 0), (0, 3), (1, 1), (0, 4), (1, 2),
               (0, 5), (1, 3), (0, 6), (1, 4), (2, 0), (1, 5), (2, 1), (1, 6),
               (2, 2), (3, 0), (2, 3), (3, 1), (2, 4), (3, 2), (2, 5), (3, 3),
               (2, 6), (3, 4), (3, 5), (3, 6)]
        return [qs[q][k] for q, k in pat]

    t_exp = {}
    t_pv = {}
    t_oev = {}
    t_cpd = {}
    t_as3 = {}
    t_rq3 = {}
    t_qtp3 = {}
    t_qtrev3 = {}
    t_otp = {}
    t_resid = {}
    last_pv_block = {}
    chain2 = {"trev": None, "ye": None, "resid": None, "rs": None,
              "qnm": None, "qqs": None, "wo": None, "ps_tr_r": None}

    for B in range(NB):
        nkt = QB * (B + 1)
        for h in range(H):
            hb = 64 * (h % 2)
            chunk = h // 2
            for kt in range(nkt):
                lo = max(kt * 128, B * 512)
                hi = (B + 1) * 512
                segw = hi - lo
                par = kt % 3

                def strip(E, kt=kt, lo=lo, hi=hi, segw=segw, par=par, hb=hb, chunk=chunk):
                    dst = ps_bc if par == 2 else ps_st[par]
                    return E.matmul(dst[:, 0:segw],
                                    kdup_fm[hb:hb + 64, kt * 128:(kt + 1) * 128],
                                    q_fm[hb:hb + 64, chunk * S + lo: chunk * S + hi],
                                    start=True, stop=True)

                dstrip = []
                if kt >= 3:
                    dstrip.append(t_exp[(B, h, kt - 3)])
                elif h > 0:
                    dstrip.append(t_exp[(B, h - 1, nkt - 3 + kt)])
                elif B > 0:
                    dstrip.append(t_exp[(B - 1, H - 1, QB * B - 3 + kt)])
                elif par == 2:
                    dstrip.append(t_negb)
                t_s = st.emit("tensor", strip, deps=dstrip)

                def expf(E, h=h, kt=kt, lo=lo, segw=segw, par=par, B=B):
                    off = kt * 512 + (lo - B * 512)
                    src_ps = ps_bc if par == 2 else ps_st[par]
                    return E.activation(out=pt_sb[:, off:off + segw],
                                        in_=src_ps[:, 0:segw],
                                        func=AF.Exp, scale=np.float32(0.125),
                                        bias=g("negBtab")[:, h:h + 1])

                dexp = [t_s]
                if B == 0 and h == 0 and kt == 0:
                    dexp.append(t_negb)
                    dexp.append(t_ptms)
                if h > 0:
                    dexp.append(t_pv[(B, h - 1, QB - 1)])
                elif B > 0:
                    dexp.append(t_pv[(B - 1, H - 1, QB - 1)])
                t_exp[(B, h, kt)] = st.emit("scalar", expf, deps=dexp)

            t_masks = []
            for qd in range(QB):
                def mask(E, B=B, qd=qd):
                    off0 = (B * QB) * 512 + (B * QB * 128 - B * 512) + 640 * qd
                    ap = bass.AP(pt_sb.tensor if hasattr(pt_sb, "tensor") else pt_sb,
                                 off0, [[T * 512, 128], [1, 128]])
                    return E.affine_select(ap, ap, [[1, 128]], Op.is_ge, 0.0,
                                           base=0, channel_multiplier=-1)

                t_masks.append(st.emit("gpsimd", mask,
                                       deps=[t_exp[(B, h, B * QB + qd)]]))

            for q in range(QB):
                qt = B * QB + q
                par = q % 2

                def pv(E, B=B, qt=qt, par=par):
                    for kt in range(qt + 1):
                        off = kt * 512 + (qt * 128 - B * 512)
                        last = E.matmul(ps_o[par][:, 0:65], pt_sb[:, off:off + 128],
                                        v1_tm[:, kt * 66:kt * 66 + 65],
                                        start=(kt == 0), stop=(kt == qt))
                    return last

                dpv = t_masks[:q + 1] + [t_exp[(B, h, qt)]]
                if q >= 2:
                    dpv.append(t_cpd.get((B, h, q - 2)))
                elif h > 0:
                    dpv.append(t_cpd.get((B, h - 1, q + 2)))
                elif B > 0:
                    dpv.append(t_cpd.get((B - 1, H - 1, q + 2)))
                t_pv[(B, h, q)] = st.emit("tensor", pv, deps=dpv)

                dcp = [t_pv[(B, h, q)]]
                if h > 0:
                    dcp.append(t_oev.get((B, h - 1, q)))
                elif B > 0:
                    dcp.append(t_oev.get((B - 1, H - 1, q)))
                t_cpd[(B, h, q)] = st.emit("vector", lambda E, q=q, par=par:
                    E.tensor_copy(oraw[:, q * 65:(q + 1) * 65], ps_o[par][:, 0:65]),
                    deps=dcp)

            t_rc = st.emit("vector", lambda E: E.reciprocal(
                osum[1][:, 0:4], oraw[:].rearrange("p (q c) -> p q c", c=65)[:, :, 64]),
                deps=[t_cpd[(B, h, q)] for q in range(QB)])

            for q in range(QB):
                def oev(E, h=h, q=q, pb=B % 2):
                    return E.tensor_scalar(
                        out=o_tm[pb][:, q * D + h * 64: q * D + (h + 1) * 64],
                        in0=oraw[:, q * 65:q * 65 + 64],
                        scalar1=osum[1][:, q:q + 1], scalar2=None, op0=Op.mult)

                doev = [t_rc]
                if B > 1 and h == 0:
                    doev.append(t_otp.get((B - 2, q)))
                t_oev[(B, h, q)] = st.emit("vector", oev, deps=doev)

            prep_quantum()
            for _ in range(2 if h < 13 else 1):
                if pend2:
                    pend2.popleft()()

        last_pv_block[B] = t_pv[(B, H - 1, QB - 1)]

        pend2.extend(make_wo_quanta(B))

    while pend2:
        pend2.popleft()()
    while prep_state["B"] < len(prepseq):
        prep_quantum()

    def waitB(E):
        E.wait_ge(s_dwB[0], 16 * 48)
        E.wait_ge(s_dxs, 16 * T)
        E.wait_ge(s_dq[0], 16 * T)

    st.emit("sync", waitB, deps=[tokB[len(passA) - 1], tokB[len(passA) - 2]])

    if env["debug"]:
        dbg = env["dbg"]
        s_dbg = Ez(nc.semaphore("p2dbg"))

        def dbgdma(E):
            E.dma_start(dbg["dbg_x1"].rearrange("(t p) d -> p t d", p=128),
                        x_tm[:].rearrange("p (t d) -> p t d", d=D)).then_inc(s_dbg, 16)
            E.dma_start(dbg["dbg_negb"], g("negBtab")[:]).then_inc(s_dbg, 16)
            E.dma_start(dbg["dbg_pt"], pt_sb[:]).then_inc(s_dbg, 16)
            E.dma_start(dbg["dbg_otm"], o_tm[1][:]).then_inc(s_dbg, 16)
            E.dma_start(dbg["dbg_tern"], tern["g"]).then_inc(s_dbg, 16)
            E.wait_ge(s_dbg, 80)
            return E.nop()

        st.emit("gpsimd", dbgdma,
                deps=[t_resid[T - 1], t_negb, tokB[len(passA) - 1]])

    env["tok_resid_p2"] = dict(t_resid)
    st.build()


def build_p3b(nc, stack, env):
    st = Sched(nc, stack, "p3b")
    Ez = stack.enter_context
    s_qb = [Ez(nc.semaphore(f"p3bq{j}")) for j in range(2)]
    s_dt = [Ez(nc.semaphore(f"p3bt{j}")) for j in range(4)]
    s_tdc = [Ez(nc.semaphore(f"p3bd{j}")) for j in range(3)]
    s_dx = [Ez(nc.semaphore(f"p3bx{j}")) for j in range(2)]
    s_dy = Ez(nc.semaphore("p3by"))
    ps_gu = [Ez(nc.psum_tensor(f"p3gu{j}", [128, 512], F32)) for j in range(4)]
    ps_d = [Ez(nc.psum_tensor(f"p3d{j}", [128, 512], F32)) for j in range(2)]
    ps_sc = [Ez(nc.psum_tensor(f"p3sc{j}", [128, 512], F32)) for j in range(2)]
    g = env.__getitem__
    st6 = [g("st6_0"), g("st6_1")]
    mvs = [g("mv_0"), g("mv_1")]
    rstd = [g("rstd_0"), g("rstd_1")]
    negmr = [g("negmr_0"), g("negmr_1")]
    tny = [g("tny_0"), g("tny_1")]
    tern = env["tern_h"]
    s4 = g("s4")

    t_qdma = {}
    t_tdma = {}
    t_mm_gu = {}
    t_ev = {}
    t_h1 = {}
    t_h = {}
    t_mx = {}
    t_habs = {}
    t_mt = {}
    t_cs4 = {}
    t_csrow = {}
    t_bcm = {}
    t_bce = {}
    t_m1 = {}
    t_sg = {}
    t_rds = {}
    t_fill = {}
    t_fev = {}
    t_fev2 = {}
    t_y3 = {}
    t_store = {}
    t_xr = {}
    chain = {"r3b": None}
    nqb = [0, 0]
    ndt = [0] * 4
    ndc = [0] * 3
    pend = deque()

    def emit_gu(b3, ic):
        par = ic % 4
        par_b = b3 % 2
        jj = ic % 2

        def dma_gu(E, ic=ic, par=par):
            gv = tern["g"].rearrange("(a p n) -> a p n", p=128, n=NC * 128)[ic]
            uv = tern["u"].rearrange("(a p n) -> a p n", p=128, n=NC * 128)[ic]
            E.dma_start(g(f"tg_{par}")[:], gv).then_inc(s_dt[par], 16)
            E.dma_start(g(f"tu_{par}")[:], uv).then_inc(s_dt[par], 16)

        t_tdma[(b3, ic)] = st.emit("sync", dma_gu,
                                   deps=[t_mm_gu.get((b3, ic - 4)),
                                         t_mm_gu.get((b3 - 1, IC - 4 + ic))
                                         if ic < 4 else None])
        ndt[par] += 2

        def mm_gu(E, par=par, par_b=par_b, dtt=16 * ndt[par], qbt=16 * nqb[par_b]):
            pp = par % 2
            E.wait_ge(s_dt[par], dtt)
            E.wait_ge(s_qb[par_b], qbt)
            qb = g(f"qb_{par_b}")[:].rearrange("p (t n) -> p t n", n=1024)
            for c2 in range(NC):
                mov = qb[:, :, c2 * 128:(c2 + 1) * 128]
                E.matmul(ps_gu[pp][:, 0:TB], g(f"tg_{par}")[:, c2 * 128:(c2 + 1) * 128],
                         mov, start=(c2 == 0), stop=(c2 == NC - 1))
                last = E.matmul(ps_gu[2 + pp][:, 0:TB],
                                g(f"tu_{par}")[:, c2 * 128:(c2 + 1) * 128],
                                mov, start=(c2 == 0), stop=(c2 == NC - 1))
            return last

        dmm = list(t_ev.get((b3, ic - 2)) or ())
        if ic < 2:
            dmm += list(t_ev.get((b3 - 1, IC - 2 + ic)) or ())
        t_mm_gu[(b3, ic)] = st.emit("tensor", mm_gu, deps=dmm)

        dev = [t_mm_gu[(b3, ic)], t_h.get((b3, ic - 2))]
        if ic < 2:
            dev.append(t_h.get((b3 - 1, IC - 2 + ic)))
        t_eg = st.emit("scalar", lambda E, pp=jj: E.activation(
            out=g(f"rb_{pp}")[:], in_=ps_gu[pp][:, 0:TB], func=AF.Relu,
            scale=np.float32(-1.0)), deps=dev)
        t_eu = st.emit("scalar", lambda E, pp=jj: E.activation(
            out=g(f"ub_{pp}")[:], in_=ps_gu[2 + pp][:, 0:TB], func=AF.Identity),
            deps=dev)
        t_ev[(b3, ic)] = (t_eg, t_eu)

        t_h1[(b3, ic)] = st.emit("gpsimd", lambda E, pp=jj: E.tensor_tensor(
            out=g(f"ub_{pp}")[:], in0=g(f"rb_{pp}")[:], in1=g(f"ub_{pp}")[:],
            op=Op.mult), deps=[t_eg, t_eu, t_m1.get((b3 - 1, ic))])
        t_h[(b3, ic)] = st.emit("gpsimd", lambda E, ic=ic, pp=jj: E.tensor_tensor(
            out=g("h_sb")[:, ic * TB:(ic + 1) * TB], in0=g(f"ub_{pp}")[:],
            in1=g(f"rb_{pp}")[:], op=Op.mult), deps=[t_h1[(b3, ic)]])

        dab = [t_h[(b3, ic)], t_mx.get((b3, ic - 2))]
        if ic < 2:
            dab.append(t_mx.get((b3 - 1, IC - 2 + ic)))
        t_habs[(b3, ic)] = st.emit("scalar", lambda E, ic=ic: E.activation(
            out=g(f"habs_{ic % 2}")[:], in_=g("h_sb")[:, ic * TB:(ic + 1) * TB],
            func=AF.Abs), deps=dab)

    def emit_mxup(b3, ic):
        dmx = [t_habs[(b3, ic)], t_mx.get((b3, ic - 1))]
        pr = ic % 2
        if ic == 0:
            dmx.append(t_mt.get(b3 - 1))
            fn = lambda E, pr=pr: E.tensor_copy(g("mxacc")[:], g(f"habs_{pr}")[:])
        else:
            fn = lambda E, pr=pr: E.tensor_tensor(out=g("mxacc")[:], in0=g("mxacc")[:],
                                                  in1=g(f"habs_{pr}")[:], op=Op.max)
        t_mx[(b3, ic)] = st.emit("vector", fn, deps=dmx)

    def emit_scales(b3):
        def mxtp(E):
            for q4 in range(4):
                last = E.transpose(ps_sc[0][:, q4 * 128:(q4 + 1) * 128],
                                   g("mxacc")[:, q4 * 128:(q4 + 1) * 128],
                                   g("identf")[:])
            return last

        t_mt[b3] = st.emit("tensor", mxtp,
                           deps=[t_mx[(b3, IC - 1)], env["tok_ident"],
                                 t_bce.get(b3 - 1)] + list(t_ev[(b3, IC - 1)]))

        def mx4(E):
            for tt in range(4):
                last = E.tensor_reduce(out=s4[:, tt:tt + 1],
                                       in_=ps_sc[0][:, tt * 128:(tt + 1) * 128],
                                       axis=AX.X, op=Op.max)
            return last

        t_m4 = st.emit("vector", mx4, deps=[t_mt[b3], t_cs4.get(b3 - 1)])

        def csc4(E, b3=b3):
            mq = g("mxq")[:, 4 * b3:4 * b3 + 4]
            E.tensor_scalar(out=s4[:, 4:8], in0=mq, scalar1=1.0 / 127.0,
                            scalar2=None, op0=Op.mult)
            E.drain()
            E.tensor_tensor(out=s4[:, 8:12], in0=s4[:, 4:8], in1=s4[:, 4:8], op=Op.mult)
            E.drain()
            E.tensor_tensor(out=s4[:, 8:12], in0=s4[:, 8:12], in1=s4[:, 4:8], op=Op.mult)
            E.drain()
            E.tensor_scalar(out=s4[:, 8:12], in0=s4[:, 8:12], scalar1=g("beta_gu2")[:],
                            scalar2=None, op0=Op.mult)
            E.drain()
            E.tensor_tensor(out=s4[:, 4:8], in0=s4[:, 8:12], in1=s4[:, 0:4], op=Op.mult)
            E.drain()
            E.tensor_scalar(out=s4[:, 4:8], in0=s4[:, 4:8], scalar1=EPS,
                            scalar2=None, op0=Op.max)
            E.drain()
            E.reciprocal(s4[:, 0:4], s4[:, 4:8])
            E.drain()
            E.tensor_scalar(out=s4[:, 8:12], in0=s4[:, 8:12], scalar1=127.0,
                            scalar2=None, op0=Op.mult)
            E.drain()
            E.tensor_tensor(out=s4[:, 12:16], in0=s4[:, 8:12], in1=s4[:, 0:4],
                            op=Op.mult)
            E.drain()
            PA = 16 + 8 * (b3 % 2)
            E.tensor_scalar(out=s4[:, PA:PA + 4], in0=s4[:, 4:8],
                            scalar1=g("beta_d")[:], scalar2=1.0 / 127.0,
                            op0=Op.mult, op1=Op.mult)
            E.drain()
            return E.tensor_tensor(out=s4[:, PA + 4:PA + 8], in0=s4[:, PA:PA + 4],
                                   in1=s4[:, PA:PA + 4], op=Op.mult)

        dcs = [t_m4, t_csrow.get(b3 - 1), t_y3.get((b3 - 2, 3))]
        t_cs4[b3] = st.emit("vector", csc4, deps=dcs)

        def cstp(E):
            for tt in range(4):
                last = E.transpose(ps_sc[1][:1, tt * 128:(tt + 1) * 128],
                                   s4[:, 12 + tt:13 + tt], g("identf")[:])
            return last

        t_ctp = st.emit("tensor", cstp, deps=[t_cs4[b3], t_csrow.get(b3 - 1)])
        t_csrow[b3] = st.emit("vector", lambda E: E.tensor_copy(
            g("csrow")[:], ps_sc[1][:1, 0:512]), deps=[t_ctp, t_bcm.get(b3 - 1)])
        t_bcm[b3] = st.emit("tensor", lambda E: E.matmul(
            ps_sc[0][:, 0:TB], g("ones_row_f")[:], g("csrow")[:],
            start=True, stop=True), deps=[t_csrow[b3], t_m4])
        t_bce[b3] = st.emit("vector", lambda E: E.tensor_copy(
            g("csb")[:], ps_sc[0][:, 0:TB]),
            deps=[t_bcm[b3], t_m1.get((b3 - 1, IC - 1))])

    def emit_m1_sg(b3, ic):
        pr = ic % 2
        dm1 = [t_bce[b3], t_rds.get((b3, ic - 2))]
        if ic < 2:
            dm1.append(t_rds.get((b3 - 1, IC - 2 + ic)))
        t_m1[(b3, ic)] = st.emit("vector", lambda E, ic=ic, pr=pr: E.tensor_tensor(
            out=g(f"m1b_{pr}")[:], in0=g("h_sb")[:, ic * TB:(ic + 1) * TB],
            in1=g("csb")[:], op=Op.mult), deps=dm1)

        def sgabs(E, pr=pr):
            E.activation(out=g(f"sgb_{pr}")[:], in_=g(f"m1b_{pr}")[:], func=AF.Sign)
            E.drain()
            return E.activation(out=g(f"m1b_{pr}")[:], in_=g(f"m1b_{pr}")[:],
                                func=AF.Abs)

        t_sg[(b3, ic)] = st.emit("scalar", sgabs, deps=[t_m1[(b3, ic)]])

    def emit_rnd2(b3, ic):
        pr = ic % 2

        def rnd2(E, ic=ic, pr=pr):
            E.tensor_scalar(out=g(f"i16f_{pr}")[:], in0=g(f"m1b_{pr}")[:],
                            scalar1=g("delta")[:], scalar2=None, op0=Op.add)
            E.drain()
            return E.tensor_tensor(out=g("hq_sb")[:, ic * TB:(ic + 1) * TB],
                                   in0=g(f"i16f_{pr}")[:], in1=g(f"sgb_{pr}")[:],
                                   op=Op.mult)

        t_rds[(b3, ic)] = st.emit("vector", rnd2,
                                  deps=[t_m1[(b3, ic)], t_sg[(b3, ic)],
                                        t_fill.get((b3 - 1, ic // CH, 3, 1))])

    def emit_tdc_dma(j):
        sl = j % 3
        dprev = None
        if j >= 3:
            bp, rp = divmod(j - 3, 8)
            cp, dhp = divmod(rp, 2)
            dprev = t_fill.get((bp, cp, 3, dhp))
        bj, rj = divmod(j, 8)
        c, dh = divmod(rj, 2)

        def dmaf(E, c=c, dh=dh, sl=sl):
            dv = tern["d"].rearrange("(h c p n) -> h c p n",
                                     c=NCH, p=128, n=CH * 512)[dh, c]
            E.dma_start(g(f"tdc_{sl}")[:], dv).then_inc(s_tdc[sl], 16)

        st.emit("sync", dmaf, deps=[dprev])
        ndc[sl] += 1

    def emit_fill(b3, c, tg, dh):
        j = b3 * 8 + c * 2 + dh
        sl = j % 3
        f = c * 8 + tg * 2 + dh
        bank = f % 2

        def mm(E, c=c, tg=tg, sl=sl, bank=bank, dct=16 * ndc[sl]):
            E.wait_ge(s_tdc[sl], dct)
            for icl in range(CH):
                ic = c * CH + icl
                last = E.matmul(
                    ps_d[bank][:, 0:TB],
                    g("hq_sb")[:, ic * TB + tg * 128: ic * TB + (tg + 1) * 128],
                    g(f"tdc_{sl}")[:, icl * 512:(icl + 1) * 512],
                    start=(icl == 0), stop=(icl == CH - 1))
            return last

        dmm = [t_rds[(b3, c * CH + CH - 1)], t_fev.get((b3, f - 2))]
        if f < 2:
            dmm.append(t_fev.get((b3 - 1, 30 + f)))
        t_fill[(b3, c, tg, dh)] = tok = st.emit("tensor", mm, deps=dmm)

        def ev(E, c=c, tg=tg, dh=dh, bank=bank):
            dst = g("d_tm")[:, tg * D + dh * 512: tg * D + dh * 512 + 512]
            if c == 0:
                return E.tensor_copy(dst, ps_d[bank][:, 0:TB])
            return E.tensor_tensor(out=dst, in0=dst, in1=ps_d[bank][:, 0:TB],
                                   op=Op.add)

        dev = [tok]
        if c == 0:
            dev.append(t_y3.get((b3 - 1, tg)))
        else:
            dev.append(t_fev2.get((b3, c - 1, tg, dh)))
        t_fev[(b3, f)] = t_fev2[(b3, c, tg, dh)] = st.emit("vector", ev, deps=dev)

        if tg == 3 and j + 3 < NB3 * 8:
            emit_tdc_dma(j + 3)

    def make_tail_quanta(b3):
        qs = []
        for tt in range(4):
            i = 4 * b3 + tt
            j = i % 2
            hold = {}

            def qa(b3=b3, tt=tt, i=i, j=j, hold=hold):
                def xdma(E, i=i, j=j):
                    if i >= 2:
                        E.wait_ge(s_dy, 16 * i)
                    E.dma_start(g(f"xrl_{j}")[:],
                                g("xsp_h").rearrange("(t p) d -> p t d", p=128)[:, i, :]
                                ).then_inc(s_dx[j], 16)

                t_xr[(b3, tt)] = st.emit("sync", xdma, deps=[t_store.get(i - 2)])

                def lnst3(E, tt=tt, j=j, PA=16 + 8 * (b3 % 2)):
                    dd = g("d_tm")[:, tt * D:(tt + 1) * D]
                    E.bn_stats(st6[j][:, 0:6], dd[:, 0:512])
                    E.bn_stats(st6[j][:, 6:12], dd[:, 512:1024])
                    E.drain()
                    E.bn_aggr(mvs[j][:], st6[j][:])
                    E.drain()
                    E.tensor_scalar(out=tny[j][:, 0:1], in0=mvs[j][:, 1:2],
                                    scalar1=s4[:, PA + 4 + tt:PA + 5 + tt],
                                    scalar2=EPS, op0=Op.mult, op1=Op.add)
                    E.drain()
                    return E.reciprocal(tny[j][:, 1:2], tny[j][:, 0:1])

                t_l3 = st.emit("vector", lnst3,
                               deps=[t_fev2[(b3, NCH - 1, tt, 0)],
                                     t_fev2[(b3, NCH - 1, tt, 1)], t_cs4[b3]])
                t_r3 = st.emit("scalar", lambda E, j=j: E.activation(
                    out=rstd[j][:], in_=tny[j][:, 1:2], func=AF.Sqrt,
                    scale=np.float32(1.0)), deps=[t_l3])
                hold["toks"] = (t_l3, t_r3)

            def qb_(b3=b3, tt=tt, i=i, j=j, hold=hold):
                t_l3, t_r3 = hold["toks"]

                def n3f(E, j=j, tt=tt, PA=16 + 8 * (b3 % 2)):
                    E.tensor_tensor(out=tny[j][:, 2:3], in0=rstd[j][:],
                                    in1=s4[:, PA + tt:PA + tt + 1], op=Op.mult)
                    E.drain()
                    return E.tensor_scalar(out=negmr[j][:], in0=mvs[j][:, 0:1],
                                           scalar1=tny[j][:, 2:3], scalar2=-1.0,
                                           op0=Op.mult, op1=Op.mult)

                t_n3 = st.emit("vector", n3f, deps=[t_r3])
                t_y3[(b3, tt)] = st.emit("scalar", lambda E, tt=tt, j=j: E.activation(
                    out=g("ybuf3")[:], in_=g("d_tm")[:, tt * D:(tt + 1) * D],
                    func=AF.Identity, scale=tny[j][:, 2:3], bias=negmr[j][:]),
                    deps=[t_n3, chain["r3b"]])

                def resid3(E, j=j, xrt=16 * ((i // 2) + 1)):
                    E.wait_ge(s_dx[j], xrt)
                    return E.tensor_tensor(out=g(f"xrl_{j}")[:], in0=g(f"xrl_{j}")[:],
                                           in1=g("ybuf3")[:], op=Op.add)

                t_r3b = st.emit("vector", resid3, deps=[t_y3[(b3, tt)], t_xr[(b3, tt)]])
                chain["r3b"] = t_r3b

                def store(E, i=i, j=j):
                    E.dma_start(env["y_h"].rearrange("(t p) d -> p t d", p=128)[:, i, :],
                                g(f"xrl_{j}")[:]).then_inc(s_dy, 16)

                t_store[i] = st.emit("sync", store, deps=[t_r3b])

            qs.append(qa)
            qs.append(qb_)
        return qs

    for j in range(3):
        emit_tdc_dma(j)

    for b3 in range(NB3):
        par_b = b3 % 2

        def qdma(E, b3=b3, par_b=par_b):
            for tt in range(4):
                E.dma_start(g(f"qb_{par_b}")[:, tt * 1024:(tt + 1) * 1024],
                            g("qsp_h")[4 * b3 + tt]).then_inc(s_qb[par_b], 16)

        t_qdma[b3] = st.emit("sync", qdma, deps=[t_mm_gu.get((b3 - 2, IC - 1))])
        nqb[par_b] += 4

        fq = deque((b3 - 1, c, tg, dh) for c in range(NCH - 1)
                   for tg in range(4) for dh in range(2)) if b3 > 0 else deque()
        for ic in range(IC):
            if b3 > 0:
                emit_m1_sg(b3 - 1, ic)
            emit_gu(b3, ic)
            if b3 > 0 and ic > 0:
                emit_rnd2(b3 - 1, ic - 1)
            if ic > 0:
                emit_mxup(b3, ic - 1)
            if pend:
                pend.popleft()()
            if fq and fq[0][1] * CH + CH <= ic:
                emit_fill(*fq.popleft())
        emit_mxup(b3, IC - 1)
        while pend:
            pend.popleft()()
        if b3 > 0:
            emit_rnd2(b3 - 1, IC - 1)
            while fq:
                emit_fill(*fq.popleft())
            for tg in range(2):
                for dh in range(2):
                    emit_fill(b3 - 1, NCH - 1, tg, dh)
            emit_scales(b3)
            for tg in range(2, 4):
                for dh in range(2):
                    emit_fill(b3 - 1, NCH - 1, tg, dh)
            pend.extend(make_tail_quanta(b3 - 1))
        else:
            emit_scales(b3)

    b3 = NB3 - 1
    fq = deque((b3, c, tg, dh) for c in range(NCH - 1)
               for tg in range(4) for dh in range(2))
    for ic in range(IC):
        emit_m1_sg(b3, ic)
        if ic > 0:
            emit_rnd2(b3, ic - 1)
        if pend:
            pend.popleft()()
        if fq and fq[0][1] * CH + CH <= ic:
            emit_fill(*fq.popleft())
    emit_rnd2(b3, IC - 1)
    while fq:
        emit_fill(*fq.popleft())
    while pend:
        pend.popleft()()
    for tg in range(4):
        for dh in range(2):
            emit_fill(b3, NCH - 1, tg, dh)
    for q in make_tail_quanta(b3):
        q()

    st.emit("sync", lambda E: E.wait_ge(s_dy, 16 * T))
    st.build()


_CACHE = {}


def _build_cached(debug=False):
    key = ("nc", debug)
    if key not in _CACHE:
        _CACHE[key] = build_nc(debug=debug)
    return _CACHE[key]


def _host_layout(wq, wk, wv, wo, wg, wu, wd):
    perm = np.concatenate([np.arange(0, 64, 2), np.arange(1, 64, 2)])
    rows = np.concatenate([h * 64 + perm for h in range(H)])
    wall = np.empty((D, D + 2 * HD), np.float32)
    wall[:, 0:D] = wq[rows, :].T
    wall[:, D:D + HD] = wk[:, perm]
    wall[:, D + HD:] = wv
    woT = np.ascontiguousarray(wo.T)
    wgT = np.ascontiguousarray(
        wg.T.reshape(NC, 128, IC, 128).transpose(2, 1, 0, 3))
    wuT = np.ascontiguousarray(
        wu.T.reshape(NC, 128, IC, 128).transpose(2, 1, 0, 3))
    wdT = np.ascontiguousarray(
        wd.T.reshape(NCH, CH, 128, 2, 512).transpose(3, 0, 2, 1, 4)).reshape(I, D)
    return wall, woT, wgT, wuT, wdT


def make_in_maps(x, wq, wk, wv, wo, wg, wu, wd):
    wall, woT, wgT, wuT, wdT = _host_layout(wq, wk, wv, wo, wg, wu, wd)
    in_maps = []
    for c in range(x.shape[0]):
        in_maps.append({
            "x": np.ascontiguousarray(x[c]),
            "xT": np.ascontiguousarray(x[c].T),
            "wall": wall, "woT": woT, "wgT": wgT, "wuT": wuT, "wdT": wdT,
        })
    return in_maps


def kernel(L0_attn_x, L0_attn_wq, L0_attn_wk, L0_attn_wv, L0_attn_wo,
           L0_ffn_w_gate, L0_ffn_w_up, L0_ffn_w_down):
    x = np.ascontiguousarray(np.asarray(L0_attn_x, np.float32))
    in_maps = make_in_maps(
        x, np.asarray(L0_attn_wq, np.float32), np.asarray(L0_attn_wk, np.float32),
        np.asarray(L0_attn_wv, np.float32), np.asarray(L0_attn_wo, np.float32),
        np.asarray(L0_ffn_w_gate, np.float32), np.asarray(L0_ffn_w_up, np.float32),
        np.asarray(L0_ffn_w_down, np.float32))
    nc, _ = _build_cached(debug=False)
    res = run_bass_kernel_spmd(nc, in_maps, core_ids=list(range(8)))
    return np.stack([res.results[c]["y"] for c in range(8)], axis=0)


# version 35 (final)
# speedup vs baseline: 1.0433x; 1.0335x over previous
import math
from collections import deque
from contextlib import ExitStack

import numpy as np
import concourse.bass as bass
import concourse.mybir as mybir
from concourse.bass_utils import run_bass_kernel_spmd
from concourse.alu_op_type import AluOpType as Op

F32 = mybir.dt.float32
BF16 = mybir.dt.bfloat16
I16 = mybir.dt.int16
I32 = mybir.dt.int32
FP8 = mybir.dt.float8e4
AF = mybir.ActivationFunctionType
AX = mybir.AxisListType

S = 2048
D = 1024
H = 16
HD = 64
I = 4096
T = S // 128
NC = D // 128
IC = I // 128
QB = 4
NB = T // QB
FB = 256
NFB = S // FB
TB = 512
NB3 = S // TB
CH = 8
NCH = 4
EPS = 1e-5
LN1000 = math.log(1000.0)
TWO_PI = 2.0 * math.pi
ENGINES = ("sync", "tensor", "vector", "scalar", "gpsimd")


class Sched:

    def __init__(self, nc, stack, name):
        self.nc = nc
        self.ops = {e: [] for e in ENGINES}
        self.cnt = {e: 0 for e in ENGINES}
        self.sems = {e: stack.enter_context(nc.semaphore(f"{name}_{e[:2]}"))
                     for e in ENGINES}

    def emit(self, eng, fn, deps=()):
        self.ops[eng].append((fn, tuple(d for d in deps if d is not None)))
        self.cnt[eng] += 1
        return (self, eng, self.cnt[eng])

    def build(self):
        with self.nc.Block() as blk:
            for e in ENGINES:
                lst = self.ops[e]
                if not lst:
                    continue

                def lane(E, e=e, lst=lst):
                    for fn, deps in lst:
                        for d in deps:
                            if d[0] == "SEM":
                                E.wait_ge(d[1], d[2])
                            elif d[0] is not self:
                                pass
                            elif d[1] == e and e != "gpsimd":
                                E.wait_ge(self.sems[e], d[2])
                            else:
                                E.wait_ge(self.sems[d[1]], d[2])
                        last = fn(E)
                        if e in ("vector", "scalar"):
                            if last is not None:
                                last.then_inc(self.sems[e], 1)
                            else:
                                E.drain().then_inc(self.sems[e], 1)
                        elif e in ("gpsimd", "tensor"):
                            assert last is not None, f"{e} group must return last inst"
                            last.then_inc(self.sems[e], 1)
                        else:
                            E.nop().then_inc(self.sems[e], 1)

                getattr(blk, e)(lane)


def build_nc(debug=False):
    nc = bass.Bass("TRN2", target_bir_lowering=False)
    env = {"debug": debug, "dbg": {}}
    env["x_h"] = nc.dram_tensor("x", [S, D], F32, kind="ExternalInput").ap()
    env["xT_h"] = nc.dram_tensor("xT", [D, S], F32, kind="ExternalInput").ap()
    env["wall_h"] = nc.dram_tensor("wall", [D, D + 2 * HD], F32, kind="ExternalInput").ap()
    env["woT_h"] = nc.dram_tensor("woT", [D, D], F32, kind="ExternalInput").ap()
    env["wsrc_h"] = {
        "g": nc.dram_tensor("wgT", [IC, 128, NC, 128], F32, kind="ExternalInput").ap(),
        "u": nc.dram_tensor("wuT", [IC, 128, NC, 128], F32, kind="ExternalInput").ap(),
        "d": nc.dram_tensor("wdT", [I, D], F32, kind="ExternalInput").ap(),
    }
    env["y_h"] = nc.dram_tensor("y", [S, D], F32, kind="ExternalOutput").ap()
    env["xsp_h"] = nc.dram_tensor("x_spill", [S, D], F32, kind="Internal").ap()
    env["qsp_h"] = nc.dram_tensor("q_spill", [T, 128, NC * 128], BF16, kind="Internal").ap()
    env["tern_h"] = {
        "g": nc.dram_tensor("tern_g", [IC * D * 128], FP8, kind="Internal").ap(),
        "u": nc.dram_tensor("tern_u", [IC * D * 128], FP8, kind="Internal").ap(),
        "d": nc.dram_tensor("tern_d", [I * D], FP8, kind="Internal").ap(),
    }
    if debug:
        for name, shape in [
            ("dbg_qfm", [128, NC * S]), ("dbg_kdup", [128, S]),
            ("dbg_v1", [128, T * 66]), ("dbg_x1", [S, D]),
            ("dbg_negb", [128, H]), ("dbg_qint", [128, NC * S]),
            ("dbg_pt", [128, T * 512]), ("dbg_otm", [128, QB * D]),
            ("dbg_tern", [IC * D * 128]),
        ]:
            env["dbg"][name] = nc.dram_tensor(name, shape, F32, kind="ExternalOutput").ap()

    with ExitStack() as top:
        def alloc(n, shape, dt, stk=top):
            t = stk.enter_context(nc.sbuf_tensor(n, shape, dt))
            env[n] = t
            return t

        alloc("identb", [128, 128], BF16)
        alloc("identf", [128, 128], F32)
        alloc("ones_col_b", [128, 1], BF16)
        alloc("ones_row_f", [1, 128], F32)
        alloc("onerow16", [1, 16], F32)
        alloc("delta", [128, 1], F32)
        alloc("sin_t", [128, T * 32], F32)
        alloc("cos_t", [128, T * 32], F32)
        alloc("negBtab", [128, H], F32)
        alloc("mxq", [128, T], F32)
        for w in "gud":
            alloc(f"sw_{w}", [128, 1], F32)
        alloc("beta_gu2", [128, 1], F32)
        alloc("beta_d", [128, 1], F32)
        alloc("wsum", [128, 48], F32)
        alloc("t1r", [1, 32], F32)
        alloc("t16r", [16, 4], F32)
        alloc("qn2acc", [128, H], F32)
        alloc("kn2acc", [128, 1], F32)
        for j in range(2):
            alloc(f"st6_{j}", [128, 12], F32)
            alloc(f"mv_{j}", [128, 2], F32)
            alloc(f"rstd_{j}", [128, 1], F32)
            alloc(f"negmr_{j}", [128, 1], F32)
            alloc(f"tny_{j}", [128, 16], F32)

        sX = ExitStack()
        alloc("x_tm", [128, T * D], F32, sX)

        with ExitStack() as sB:
            alloc("q_fm", [128, NC * S], BF16, sB)
            for j in range(2):
                alloc(f"wch_{j}", [128, 2048], F32, sB)
            alloc("kdup_fm", [128, S], BF16, sB)
            alloc("v1_tm", [128, T * 66], BF16, sB)

            with ExitStack() as sA:
                alloc("x_fm", [128, NC * S], BF16, sA)
                alloc("wall_sb", [128, NC * 1152], BF16, sA)
                build_p0(nc, sA, env)
                build_p1(nc, sA, env)

            with ExitStack() as sC:
                alloc("woT_sb", [128, NC * D], BF16, sC)
                alloc("pt_sb", [128, T * 512], BF16, sC)
                for j in range(2):
                    alloc(f"o_tm{j}", [128, QB * D], BF16, sC)
                alloc("o_fmt", [128, NC * 128], BF16, sC)
                alloc("tch_0", [128, 2048], FP8, sC)
                for j in range(2):
                    alloc(f"osum_{j}", [128, 4], F32, sC)
                alloc("zch2", [128, 2048], F32, sC)
                alloc("pch", [128, 2048], FP8, sC)
                alloc("ybuf", [128, D], F32, sC)
                alloc("oraw", [128, QB * 65], F32, sC)
                alloc("qst_0", [128, NC * 128], BF16, sC)
                for j in range(2):
                    alloc(f"ln2_{j}", [128, D], F32, sC)
                    alloc(f"sgq_{j}", [128, D], FP8, sC)
                alloc("i16_0", [128, D], I16, sC)
                alloc("qtm_0", [128, D], BF16, sC)
                build_p2(nc, sC, env)

        sX.close()

        with ExitStack() as sD:
            for j in range(2):
                alloc(f"qb_{j}", [128, 4 * 1024], BF16, sD)
                alloc(f"tg_{j}", [128, NC * 128], FP8, sD)
                alloc(f"tg_{2 + j}", [128, NC * 128], FP8, sD)
                alloc(f"tu_{j}", [128, NC * 128], FP8, sD)
                alloc(f"tu_{2 + j}", [128, NC * 128], FP8, sD)
                alloc(f"rb_{j}", [128, TB], F32, sD)
                alloc(f"ub_{j}", [128, TB], F32, sD)
                alloc(f"xrl_{j}", [128, D], F32, sD)
            for j in range(3):
                alloc(f"tdc_{j}", [128, CH * 512], FP8, sD)
            alloc("h_sb", [128, IC * TB], F32, sD)
            alloc("hq_sb", [128, IC * TB], BF16, sD)
            alloc("mxacc", [128, TB], F32, sD)
            for j in range(2):
                alloc(f"habs_{j}", [128, TB], F32, sD)
            alloc("csrow", [1, TB], F32, sD)
            alloc("csb", [128, TB], F32, sD)
            for j in range(2):
                alloc(f"m1b_{j}", [128, TB], F32, sD)
                alloc(f"sgb_{j}", [128, TB], FP8, sD)
                alloc(f"i16f_{j}", [128, TB], I16, sD)
            alloc("s4", [128, 32], F32, sD)
            alloc("d_tm", [128, 4 * D], F32, sD)
            alloc("ybuf3", [128, D], F32, sD)
            build_p3b(nc, sD, env)

    return nc, env["dbg"]


def build_p0(nc, stack, env):
    st = Sched(nc, stack, "p0")
    Ez = stack.enter_context
    s_dg = Ez(nc.semaphore("p0dg"))
    s_ds = Ez(nc.semaphore("p0ds"))
    ps_a = Ez(nc.psum_tensor("p0psa", [128, 512], F32))
    trows = Ez(nc.sbuf_tensor("trows", [1, T * 128], F32))
    jrow = Ez(nc.sbuf_tensor("jrow", [1, 32], F32))
    freqs = Ez(nc.sbuf_tensor("freqs", [1, 32], F32))
    rr = Ez(nc.sbuf_tensor("rr", [128, 32], F32))
    rk = Ez(nc.sbuf_tensor("rk", [128, 32], I32))
    rf = Ez(nc.sbuf_tensor("rf", [128, 32], F32))
    rf2 = Ez(nc.sbuf_tensor("rf2", [128, 32], F32))
    rs1 = Ez(nc.sbuf_tensor("rs1", [128, 32], F32))
    rs2 = Ez(nc.sbuf_tensor("rs2", [128, 32], F32))
    d15 = Ez(nc.sbuf_tensor("d15", [128, 1], F32))
    d15i = Ez(nc.sbuf_tensor("d15i", [128, 1], I16))
    g = env.__getitem__

    st.emit("sync", lambda E: E.dma_start(
        g("x_tm")[:], g("x_h").rearrange("(t p) d -> p t d", p=128)).then_inc(s_ds, 16))

    def gdmas(E):
        E.dma_start(g("x_fm")[:],
                    g("xT_h").rearrange("(c p) t -> p c t", p=128)).then_inc(s_dg, 16)
        E.dma_start(g("wall_sb")[:],
                    g("wall_h").rearrange("(c p) n -> p c n", p=128)).then_inc(s_dg, 16)
        return E.nop()

    st.emit("gpsimd", gdmas)

    def memsets(E):
        for n in ("identb", "identf", "ones_col_b", "ones_row_f", "onerow16"):
            E.memset(g(n)[:], 1.0)
        for n in ("qn2acc", "kn2acc", "wsum"):
            E.memset(g(n)[:], 0.0)
        E.memset(g("v1_tm")[:], 1.0)
        return E.memset(d15[:], 1.5)

    t_ms = st.emit("vector", memsets)

    def mkdelta(E):
        E.tensor_scalar(out=d15i[:], in0=d15[:], scalar1=1.0, scalar2=None, op0=Op.mult)
        E.drain()
        return E.tensor_scalar(out=g("delta")[:], in0=d15i[:], scalar1=-0.5,
                               scalar2=1.0, op0=Op.mult, op1=Op.add)

    st.emit("vector", mkdelta, deps=[t_ms])

    t_idb = st.emit("gpsimd", lambda E: E.affine_select(
        g("identb")[:], g("identb")[:], [[1, 128]], Op.is_equal, 0.0,
        base=0, channel_multiplier=-1), deps=[t_ms])
    t_idf = st.emit("gpsimd", lambda E: E.affine_select(
        g("identf")[:], g("identf")[:], [[1, 128]], Op.is_equal, 0.0,
        base=0, channel_multiplier=-1), deps=[t_ms])
    env["tok_ident"] = t_idf
    env["tok_identb"] = t_idb

    t_io1 = st.emit("gpsimd", lambda E: E.iota(
        trows[:], [[128, T], [1, 128]], base=0, channel_multiplier=0,
        allow_small_or_imprecise_dtypes=True))
    t_io2 = st.emit("gpsimd", lambda E: E.iota(
        jrow[:], [[1, 32]], base=0, channel_multiplier=0,
        allow_small_or_imprecise_dtypes=True))
    t_fr = st.emit("scalar", lambda E: E.activation(
        out=freqs[:], in_=jrow[:], func=AF.Exp, scale=np.float32(-LN1000 / 32.0)), deps=[t_io2])

    t_prev = None
    for i in range(T):
        t_an = st.emit("tensor", lambda E, i=i: E.matmul(
            ps_a[:, 0:32], trows[:, i * 128:(i + 1) * 128], freqs[:],
            start=True, stop=True), deps=[t_fr, t_io1, t_prev])

        def frac(E):
            E.tensor_scalar(out=rr[:], in0=ps_a[:, 0:32], scalar1=1.0 / TWO_PI,
                            scalar2=None, op0=Op.mult)
            E.drain()
            E.tensor_scalar(out=rk[:], in0=rr[:], scalar1=1.0, scalar2=None, op0=Op.mult)
            E.drain()
            E.tensor_tensor(out=rf[:], in0=rr[:], in1=rk[:], op=Op.subtract)
            E.drain()
            E.scalar_tensor_tensor(out=rf[:], in0=rf[:], scalar=0.0, in1=rf[:],
                                   op0=Op.is_lt, op1=Op.add)
            E.drain()
            E.tensor_scalar(out=rf2[:], in0=rf[:], scalar1=0.25, scalar2=None, op0=Op.add)
            E.drain()
            return E.scalar_tensor_tensor(out=rf2[:], in0=rf2[:], scalar=1.0,
                                          in1=rf2[:], op0=Op.is_lt, op1=Op.add)

        t_vf = st.emit("vector", frac, deps=[t_an])

        def sins(E):
            E.activation(out=rs1[:], in_=rf[:], func=AF.Sin, scale=np.float32(TWO_PI), bias=np.float32(-math.pi))
            E.drain()
            return E.activation(out=rs2[:], in_=rf2[:], func=AF.Sin,
                                scale=np.float32(TWO_PI),
                                bias=np.float32(-3.0 * math.pi))

        t_sn = st.emit("scalar", sins, deps=[t_vf])

        def neg(E, i=i):
            E.tensor_scalar(out=g("sin_t")[:, i * 32:(i + 1) * 32], in0=rs1[:],
                            scalar1=-1.0, scalar2=None, op0=Op.mult)
            return E.tensor_scalar(out=g("cos_t")[:, i * 32:(i + 1) * 32],
                                   in0=rs2[:], scalar1=-1.0, scalar2=None,
                                   op0=Op.mult)

        t_prev = st.emit("vector", neg, deps=[t_sn])

    def waitall(E):
        E.wait_ge(s_ds, 16)
        E.wait_ge(s_dg, 32)

    st.emit("sync", waitall, deps=[t_prev, env["tok_ident"]])
    st.build()


def build_p1(nc, stack, env):
    st = Sched(nc, stack, "p1")
    Ez = stack.enter_context
    ps_qkv = Ez(nc.psum_tensor("p1qkv", [128, 1536], F32))
    ps_tr = Ez(nc.psum_tensor("p1tr", [128, 1152], BF16))
    ps_nm = Ez(nc.psum_tensor("p1nm", [128, 512], F32))
    call = Ez(nc.sbuf_tensor("call", [1, 1152], BF16))
    negmrow_all = Ez(nc.sbuf_tensor("negmrow_all", [1, T * 128], BF16))
    rstd_all = Ez(nc.sbuf_tensor("rstd_all", [128, T], F32))
    qtq = [Ez(nc.sbuf_tensor(f"qtq{j}", [128, 1024], BF16)) for j in range(2)]
    qtkv = [Ez(nc.sbuf_tensor(f"qtkv{j}", [128, 128], F32)) for j in range(2)]
    qro = [Ez(nc.sbuf_tensor(f"qro{j}", [128, 1024], BF16)) for j in range(2)]
    kro = [Ez(nc.sbuf_tensor(f"kro{j}", [128, 128], BF16)) for j in range(2)]
    sc1 = Ez(nc.sbuf_tensor("sc1", [128, 1024], F32))
    g = env.__getitem__
    x_fm, wall_sb = g("x_fm"), g("wall_sb")
    st6 = [g("st6_0"), g("st6_1")]
    mvs = [g("mv_0"), g("mv_1")]
    tny = [g("tny_0"), g("tny_1")]

    def colsum(E):
        for c in range(NC):
            wc = wall_sb[:, c * 1152:(c + 1) * 1152]
            for lo, hi in ((0, 512), (512, 1024), (1024, 1152)):
                last = E.matmul(ps_qkv[:1, lo:hi], g("ones_col_b")[:], wc[:, lo:hi],
                                start=(c == 0), stop=(c == NC - 1))
        return last

    t_cs = st.emit("tensor", colsum)
    t_cse = st.emit("vector", lambda E: E.tensor_copy(call[:], ps_qkv[:1, 0:1152]),
                    deps=[t_cs])

    s_dwA = [Ez(nc.semaphore(f"p1wA{j}")) for j in range(2)]
    ps_fa = Ez(nc.psum_tensor("p1fa", [128, 512], F32))
    wsrc = env["wsrc_h"]
    passA = [(w, c) for w in "gud" for c in range(16)]
    tokA = {}
    t_abs = {}
    t_finred = {}
    t_findone = {}

    def wflat(w):
        a = wsrc[w]
        if len(a.shape) == 4:
            return a.rearrange("a b c d -> (a b c d)")
        return a.rearrange("a b -> (a b)")

    def emit_dmaA(k):
        if k < 0 or k >= len(passA):
            return
        w, c = passA[k]
        j = k % 2
        view = wflat(w).rearrange("(c p n) -> p c n", p=128, n=2048)[:, c, :]
        st.emit("sync", lambda E: E.dma_start(
            g(f"wch_{j}")[:], view).then_inc(s_dwA[j], 16),
            deps=[tokA.get(k - 2)])

    def emit_passA(k):
        j = k % 2
        col = k

        def red(E, j=j, col=col, k=k):
            E.wait_ge(s_dwA[j], 16 * (k // 2 + 1))
            return E.tensor_reduce(out=g("wsum")[:, col:col + 1],
                                   in_=g(f"wch_{j}")[:], axis=AX.X, op=Op.add,
                                   apply_absolute_value=True)

        tokA[k] = st.emit("vector", red)
        emit_dmaA(k + 2)

    def emit_finishA(w):
        wi = "gud".index(w)
        f1 = st.emit("vector", lambda E: E.tensor_reduce(
            out=g("tny_0")[:, 14:15], in_=g("wsum")[:, 16 * wi:16 * wi + 16],
            axis=AX.X, op=Op.add), deps=[tokA[16 * wi + c] for c in range(16)])
        f2 = st.emit("tensor", lambda E: E.transpose(
            ps_fa[0:1, 0:128], g("tny_0")[:, 14:15], g("identf")[:]),
            deps=[f1, env["tok_ident"]])

        def finv(E):
            E.tensor_reduce(out=g("t1r")[:, 0:1], in_=ps_fa[0:1, 0:128], axis=AX.X,
                            op=Op.add)
            E.drain()
            E.tensor_scalar(out=g("t1r")[:, 1:2], in0=g("t1r")[:, 0:1],
                            scalar1=1.0 / (I * D), scalar2=EPS, op0=Op.mult,
                            op1=Op.max)
            E.drain()
            return E.reciprocal(g("t1r")[:, 2:3], g("t1r")[:, 1:2])

        t_finred[w] = f3 = st.emit("vector", finv, deps=[f2])
        f4 = st.emit("tensor", lambda E: (
            E.matmul(ps_fa[:, 300:301], g("ones_row_f")[:], g("t1r")[:, 2:3],
                     start=True, stop=True),
            E.matmul(ps_fa[:, 301:302], g("ones_row_f")[:], g("t1r")[:, 1:2],
                     start=True, stop=True))[-1], deps=[f3])

        def fin5(E, w=w, wi=wi):
            E.tensor_copy(g(f"sw_{w}")[:], ps_fa[:, 300:301])
            E.drain()
            if w == "d":
                return E.tensor_copy(g("beta_d")[:], ps_fa[:, 301:302])
            return E.tensor_copy(g("tny_1")[:, 5 + wi:6 + wi], ps_fa[:, 301:302])

        t_findone[w] = st.emit("vector", fin5, deps=[f4])

    def emit_comb():
        def comb(E):
            E.tensor_tensor(out=g("beta_gu2")[:], in0=g("tny_1")[:, 5:6],
                            in1=g("tny_1")[:, 5:6], op=Op.mult)
            E.drain()
            return E.tensor_tensor(out=g("beta_gu2")[:], in0=g("beta_gu2")[:],
                                   in1=g("tny_1")[:, 6:7], op=Op.mult)

        st.emit("vector", comb, deps=[t_findone["g"], t_findone["u"]])

    emit_dmaA(0)
    emit_dmaA(1)
    pA_items = ([("A", k) for k in range(16)] + [("F", "g")]
                + [("A", k) for k in range(16, 32)] + [("F", "u")]
                + [("A", k) for k in range(32, 48)] + [("F", "d"), ("C", None)])
    pA_next = [0]

    def pA_quantum(n):
        for _ in range(n):
            if pA_next[0] < len(pA_items):
                kind, a = pA_items[pA_next[0]]
                pA_next[0] += 1
                if kind == "A":
                    emit_passA(a)
                elif kind == "F":
                    emit_finishA(a)
                else:
                    emit_comb()

    t_stats = {}
    t_ne = None
    t_nm = {}
    t_sq = {}
    for i in range(T):
        j = i % 2

        def stats(E, i=i, j=j):
            xx = g("x_tm")[:, i * D:(i + 1) * D]
            E.bn_stats(st6[j][:, 0:6], xx[:, 0:512])
            E.bn_stats(st6[j][:, 6:12], xx[:, 512:1024])
            E.drain()
            E.bn_aggr(mvs[j][:], st6[j][:])
            E.drain()
            E.tensor_scalar(out=tny[j][:, 0:1], in0=mvs[j][:, 1:2], scalar1=EPS,
                            scalar2=None, op0=Op.add)
            E.drain()
            E.reciprocal(tny[j][:, 1:2], tny[j][:, 0:1])
            E.drain()
            return E.tensor_scalar(out=tny[j][:, 2:3], in0=mvs[j][:, 0:1],
                                   scalar1=-1.0, scalar2=None, op0=Op.mult)

        t_stats[i] = st.emit("vector", stats, deps=[t_nm.get(i - 2),
                                                    t_sq.get(i - 2)])
        t_sq[i] = st.emit("scalar", lambda E, i=i, j=j: E.activation(
            out=rstd_all[:, i:i + 1], in_=tny[j][:, 1:2], func=AF.Sqrt,
            scale=np.float32(1.0)), deps=[t_stats[i]])
        t_nm[i] = st.emit("tensor", lambda E, j=j: E.transpose(
            ps_nm[:1, 0:128], tny[j][:, 2:3], g("identf")[:]),
            deps=[t_stats[i], t_ne, env["tok_ident"]])
        t_ne = st.emit("vector", lambda E, i=i: E.tensor_copy(
            negmrow_all[:, i * 128:(i + 1) * 128], ps_nm[:1, 0:128]), deps=[t_nm[i]])

    t_scal_last = (st, "scalar", st.cnt["scalar"])

    t_ev = {}
    t_rope = {}
    t_tre = None
    for i in range(T):
        j = i % 2

        def qkv(E, i=i):
            for c in range(NC):
                stat = x_fm[:, c * S + i * 128: c * S + (i + 1) * 128]
                wc = wall_sb[:, c * 1152:(c + 1) * 1152]
                for lo, hi in ((0, 512), (512, 1024), (1024, 1152)):
                    E.matmul(ps_qkv[:, lo:hi], stat, wc[:, lo:hi],
                             start=(c == 0), stop=False)
            nrow = negmrow_all[:, i * 128:(i + 1) * 128]
            for lo, hi in ((0, 512), (512, 1024), (1024, 1152)):
                last = E.matmul(ps_qkv[:, lo:hi], nrow, call[:, lo:hi],
                                start=False, stop=True)
            return last

        dq = [t_cse, t_ne, t_ev.get(i - 1)]
        t_mm = st.emit("tensor", qkv, deps=dq)

        def evac(E, i=i, j=j):
            for lo, hi in ((0, 512), (512, 1024)):
                E.activation(out=qtq[j][:, lo:hi], in_=ps_qkv[:, lo:hi],
                             func=AF.Identity, scale=rstd_all[:, i:i + 1])
            E.activation(out=qtkv[j][:], in_=ps_qkv[:, 1024:1152],
                         func=AF.Identity, scale=rstd_all[:, i:i + 1])

        t_ev[i] = st.emit("scalar", evac,
                          deps=[t_mm, t_scal_last if i == 0 else None,
                                t_rope.get(i - 2)])

        def rope(E, i=i, j=j):
            kv = qtkv[j]
            sin_b = g("sin_t")[:, i * 32:(i + 1) * 32].rearrange(
                "p (o k) -> p o k", o=1).broadcast_to((128, H, 32))
            cos_b = g("cos_t")[:, i * 32:(i + 1) * 32].rearrange(
                "p (o k) -> p o k", o=1).broadcast_to((128, H, 32))
            qq = qtq[j][:, 0:1024].rearrange("p (h k) -> p h k", k=64)
            qa, qb = qq[:, :, 0:32], qq[:, :, 32:64]
            ro = qro[j][:, 0:1024].rearrange("p (h k) -> p h k", k=64)
            ra, rb2 = ro[:, :, 0:32], ro[:, :, 32:64]
            w1 = sc1[:, 0:512].rearrange("p (h k) -> p h k", k=32)
            w2 = sc1[:, 512:1024].rearrange("p (h k) -> p h k", k=32)
            E.tensor_tensor(out=w1, in0=qa, in1=cos_b, op=Op.mult)
            E.tensor_tensor(out=w2, in0=qb, in1=sin_b, op=Op.mult)
            E.drain()
            E.tensor_tensor(out=ra, in0=w1, in1=w2, op=Op.subtract)
            E.drain()
            E.tensor_tensor(out=w1, in0=qb, in1=cos_b, op=Op.mult)
            E.tensor_tensor(out=w2, in0=qa, in1=sin_b, op=Op.mult)
            E.drain()
            E.tensor_tensor(out=rb2, in0=w1, in1=w2, op=Op.add)
            E.drain()
            ka, kb = kv[:, 0:32], kv[:, 32:64]
            s1 = g("sin_t")[:, i * 32:(i + 1) * 32]
            c1 = g("cos_t")[:, i * 32:(i + 1) * 32]
            E.tensor_tensor(out=sc1[:, 0:32], in0=ka, in1=c1, op=Op.mult)
            E.tensor_tensor(out=sc1[:, 32:64], in0=kb, in1=s1, op=Op.mult)
            E.tensor_tensor(out=sc1[:, 64:96], in0=kb, in1=c1, op=Op.mult)
            E.tensor_tensor(out=sc1[:, 96:128], in0=ka, in1=s1, op=Op.mult)
            E.drain()
            E.tensor_tensor(out=kro[j][:, 0:32], in0=sc1[:, 0:32], in1=sc1[:, 32:64],
                            op=Op.subtract)
            E.tensor_tensor(out=kro[j][:, 32:64], in0=sc1[:, 64:96],
                            in1=sc1[:, 96:128], op=Op.add)
            E.drain()
            E.tensor_copy(kro[j][:, 64:128], kro[j][:, 0:64])
            E.tensor_copy(g("v1_tm")[:, i * 66:i * 66 + 64], kv[:, 64:128])
            E.drain()
            E.tensor_tensor(out=sc1[:], in0=qro[j][:], in1=qro[j][:], op=Op.mult)
            E.tensor_tensor(out=kv[:, 0:64], in0=kro[j][:, 0:64],
                            in1=kro[j][:, 0:64], op=Op.mult)
            E.drain()
            E.tensor_reduce(out=kv[:, 64:65], in_=kv[:, 0:64], axis=AX.X, op=Op.add)
            E.tensor_reduce(out=kv[:, 80:88],
                            in_=sc1[:, 0:512].rearrange("p (h k) -> p h k", k=64),
                            axis=AX.X, op=Op.add)
            E.tensor_reduce(out=kv[:, 88:96],
                            in_=sc1[:, 512:1024].rearrange("p (h k) -> p h k", k=64),
                            axis=AX.X, op=Op.add)
            E.drain()
            E.tensor_tensor(out=g("qn2acc")[:], in0=g("qn2acc")[:],
                            in1=kv[:, 80:96], op=Op.max)
            return E.tensor_tensor(out=g("kn2acc")[:], in0=g("kn2acc")[:],
                                   in1=kv[:, 64:65], op=Op.max)

        t_rope[i] = st.emit("vector", rope, deps=[t_ev[i]])

        def tposes(E, j=j):
            for c in range(NC):
                E.transpose(ps_tr[:, c * 128:(c + 1) * 128],
                            qro[j][:, c * 128:(c + 1) * 128], g("identb")[:])
            return E.transpose(ps_tr[:, 1024:1152], kro[j][:], g("identb")[:])

        t_tp = st.emit("tensor", tposes, deps=[t_rope[i], env["tok_identb"], t_tre])

        def trevac(E, i=i):
            for c in range(NC):
                E.activation(out=g("q_fm")[:, c * S + i * 128: c * S + (i + 1) * 128],
                             in_=ps_tr[:, c * 128:(c + 1) * 128], func=AF.Identity)
            return E.activation(out=g("kdup_fm")[:, i * 128:(i + 1) * 128],
                                in_=ps_tr[:, 1024:1152], func=AF.Identity)

        t_tre = st.emit("scalar", trevac, deps=[t_tp])
        pA_quantum(4)

    pA_quantum(len(pA_items))

    if env["debug"]:
        dbg = env["dbg"]
        s_dbg = Ez(nc.semaphore("p1dbg"))

        def dbgdma(E):
            E.dma_start(dbg["dbg_qfm"], g("q_fm")[:]).then_inc(s_dbg, 16)
            E.dma_start(dbg["dbg_kdup"], g("kdup_fm")[:]).then_inc(s_dbg, 16)
            E.dma_start(dbg["dbg_v1"], g("v1_tm")[:]).then_inc(s_dbg, 16)
            E.wait_ge(s_dbg, 48)
            return E.nop()

        st.emit("gpsimd", dbgdma, deps=[t_tre, t_rope[T - 1], t_rope[T - 2]])
    st.build()


def build_p2(nc, stack, env):
    st = Sched(nc, stack, "p2")
    Ez = stack.enter_context
    s_dwo = Ez(nc.semaphore("p2dwo"))
    s_dq = [Ez(nc.semaphore(f"p2q{j}")) for j in range(2)]
    s_dxs = Ez(nc.semaphore("p2xs"))
    s_dwA = [Ez(nc.semaphore(f"p2dA{j}")) for j in range(2)]
    s_dwB = [Ez(nc.semaphore(f"p2dB{j}")) for j in range(2)]
    ps_st = [Ez(nc.psum_tensor(f"p2st{j}", [128, 512], F32)) for j in range(2)]
    ps_o = [Ez(nc.psum_tensor(f"p2o{j}", [128, 512], F32)) for j in range(2)]
    ps_wo = Ez(nc.psum_tensor("p2wo", [128, 1024], F32))
    ps_tr = Ez(nc.psum_tensor("p2tr", [128, 1024], BF16))
    ps_bc = Ez(nc.psum_tensor("p2bc", [128, 512], F32))
    g = env.__getitem__
    st6 = [g("st6_0"), g("st6_1")]
    mvs = [g("mv_0"), g("mv_1")]
    rstd = [g("rstd_0"), g("rstd_1")]
    negmr = [g("negmr_0"), g("negmr_1")]
    tny = [g("tny_0"), g("tny_1")]
    t1r, t16r = g("t1r"), g("t16r")
    osum = [g("osum_0"), g("osum_1")]

    def _woT(E):
        E.dma_start(g("woT_sb")[:],
                    g("woT_h").rearrange("(c p) n -> p c n", p=128)).then_inc(s_dwo, 16)
        return E.nop()

    st.emit("gpsimd", _woT)
    t_ptms = st.emit("vector", lambda E: E.memset(g("pt_sb")[:], 0.0))

    tk = st.emit("tensor", lambda E: (
        E.transpose(ps_bc[0:16, 0:128], g("qn2acc")[:], g("identf")[:]),
        E.transpose(ps_bc[0:1, 128:256], g("kn2acc")[:], g("identf")[:]))[-1],
        deps=[env["tok_ident"]])

    def nb_v2(E):
        E.tensor_reduce(out=t16r[:, 0:1], in_=ps_bc[0:16, 0:128], axis=AX.X, op=Op.max)
        E.drain()
        return E.tensor_reduce(out=t1r[:, 0:1], in_=ps_bc[0:1, 128:256], axis=AX.X,
                               op=Op.max)

    tk = st.emit("vector", nb_v2, deps=[tk])
    tk2 = st.emit("tensor", lambda E: E.matmul(
        ps_bc[0:16, 256:257], g("onerow16")[:], t1r[:, 0:1], start=True, stop=True),
        deps=[tk])
    tk3 = st.emit("vector", lambda E: E.tensor_tensor(
        out=t16r[:, 1:2], in0=t16r[:, 0:1], in1=ps_bc[0:16, 256:257], op=Op.mult),
        deps=[tk2])
    tk4 = st.emit("scalar", lambda E: E.activation(
        out=t16r[:, 2:3], in_=t16r[:, 1:2], func=AF.Sqrt, scale=np.float32(1.0 / 64.0), bias=np.float32(1e-30)),
        deps=[tk3])
    tk5 = st.emit("vector", lambda E: E.tensor_scalar(
        out=t16r[:, 3:4], in0=t16r[:, 2:3], scalar1=-1.0, scalar2=None, op0=Op.mult),
        deps=[tk4])
    tk6 = st.emit("tensor", lambda E: E.transpose(
        ps_bc[0:1, 256:272], t16r[:, 3:4], g("identf")[0:16, 0:16]), deps=[tk5])
    tk7 = st.emit("vector", lambda E: E.tensor_copy(t1r[:, 8:24], ps_bc[0:1, 256:272]),
                  deps=[tk6])
    tk8 = st.emit("tensor", lambda E: E.matmul(
        ps_bc[:, 256:272], g("ones_row_f")[:], t1r[:, 8:24], start=True, stop=True),
        deps=[tk7])
    t_negb = st.emit("vector", lambda E: E.tensor_copy(g("negBtab")[:], ps_bc[:, 256:272]),
                     deps=[tk8])

    wsrc, tern = env["wsrc_h"], env["tern_h"]
    passA = [(w, c) for w in "gud" for c in range(16)]
    tokB = {}
    tokBv = {}

    def wflat(w):
        a = wsrc[w]
        if len(a.shape) == 4:
            return a.rearrange("a b c d -> (a b c d)")
        return a.rearrange("a b -> (a b)")

    def emit_passB(k):
        w, c = passA[k]
        j = k % 2
        view = wflat(w).rearrange("(c p n) -> p c n", p=128, n=2048)[:, c, :]
        tB = st.emit("sync", lambda E: E.dma_start(
            g(f"wch_{j}")[:], view).then_inc(s_dwA[j], 16),
            deps=[tokBv.get(k - 2)])

        def tzg(E, w=w, j=j, k=k):
            E.wait_ge(s_dwA[j], 16 * (k // 2 + 1))
            return E.tensor_scalar(out=g("zch2")[:], in0=g(f"wch_{j}")[:],
                                   scalar1=g(f"sw_{w}")[:], scalar2=1.0,
                                   op0=Op.mult, op1=Op.mult)

        tzg_tok = st.emit("gpsimd", tzg, deps=[tokBv.get(k - 2)])

        def tz(E, j=j, k=k):
            E.wait_ge(s_dwB[0], 16 * k)
            E.tensor_scalar(out=g("pch")[:], in0=g("zch2")[:], scalar1=0.5,
                            scalar2=None, op0=Op.is_ge)
            E.drain()
            return E.scalar_tensor_tensor(out=g("tch_0")[:], in0=g("zch2")[:],
                                          scalar=-0.5, in1=g("pch")[:], op0=Op.is_le,
                                          op1=Op.subtract)

        tokBv[k] = st.emit("vector", tz, deps=[tzg_tok])
        tern_view = tern[w].rearrange("(c p n) -> p c n", p=128, n=2048)[:, c, :]
        tokB[k] = st.emit("sync", lambda E: E.dma_start(
            tern_view, g("tch_0")[:]).then_inc(s_dwB[0], 16), deps=[tokBv[k]])

    woT_sb, pt_sb, o_fmt = g("woT_sb"), g("pt_sb"), g("o_fmt")
    o_tm = [g("o_tm0"), g("o_tm1")]
    q_fm, kdup_fm, v1_tm = g("q_fm"), g("kdup_fm"), g("v1_tm")
    x_tm, ybuf, oraw = g("x_tm"), g("ybuf"), g("oraw")

    prepseq = [("B", k) for k in range(len(passA))]
    prep_state = {"B": 0}

    def prep_quantum():
        if prep_state["B"] < len(prepseq):
            kind, a = prepseq[prep_state["B"]]
            prep_state["B"] += 1
            emit_passB(a)

    pend2 = deque()

    def make_wo_quanta(B):
        qs = []
        for q in range(QB):
            i = B * QB + q
            j = i % 2

            def qA(E=None, B=B, q=q, i=i, pb=B % 2):
                def otp(E, q=q, pb=pb):
                    for c in range(NC):
                        last = E.transpose(ps_tr[:, c * 128:(c + 1) * 128],
                                           o_tm[pb][:, q * D + c * 128:
                                                     q * D + (c + 1) * 128],
                                           g("identb")[:])
                    return last

                t_ot = st.emit("tensor", otp,
                               deps=[t_oev[(B, h, q)] for h in range(H)]
                                    + [env["tok_identb"], chain2["ps_tr_r"]])
                t_otp[(B, q)] = t_ot

                def otrev(E):
                    for c in range(NC):
                        last = E.tensor_copy(o_fmt[:, c * 128:(c + 1) * 128],
                                             ps_tr[:, c * 128:(c + 1) * 128])
                    return last

                chain2["trev"] = chain2["ps_tr_r"] = st.emit(
                    "vector", otrev, deps=[t_ot, chain2["wo"]])

            def qB_(E=None, B=B, q=q, i=i, j=j):
                def womm(E):
                    E.wait_ge(s_dwo, 16)
                    for c in range(NC):
                        stat = o_fmt[:, c * 128:(c + 1) * 128]
                        wc = woT_sb[:, c * D:(c + 1) * D]
                        E.matmul(ps_wo[:, 0:512], stat, wc[:, 0:512],
                                 start=(c == 0), stop=(c == NC - 1))
                        last = E.matmul(ps_wo[:, 512:1024], stat, wc[:, 512:1024],
                                        start=(c == 0), stop=(c == NC - 1))
                    return last

                t_wo = st.emit("tensor", womm, deps=[chain2["trev"], chain2["ye"]])
                chain2["wo"] = t_wo

                def lnstats(E, j=j):
                    E.bn_stats(st6[j][:, 0:6], ps_wo[:, 0:512])
                    E.bn_stats(st6[j][:, 6:12], ps_wo[:, 512:1024])
                    E.drain()
                    E.bn_aggr(mvs[j][:], st6[j][:])
                    E.drain()
                    E.tensor_scalar(out=tny[j][:, 0:1], in0=mvs[j][:, 1:2],
                                    scalar1=EPS, scalar2=None, op0=Op.add)
                    E.drain()
                    return E.reciprocal(tny[j][:, 1:2], tny[j][:, 0:1])

                t_ls = st.emit("vector", lnstats, deps=[t_wo, t_as3.get(i - 2)])
                chain2["rs"] = st.emit("scalar", lambda E, j=j: E.activation(
                    out=rstd[j][:], in_=tny[j][:, 1:2], func=AF.Sqrt,
                    scale=np.float32(1.0)), deps=[t_ls])

            def qC_(E=None, B=B, q=q, i=i, j=j):
                t_n2 = st.emit("vector", lambda E, j=j: E.tensor_scalar(
                    out=negmr[j][:], in0=mvs[j][:, 0:1], scalar1=rstd[j][:],
                    scalar2=-1.0, op0=Op.mult, op1=Op.mult), deps=[chain2["rs"]])

                def yev(E, j=j):
                    return E.activation(out=ybuf[:], in_=ps_wo[:, 0:1024],
                                        func=AF.Identity,
                                        scale=rstd[j][:], bias=negmr[j][:])

                chain2["ye"] = t_ye = st.emit("scalar", yev,
                                              deps=[t_n2, chain2["resid"]])

                def resid(E, i=i):
                    return E.tensor_tensor(out=x_tm[:, i * D:(i + 1) * D],
                                           in0=x_tm[:, i * D:(i + 1) * D],
                                           in1=ybuf[:], op=Op.add)

                t_resid[i] = st.emit("vector", resid, deps=[t_ye])
                chain2["resid"] = t_resid[i]

                def xspill(E, i=i):
                    E.dma_start(g("xsp_h").rearrange("(t p) d -> p t d", p=128)[:, i, :],
                                x_tm[:, i * D:(i + 1) * D]).then_inc(s_dxs, 16)

                st.emit("sync", xspill, deps=[t_resid[i]])

            def qD_(E=None, B=B, q=q, i=i, j=j):
                def lnq_stats(E, i=i, j=j):
                    xx = x_tm[:, i * D:(i + 1) * D]
                    E.bn_stats(st6[j][:, 0:6], xx[:, 0:512])
                    E.bn_stats(st6[j][:, 6:12], xx[:, 512:1024])
                    E.drain()
                    E.bn_aggr(mvs[j][:], st6[j][:])
                    E.drain()
                    E.tensor_scalar(out=tny[j][:, 0:1], in0=mvs[j][:, 1:2],
                                    scalar1=EPS, scalar2=None, op0=Op.add)
                    E.drain()
                    return E.reciprocal(tny[j][:, 1:2], tny[j][:, 0:1])

                t_q_ls = st.emit("vector", lnq_stats,
                                 deps=[t_resid[i], t_as3.get(i - 2)])
                t_q_rs = st.emit("scalar", lambda E, j=j: E.activation(
                    out=rstd[j][:], in_=tny[j][:, 1:2], func=AF.Sqrt,
                    scale=np.float32(1.0)), deps=[t_q_ls])
                chain2["qnm"] = st.emit("vector", lambda E, j=j: E.tensor_scalar(
                    out=negmr[j][:], in0=mvs[j][:, 0:1], scalar1=rstd[j][:],
                    scalar2=-1.0, op0=Op.mult, op1=Op.mult), deps=[t_q_rs])

            def qE_(E=None, B=B, q=q, i=i, j=j):
                t_q_l2 = st.emit("scalar", lambda E, i=i, j=j: E.activation(
                    out=g(f"ln2_{j}")[:], in_=x_tm[:, i * D:(i + 1) * D],
                    func=AF.Identity, scale=rstd[j][:], bias=negmr[j][:]),
                    deps=[chain2["qnm"]])

                def lnq_scale(E, i=i, j=j):
                    E.tensor_reduce(out=tny[j][:, 2:3], in_=g(f"ln2_{j}")[:],
                                    axis=AX.X, op=Op.max,
                                    apply_absolute_value=True)
                    E.drain()
                    E.tensor_scalar(out=g("mxq")[:, i:i + 1], in0=tny[j][:, 2:3],
                                    scalar1=EPS, scalar2=None, op0=Op.max)
                    E.drain()
                    E.reciprocal(tny[j][:, 3:4], g("mxq")[:, i:i + 1])
                    E.drain()
                    return E.tensor_scalar(out=tny[j][:, 4:5], in0=tny[j][:, 3:4],
                                           scalar1=127.0, scalar2=None,
                                           op0=Op.mult)

                chain2["qqs"] = st.emit("vector", lnq_scale, deps=[t_q_l2])

            def qF_(E=None, B=B, q=q, i=i, j=j):
                def lnq_absign(E, j=j):
                    E.activation(out=g(f"sgq_{j}")[:], in_=g(f"ln2_{j}")[:],
                                 func=AF.Sign)
                    E.drain()
                    return E.activation(out=g(f"ln2_{j}")[:], in_=g(f"ln2_{j}")[:],
                                        func=AF.Abs, scale=tny[j][:, 4:5])

                t_as3[i] = st.emit("scalar", lnq_absign,
                                   deps=[chain2["qqs"], t_rq3.get(i - 2)])

                def lnq_round(E, j=j):
                    E.tensor_scalar(out=g("i16_0")[:], in0=g(f"ln2_{j}")[:],
                                    scalar1=g("delta")[:], scalar2=None, op0=Op.add)
                    E.drain()
                    return E.tensor_tensor(out=g("qtm_0")[:], in0=g("i16_0")[:],
                                           in1=g(f"sgq_{j}")[:], op=Op.mult)

                t_rq3[i] = st.emit("vector", lnq_round,
                                   deps=[t_as3[i], t_qtp3.get(i - 1)])

            def qG_(E=None, B=B, q=q, i=i, j=j):
                def lnq_tp(E):
                    for c in range(NC):
                        last = E.transpose(ps_tr[:, c * 128:(c + 1) * 128],
                                           g("qtm_0")[:, c * 128:(c + 1) * 128],
                                           g("identb")[:])
                    return last

                t_qtp3[i] = st.emit("tensor", lnq_tp,
                                    deps=[t_rq3[i], chain2["ps_tr_r"],
                                          env["tok_identb"]])

                def lnq_trev(E, i=i):
                    if i >= 1:
                        E.wait_ge(s_dq[0], 16 * i)
                    for c in range(NC):
                        last = E.tensor_copy(g("qst_0")[:, c * 128:(c + 1) * 128],
                                             ps_tr[:, c * 128:(c + 1) * 128])
                    return last

                t_qtrev3[i] = st.emit("vector", lnq_trev, deps=[t_qtp3[i]])
                chain2["ps_tr_r"] = t_qtrev3[i]

                st.emit("sync", lambda E, i=i: E.dma_start(
                    g("qsp_h")[i], g("qst_0")[:]).then_inc(s_dq[0], 16),
                    deps=[t_qtrev3[i]])

            qs.append([qA, qB_, qC_, qD_, qE_, qF_, qG_])
        pat = [(0, 0), (0, 1), (0, 2), (1, 0), (0, 3), (1, 1), (0, 4), (1, 2),
               (0, 5), (1, 3), (0, 6), (1, 4), (2, 0), (1, 5), (2, 1), (1, 6),
               (2, 2), (3, 0), (2, 3), (3, 1), (2, 4), (3, 2), (2, 5), (3, 3),
               (2, 6), (3, 4), (3, 5), (3, 6)]
        return [qs[q][k] for q, k in pat]

    t_exp = {}
    t_pv = {}
    t_oev = {}
    t_cpd = {}
    t_as3 = {}
    t_rq3 = {}
    t_qtp3 = {}
    t_qtrev3 = {}
    t_otp = {}
    t_resid = {}
    last_pv_block = {}
    chain2 = {"trev": None, "ye": None, "resid": None, "rs": None,
              "qnm": None, "qqs": None, "wo": None, "ps_tr_r": None}

    for B in range(NB):
        nkt = QB * (B + 1)
        for h in range(H):
            hb = 64 * (h % 2)
            chunk = h // 2
            for kt in range(nkt):
                lo = max(kt * 128, B * 512)
                hi = (B + 1) * 512
                segw = hi - lo
                par = kt % 3

                def strip(E, kt=kt, lo=lo, hi=hi, segw=segw, par=par, hb=hb, chunk=chunk):
                    dst = ps_bc if par == 2 else ps_st[par]
                    return E.matmul(dst[:, 0:segw],
                                    kdup_fm[hb:hb + 64, kt * 128:(kt + 1) * 128],
                                    q_fm[hb:hb + 64, chunk * S + lo: chunk * S + hi],
                                    start=True, stop=True)

                dstrip = []
                if kt >= 3:
                    dstrip.append(t_exp[(B, h, kt - 3)])
                elif h > 0:
                    dstrip.append(t_exp[(B, h - 1, nkt - 3 + kt)])
                elif B > 0:
                    dstrip.append(t_exp[(B - 1, H - 1, QB * B - 3 + kt)])
                elif par == 2:
                    dstrip.append(t_negb)
                t_s = st.emit("tensor", strip, deps=dstrip)

                def expf(E, h=h, kt=kt, lo=lo, segw=segw, par=par, B=B):
                    off = kt * 512 + (lo - B * 512)
                    src_ps = ps_bc if par == 2 else ps_st[par]
                    return E.activation(out=pt_sb[:, off:off + segw],
                                        in_=src_ps[:, 0:segw],
                                        func=AF.Exp, scale=np.float32(0.125),
                                        bias=g("negBtab")[:, h:h + 1])

                dexp = [t_s]
                if B == 0 and h == 0 and kt == 0:
                    dexp.append(t_negb)
                    dexp.append(t_ptms)
                if h > 0:
                    dexp.append(t_pv[(B, h - 1, QB - 1)])
                elif B > 0:
                    dexp.append(t_pv[(B - 1, H - 1, QB - 1)])
                t_exp[(B, h, kt)] = st.emit("scalar", expf, deps=dexp)

            t_masks = []
            for qd in range(QB):
                def mask(E, B=B, qd=qd):
                    off0 = (B * QB) * 512 + (B * QB * 128 - B * 512) + 640 * qd
                    ap = bass.AP(pt_sb.tensor if hasattr(pt_sb, "tensor") else pt_sb,
                                 off0, [[T * 512, 128], [1, 128]])
                    return E.affine_select(ap, ap, [[1, 128]], Op.is_ge, 0.0,
                                           base=0, channel_multiplier=-1)

                t_masks.append(st.emit("gpsimd", mask,
                                       deps=[t_exp[(B, h, B * QB + qd)]]))

            for q in range(QB):
                qt = B * QB + q
                par = q % 2

                def pv(E, B=B, qt=qt, par=par):
                    for kt in range(qt + 1):
                        off = kt * 512 + (qt * 128 - B * 512)
                        last = E.matmul(ps_o[par][:, 0:65], pt_sb[:, off:off + 128],
                                        v1_tm[:, kt * 66:kt * 66 + 65],
                                        start=(kt == 0), stop=(kt == qt))
                    return last

                dpv = t_masks[:q + 1] + [t_exp[(B, h, qt)]]
                if q >= 2:
                    dpv.append(t_cpd.get((B, h, q - 2)))
                elif h > 0:
                    dpv.append(t_cpd.get((B, h - 1, q + 2)))
                elif B > 0:
                    dpv.append(t_cpd.get((B - 1, H - 1, q + 2)))
                t_pv[(B, h, q)] = st.emit("tensor", pv, deps=dpv)

                dcp = [t_pv[(B, h, q)]]
                if h > 0:
                    dcp.append(t_oev.get((B, h - 1, q)))
                elif B > 0:
                    dcp.append(t_oev.get((B - 1, H - 1, q)))
                t_cpd[(B, h, q)] = st.emit("vector", lambda E, q=q, par=par:
                    E.tensor_copy(oraw[:, q * 65:(q + 1) * 65], ps_o[par][:, 0:65]),
                    deps=dcp)

            t_rc = st.emit("vector", lambda E: E.reciprocal(
                osum[1][:, 0:4], oraw[:].rearrange("p (q c) -> p q c", c=65)[:, :, 64]),
                deps=[t_cpd[(B, h, q)] for q in range(QB)])

            for q in range(QB):
                def oev(E, h=h, q=q, pb=B % 2):
                    return E.tensor_scalar(
                        out=o_tm[pb][:, q * D + h * 64: q * D + (h + 1) * 64],
                        in0=oraw[:, q * 65:q * 65 + 64],
                        scalar1=osum[1][:, q:q + 1], scalar2=None, op0=Op.mult)

                doev = [t_rc]
                if B > 1 and h == 0:
                    doev.append(t_otp.get((B - 2, q)))
                t_oev[(B, h, q)] = st.emit("vector", oev, deps=doev)

            prep_quantum()
            for _ in range(2 if h < 13 else 1):
                if pend2:
                    pend2.popleft()()

        last_pv_block[B] = t_pv[(B, H - 1, QB - 1)]

        pend2.extend(make_wo_quanta(B))

    while pend2:
        pend2.popleft()()
    while prep_state["B"] < len(prepseq):
        prep_quantum()

    def waitB(E):
        E.wait_ge(s_dwB[0], 16 * 48)
        E.wait_ge(s_dxs, 16 * T)
        E.wait_ge(s_dq[0], 16 * T)

    st.emit("sync", waitB, deps=[tokB[len(passA) - 1], tokB[len(passA) - 2]])

    if env["debug"]:
        dbg = env["dbg"]
        s_dbg = Ez(nc.semaphore("p2dbg"))

        def dbgdma(E):
            E.dma_start(dbg["dbg_x1"].rearrange("(t p) d -> p t d", p=128),
                        x_tm[:].rearrange("p (t d) -> p t d", d=D)).then_inc(s_dbg, 16)
            E.dma_start(dbg["dbg_negb"], g("negBtab")[:]).then_inc(s_dbg, 16)
            E.dma_start(dbg["dbg_pt"], pt_sb[:]).then_inc(s_dbg, 16)
            E.dma_start(dbg["dbg_otm"], o_tm[1][:]).then_inc(s_dbg, 16)
            E.dma_start(dbg["dbg_tern"], tern["g"]).then_inc(s_dbg, 16)
            E.wait_ge(s_dbg, 80)
            return E.nop()

        st.emit("gpsimd", dbgdma,
                deps=[t_resid[T - 1], t_negb, tokB[len(passA) - 1]])

    env["tok_resid_p2"] = dict(t_resid)
    st.build()


def build_p3b(nc, stack, env):
    st = Sched(nc, stack, "p3b")
    Ez = stack.enter_context
    s_qb = [Ez(nc.semaphore(f"p3bq{j}")) for j in range(2)]
    s_dt = [Ez(nc.semaphore(f"p3bt{j}")) for j in range(4)]
    s_tdc = [Ez(nc.semaphore(f"p3bd{j}")) for j in range(3)]
    s_dx = [Ez(nc.semaphore(f"p3bx{j}")) for j in range(2)]
    s_dy = Ez(nc.semaphore("p3by"))
    ps_gu = [Ez(nc.psum_tensor(f"p3gu{j}", [128, 512], F32)) for j in range(4)]
    ps_d = [Ez(nc.psum_tensor(f"p3d{j}", [128, 512], F32)) for j in range(2)]
    ps_sc = [Ez(nc.psum_tensor(f"p3sc{j}", [128, 512], F32)) for j in range(2)]
    g = env.__getitem__
    st6 = [g("st6_0"), g("st6_1")]
    mvs = [g("mv_0"), g("mv_1")]
    rstd = [g("rstd_0"), g("rstd_1")]
    negmr = [g("negmr_0"), g("negmr_1")]
    tny = [g("tny_0"), g("tny_1")]
    tern = env["tern_h"]
    s4 = g("s4")

    t_qdma = {}
    t_tdma = {}
    t_mm_gu = {}
    t_ev = {}
    t_h1 = {}
    t_h = {}
    t_mx = {}
    t_habs = {}
    t_mt = {}
    t_cs4 = {}
    t_csrow = {}
    t_bcm = {}
    t_bce = {}
    t_m1 = {}
    t_sg = {}
    t_rds = {}
    t_fill = {}
    t_fev = {}
    t_fev2 = {}
    t_y3 = {}
    t_store = {}
    t_xr = {}
    chain = {"r3b": None}
    nqb = [0, 0]
    ndt = [0] * 4
    ndc = [0] * 3
    pend = deque()

    def emit_gu(b3, ic):
        par = ic % 4
        par_b = b3 % 2
        jj = ic % 2

        def dma_gu(E, ic=ic, par=par):
            gv = tern["g"].rearrange("(a p n) -> a p n", p=128, n=NC * 128)[ic]
            uv = tern["u"].rearrange("(a p n) -> a p n", p=128, n=NC * 128)[ic]
            E.dma_start(g(f"tg_{par}")[:], gv).then_inc(s_dt[par], 16)
            E.dma_start(g(f"tu_{par}")[:], uv).then_inc(s_dt[par], 16)

        t_tdma[(b3, ic)] = st.emit("sync", dma_gu,
                                   deps=[t_mm_gu.get((b3, ic - 4)),
                                         t_mm_gu.get((b3 - 1, IC - 4 + ic))
                                         if ic < 4 else None])
        ndt[par] += 2

        def mm_gu(E, par=par, par_b=par_b, dtt=16 * ndt[par], qbt=16 * nqb[par_b]):
            pp = par % 2
            E.wait_ge(s_dt[par], dtt)
            E.wait_ge(s_qb[par_b], qbt)
            qb = g(f"qb_{par_b}")[:].rearrange("p (t n) -> p t n", n=1024)
            for c2 in range(NC):
                mov = qb[:, :, c2 * 128:(c2 + 1) * 128]
                E.matmul(ps_gu[pp][:, 0:TB], g(f"tg_{par}")[:, c2 * 128:(c2 + 1) * 128],
                         mov, start=(c2 == 0), stop=(c2 == NC - 1))
                last = E.matmul(ps_gu[2 + pp][:, 0:TB],
                                g(f"tu_{par}")[:, c2 * 128:(c2 + 1) * 128],
                                mov, start=(c2 == 0), stop=(c2 == NC - 1))
            return last

        dmm = list(t_ev.get((b3, ic - 2)) or ())
        if ic < 2:
            dmm += list(t_ev.get((b3 - 1, IC - 2 + ic)) or ())
        t_mm_gu[(b3, ic)] = st.emit("tensor", mm_gu, deps=dmm)

        dev = [t_mm_gu[(b3, ic)], t_h.get((b3, ic - 2))]
        if ic < 2:
            dev.append(t_h.get((b3 - 1, IC - 2 + ic)))
        t_eg = st.emit("scalar", lambda E, pp=jj: E.activation(
            out=g(f"rb_{pp}")[:], in_=ps_gu[pp][:, 0:TB], func=AF.Relu,
            scale=np.float32(-1.0)), deps=dev)
        t_eu = st.emit("scalar", lambda E, pp=jj: E.activation(
            out=g(f"ub_{pp}")[:], in_=ps_gu[2 + pp][:, 0:TB], func=AF.Identity),
            deps=dev)
        t_ev[(b3, ic)] = (t_eg, t_eu)

        t_h1[(b3, ic)] = st.emit("gpsimd", lambda E, pp=jj: E.tensor_tensor(
            out=g(f"ub_{pp}")[:], in0=g(f"rb_{pp}")[:], in1=g(f"ub_{pp}")[:],
            op=Op.mult), deps=[t_eg, t_eu, t_m1.get((b3 - 1, ic))])
        t_h[(b3, ic)] = st.emit("gpsimd", lambda E, ic=ic, pp=jj: E.tensor_tensor(
            out=g("h_sb")[:, ic * TB:(ic + 1) * TB], in0=g(f"ub_{pp}")[:],
            in1=g(f"rb_{pp}")[:], op=Op.mult), deps=[t_h1[(b3, ic)]])

        dab = [t_h[(b3, ic)], t_mx.get((b3, ic - 2))]
        if ic < 2:
            dab.append(t_mx.get((b3 - 1, IC - 2 + ic)))
        t_habs[(b3, ic)] = st.emit("scalar", lambda E, ic=ic: E.activation(
            out=g(f"habs_{ic % 2}")[:], in_=g("h_sb")[:, ic * TB:(ic + 1) * TB],
            func=AF.Abs), deps=dab)

    def emit_mxup(b3, ic):
        dmx = [t_habs[(b3, ic)], t_mx.get((b3, ic - 1))]
        pr = ic % 2
        if ic == 0:
            dmx.append(t_mt.get(b3 - 1))
            fn = lambda E, pr=pr: E.tensor_copy(g("mxacc")[:], g(f"habs_{pr}")[:])
        else:
            fn = lambda E, pr=pr: E.tensor_tensor(out=g("mxacc")[:], in0=g("mxacc")[:],
                                                  in1=g(f"habs_{pr}")[:], op=Op.max)
        t_mx[(b3, ic)] = st.emit("vector", fn, deps=dmx)

    def emit_scales(b3):
        def mxtp(E):
            for q4 in range(4):
                last = E.transpose(ps_sc[0][:, q4 * 128:(q4 + 1) * 128],
                                   g("mxacc")[:, q4 * 128:(q4 + 1) * 128],
                                   g("identf")[:])
            return last

        t_mt[b3] = st.emit("tensor", mxtp,
                           deps=[t_mx[(b3, IC - 1)], env["tok_ident"],
                                 t_bce.get(b3 - 1)] + list(t_ev[(b3, IC - 1)]))

        def mx4(E):
            for tt in range(4):
                last = E.tensor_reduce(out=s4[:, tt:tt + 1],
                                       in_=ps_sc[0][:, tt * 128:(tt + 1) * 128],
                                       axis=AX.X, op=Op.max)
            return last

        t_m4 = st.emit("vector", mx4, deps=[t_mt[b3], t_cs4.get(b3 - 1)])

        def csc4(E, b3=b3):
            mq = g("mxq")[:, 4 * b3:4 * b3 + 4]
            E.tensor_scalar(out=s4[:, 4:8], in0=mq, scalar1=1.0 / 127.0,
                            scalar2=None, op0=Op.mult)
            E.drain()
            E.tensor_tensor(out=s4[:, 8:12], in0=s4[:, 4:8], in1=s4[:, 4:8], op=Op.mult)
            E.drain()
            E.tensor_tensor(out=s4[:, 8:12], in0=s4[:, 8:12], in1=s4[:, 4:8], op=Op.mult)
            E.drain()
            E.tensor_scalar(out=s4[:, 8:12], in0=s4[:, 8:12], scalar1=g("beta_gu2")[:],
                            scalar2=None, op0=Op.mult)
            E.drain()
            E.tensor_tensor(out=s4[:, 4:8], in0=s4[:, 8:12], in1=s4[:, 0:4], op=Op.mult)
            E.drain()
            E.tensor_scalar(out=s4[:, 4:8], in0=s4[:, 4:8], scalar1=EPS,
                            scalar2=None, op0=Op.max)
            E.drain()
            E.reciprocal(s4[:, 0:4], s4[:, 4:8])
            E.drain()
            E.tensor_scalar(out=s4[:, 8:12], in0=s4[:, 8:12], scalar1=127.0,
                            scalar2=None, op0=Op.mult)
            E.drain()
            E.tensor_tensor(out=s4[:, 12:16], in0=s4[:, 8:12], in1=s4[:, 0:4],
                            op=Op.mult)
            E.drain()
            PA = 16 + 8 * (b3 % 2)
            E.tensor_scalar(out=s4[:, PA:PA + 4], in0=s4[:, 4:8],
                            scalar1=g("beta_d")[:], scalar2=1.0 / 127.0,
                            op0=Op.mult, op1=Op.mult)
            E.drain()
            return E.tensor_tensor(out=s4[:, PA + 4:PA + 8], in0=s4[:, PA:PA + 4],
                                   in1=s4[:, PA:PA + 4], op=Op.mult)

        dcs = [t_m4, t_csrow.get(b3 - 1), t_y3.get((b3 - 2, 3))]
        t_cs4[b3] = st.emit("vector", csc4, deps=dcs)

        def cstp(E):
            for tt in range(4):
                last = E.transpose(ps_sc[1][:1, tt * 128:(tt + 1) * 128],
                                   s4[:, 12 + tt:13 + tt], g("identf")[:])
            return last

        t_ctp = st.emit("tensor", cstp, deps=[t_cs4[b3], t_csrow.get(b3 - 1)])
        t_csrow[b3] = st.emit("vector", lambda E: E.tensor_copy(
            g("csrow")[:], ps_sc[1][:1, 0:512]), deps=[t_ctp, t_bcm.get(b3 - 1)])
        t_bcm[b3] = st.emit("tensor", lambda E: E.matmul(
            ps_sc[0][:, 0:TB], g("ones_row_f")[:], g("csrow")[:],
            start=True, stop=True), deps=[t_csrow[b3], t_m4])
        t_bce[b3] = st.emit("vector", lambda E: E.tensor_copy(
            g("csb")[:], ps_sc[0][:, 0:TB]),
            deps=[t_bcm[b3], t_m1.get((b3 - 1, IC - 1))])

    def emit_m1_sg(b3, ic):
        pr = ic % 2
        dm1 = [t_bce[b3], t_rds.get((b3, ic - 2))]
        if ic < 2:
            dm1.append(t_rds.get((b3 - 1, IC - 2 + ic)))
        t_m1[(b3, ic)] = st.emit("vector", lambda E, ic=ic, pr=pr: E.tensor_tensor(
            out=g(f"m1b_{pr}")[:], in0=g("h_sb")[:, ic * TB:(ic + 1) * TB],
            in1=g("csb")[:], op=Op.mult), deps=dm1)

        def sgabs(E, pr=pr):
            E.activation(out=g(f"sgb_{pr}")[:], in_=g(f"m1b_{pr}")[:], func=AF.Sign)
            E.drain()
            return E.activation(out=g(f"m1b_{pr}")[:], in_=g(f"m1b_{pr}")[:],
                                func=AF.Abs)

        t_sg[(b3, ic)] = st.emit("scalar", sgabs, deps=[t_m1[(b3, ic)]])

    def emit_rnd2(b3, ic):
        pr = ic % 2

        def rnd2(E, ic=ic, pr=pr):
            E.tensor_scalar(out=g(f"i16f_{pr}")[:], in0=g(f"m1b_{pr}")[:],
                            scalar1=g("delta")[:], scalar2=None, op0=Op.add)
            E.drain()
            return E.tensor_tensor(out=g("hq_sb")[:, ic * TB:(ic + 1) * TB],
                                   in0=g(f"i16f_{pr}")[:], in1=g(f"sgb_{pr}")[:],
                                   op=Op.mult)

        t_rds[(b3, ic)] = st.emit("vector", rnd2,
                                  deps=[t_m1[(b3, ic)], t_sg[(b3, ic)],
                                        t_fill.get((b3 - 1, ic // CH, 3, 1))])

    def emit_tdc_dma(j):
        sl = j % 3
        dprev = None
        if j >= 3:
            bp, rp = divmod(j - 3, 8)
            cp, dhp = divmod(rp, 2)
            dprev = t_fill.get((bp, cp, 3, dhp))
        bj, rj = divmod(j, 8)
        c, dh = divmod(rj, 2)

        def dmaf(E, c=c, dh=dh, sl=sl):
            dv = tern["d"].rearrange("(h c p n) -> h c p n",
                                     c=NCH, p=128, n=CH * 512)[dh, c]
            E.dma_start(g(f"tdc_{sl}")[:], dv).then_inc(s_tdc[sl], 16)

        st.emit("sync", dmaf, deps=[dprev])
        ndc[sl] += 1

    def emit_fill(b3, c, tg, dh):
        j = b3 * 8 + c * 2 + dh
        sl = j % 3
        f = c * 8 + tg * 2 + dh
        bank = f % 2

        def mm(E, c=c, tg=tg, sl=sl, bank=bank, dct=16 * ndc[sl]):
            E.wait_ge(s_tdc[sl], dct)
            for icl in range(CH):
                ic = c * CH + icl
                last = E.matmul(
                    ps_d[bank][:, 0:TB],
                    g("hq_sb")[:, ic * TB + tg * 128: ic * TB + (tg + 1) * 128],
                    g(f"tdc_{sl}")[:, icl * 512:(icl + 1) * 512],
                    start=(icl == 0), stop=(icl == CH - 1))
            return last

        dmm = [t_rds[(b3, c * CH + CH - 1)], t_fev.get((b3, f - 2))]
        if f < 2:
            dmm.append(t_fev.get((b3 - 1, 30 + f)))
        t_fill[(b3, c, tg, dh)] = tok = st.emit("tensor", mm, deps=dmm)

        def ev(E, c=c, tg=tg, dh=dh, bank=bank):
            dst = g("d_tm")[:, tg * D + dh * 512: tg * D + dh * 512 + 512]
            if c == 0:
                return E.tensor_copy(dst, ps_d[bank][:, 0:TB])
            return E.tensor_tensor(out=dst, in0=dst, in1=ps_d[bank][:, 0:TB],
                                   op=Op.add)

        dev = [tok]
        if c == 0:
            dev.append(t_y3.get((b3 - 1, tg)))
        else:
            dev.append(t_fev2.get((b3, c - 1, tg, dh)))
        t_fev[(b3, f)] = t_fev2[(b3, c, tg, dh)] = st.emit("vector", ev, deps=dev)

        if tg == 3 and j + 3 < NB3 * 8:
            emit_tdc_dma(j + 3)

    def make_tail_quanta(b3):
        qs = []
        for tt in range(4):
            i = 4 * b3 + tt
            j = i % 2
            hold = {}

            def qa(b3=b3, tt=tt, i=i, j=j, hold=hold):
                def xdma(E, i=i, j=j):
                    if i >= 2:
                        E.wait_ge(s_dy, 16 * i)
                    E.dma_start(g(f"xrl_{j}")[:],
                                g("xsp_h").rearrange("(t p) d -> p t d", p=128)[:, i, :]
                                ).then_inc(s_dx[j], 16)

                t_xr[(b3, tt)] = st.emit("sync", xdma, deps=[t_store.get(i - 2)])

                def lnst3(E, tt=tt, j=j, PA=16 + 8 * (b3 % 2)):
                    dd = g("d_tm")[:, tt * D:(tt + 1) * D]
                    E.bn_stats(st6[j][:, 0:6], dd[:, 0:512])
                    E.bn_stats(st6[j][:, 6:12], dd[:, 512:1024])
                    E.drain()
                    E.bn_aggr(mvs[j][:], st6[j][:])
                    E.drain()
                    E.tensor_scalar(out=tny[j][:, 0:1], in0=mvs[j][:, 1:2],
                                    scalar1=s4[:, PA + 4 + tt:PA + 5 + tt],
                                    scalar2=EPS, op0=Op.mult, op1=Op.add)
                    E.drain()
                    return E.reciprocal(tny[j][:, 1:2], tny[j][:, 0:1])

                t_l3 = st.emit("vector", lnst3,
                               deps=[t_fev2[(b3, NCH - 1, tt, 0)],
                                     t_fev2[(b3, NCH - 1, tt, 1)], t_cs4[b3]])
                t_r3 = st.emit("scalar", lambda E, j=j: E.activation(
                    out=rstd[j][:], in_=tny[j][:, 1:2], func=AF.Sqrt,
                    scale=np.float32(1.0)), deps=[t_l3])
                hold["toks"] = (t_l3, t_r3)

            def qb_(b3=b3, tt=tt, i=i, j=j, hold=hold):
                t_l3, t_r3 = hold["toks"]

                def n3f(E, j=j, tt=tt, PA=16 + 8 * (b3 % 2)):
                    E.tensor_tensor(out=tny[j][:, 2:3], in0=rstd[j][:],
                                    in1=s4[:, PA + tt:PA + tt + 1], op=Op.mult)
                    E.drain()
                    return E.tensor_scalar(out=negmr[j][:], in0=mvs[j][:, 0:1],
                                           scalar1=tny[j][:, 2:3], scalar2=-1.0,
                                           op0=Op.mult, op1=Op.mult)

                t_n3 = st.emit("vector", n3f, deps=[t_r3])
                t_y3[(b3, tt)] = st.emit("scalar", lambda E, tt=tt, j=j: E.activation(
                    out=g("ybuf3")[:], in_=g("d_tm")[:, tt * D:(tt + 1) * D],
                    func=AF.Identity, scale=tny[j][:, 2:3], bias=negmr[j][:]),
                    deps=[t_n3, chain["r3b"]])

                def resid3(E, j=j, xrt=16 * ((i // 2) + 1)):
                    E.wait_ge(s_dx[j], xrt)
                    return E.tensor_tensor(out=g(f"xrl_{j}")[:], in0=g(f"xrl_{j}")[:],
                                           in1=g("ybuf3")[:], op=Op.add)

                t_r3b = st.emit("vector", resid3, deps=[t_y3[(b3, tt)], t_xr[(b3, tt)]])
                chain["r3b"] = t_r3b

                def store(E, i=i, j=j):
                    E.dma_start(env["y_h"].rearrange("(t p) d -> p t d", p=128)[:, i, :],
                                g(f"xrl_{j}")[:]).then_inc(s_dy, 16)

                t_store[i] = st.emit("sync", store, deps=[t_r3b])

            qs.append(qa)
            qs.append(qb_)
        return qs

    for j in range(3):
        emit_tdc_dma(j)

    for b3 in range(NB3):
        par_b = b3 % 2

        def qdma(E, b3=b3, par_b=par_b):
            for tt in range(4):
                E.dma_start(g(f"qb_{par_b}")[:, tt * 1024:(tt + 1) * 1024],
                            g("qsp_h")[4 * b3 + tt]).then_inc(s_qb[par_b], 16)

        t_qdma[b3] = st.emit("sync", qdma, deps=[t_mm_gu.get((b3 - 2, IC - 1))])
        nqb[par_b] += 4

        fq = deque((b3 - 1, c, tg, dh) for c in range(NCH - 1)
                   for tg in range(4) for dh in range(2)) if b3 > 0 else deque()
        for ic in range(IC):
            if b3 > 0:
                emit_m1_sg(b3 - 1, ic)
            emit_gu(b3, ic)
            if b3 > 0 and ic > 0:
                emit_rnd2(b3 - 1, ic - 1)
            if ic > 0:
                emit_mxup(b3, ic - 1)
            if pend:
                pend.popleft()()
            if fq and fq[0][1] * CH + CH <= ic:
                emit_fill(*fq.popleft())
        emit_mxup(b3, IC - 1)
        while pend:
            pend.popleft()()
        if b3 > 0:
            emit_rnd2(b3 - 1, IC - 1)
            while fq:
                emit_fill(*fq.popleft())
            for tg in range(2):
                for dh in range(2):
                    emit_fill(b3 - 1, NCH - 1, tg, dh)
            emit_scales(b3)
            for tg in range(2, 4):
                for dh in range(2):
                    emit_fill(b3 - 1, NCH - 1, tg, dh)
            pend.extend(make_tail_quanta(b3 - 1))
        else:
            emit_scales(b3)

    b3 = NB3 - 1
    fq = deque((b3, c, tg, dh) for c in range(NCH - 1)
               for tg in range(4) for dh in range(2))
    for ic in range(IC):
        emit_m1_sg(b3, ic)
        if ic > 0:
            emit_rnd2(b3, ic - 1)
        if pend:
            pend.popleft()()
        if fq and fq[0][1] * CH + CH <= ic:
            emit_fill(*fq.popleft())
    emit_rnd2(b3, IC - 1)
    while fq:
        emit_fill(*fq.popleft())
    while pend:
        pend.popleft()()
    for tg in range(4):
        for dh in range(2):
            emit_fill(b3, NCH - 1, tg, dh)
    for q in make_tail_quanta(b3):
        q()

    st.emit("sync", lambda E: E.wait_ge(s_dy, 16 * T))
    st.build()


_CACHE = {}


def _build_cached(debug=False):
    key = ("nc", debug)
    if key not in _CACHE:
        _CACHE[key] = build_nc(debug=debug)
    return _CACHE[key]


def _host_layout(wq, wk, wv, wo, wg, wu, wd):
    perm = np.concatenate([np.arange(0, 64, 2), np.arange(1, 64, 2)])
    rows = np.concatenate([h * 64 + perm for h in range(H)])
    wall = np.empty((D, D + 2 * HD), np.float32)
    wall[:, 0:D] = wq[rows, :].T
    wall[:, D:D + HD] = wk[:, perm]
    wall[:, D + HD:] = wv
    woT = np.ascontiguousarray(wo.T)
    wgT = np.ascontiguousarray(
        wg.T.reshape(NC, 128, IC, 128).transpose(2, 1, 0, 3))
    wuT = np.ascontiguousarray(
        wu.T.reshape(NC, 128, IC, 128).transpose(2, 1, 0, 3))
    wdT = np.ascontiguousarray(
        wd.T.reshape(NCH, CH, 128, 2, 512).transpose(3, 0, 2, 1, 4)).reshape(I, D)
    return wall, woT, wgT, wuT, wdT


def make_in_maps(x, wq, wk, wv, wo, wg, wu, wd):
    wall, woT, wgT, wuT, wdT = _host_layout(wq, wk, wv, wo, wg, wu, wd)
    in_maps = []
    for c in range(x.shape[0]):
        in_maps.append({
            "x": np.ascontiguousarray(x[c]),
            "xT": np.ascontiguousarray(x[c].T),
            "wall": wall, "woT": woT, "wgT": wgT, "wuT": wuT, "wdT": wdT,
        })
    return in_maps


def kernel(L0_attn_x, L0_attn_wq, L0_attn_wk, L0_attn_wv, L0_attn_wo,
           L0_ffn_w_gate, L0_ffn_w_up, L0_ffn_w_down):
    x = np.ascontiguousarray(np.asarray(L0_attn_x, np.float32))
    in_maps = make_in_maps(
        x, np.asarray(L0_attn_wq, np.float32), np.asarray(L0_attn_wk, np.float32),
        np.asarray(L0_attn_wv, np.float32), np.asarray(L0_attn_wo, np.float32),
        np.asarray(L0_ffn_w_gate, np.float32), np.asarray(L0_ffn_w_up, np.float32),
        np.asarray(L0_ffn_w_down, np.float32))
    nc, _ = _build_cached(debug=False)
    res = run_bass_kernel_spmd(nc, in_maps, core_ids=list(range(8)))
    return np.stack([res.results[c]["y"] for c in range(8)], axis=0)


# version 36
# speedup vs baseline: 1.0448x; 1.0014x over previous
import math
from collections import deque
from contextlib import ExitStack

import numpy as np
import concourse.bass as bass
import concourse.mybir as mybir
from concourse.bass_utils import run_bass_kernel_spmd
from concourse.alu_op_type import AluOpType as Op

F32 = mybir.dt.float32
BF16 = mybir.dt.bfloat16
I16 = mybir.dt.int16
I32 = mybir.dt.int32
FP8 = mybir.dt.float8e4
AF = mybir.ActivationFunctionType
AX = mybir.AxisListType

S = 2048
D = 1024
H = 16
HD = 64
I = 4096
T = S // 128
NC = D // 128
IC = I // 128
QB = 4
NB = T // QB
FB = 256
NFB = S // FB
TB = 512
NB3 = S // TB
CH = 8
NCH = 4
EPS = 1e-5
LN1000 = math.log(1000.0)
TWO_PI = 2.0 * math.pi
ENGINES = ("sync", "tensor", "vector", "scalar", "gpsimd")


class Sched:

    def __init__(self, nc, stack, name):
        self.nc = nc
        self.ops = {e: [] for e in ENGINES}
        self.cnt = {e: 0 for e in ENGINES}
        self.sems = {e: stack.enter_context(nc.semaphore(f"{name}_{e[:2]}"))
                     for e in ENGINES}

    def emit(self, eng, fn, deps=()):
        self.ops[eng].append((fn, tuple(d for d in deps if d is not None)))
        self.cnt[eng] += 1
        return (self, eng, self.cnt[eng])

    def build(self):
        with self.nc.Block() as blk:
            for e in ENGINES:
                lst = self.ops[e]
                if not lst:
                    continue

                def lane(E, e=e, lst=lst):
                    for fn, deps in lst:
                        for d in deps:
                            if d[0] == "SEM":
                                E.wait_ge(d[1], d[2])
                            elif d[0] is not self:
                                pass
                            elif d[1] == e and e != "gpsimd":
                                E.wait_ge(self.sems[e], d[2])
                            else:
                                E.wait_ge(self.sems[d[1]], d[2])
                        last = fn(E)
                        if e in ("vector", "scalar"):
                            if last is not None:
                                last.then_inc(self.sems[e], 1)
                            else:
                                E.drain().then_inc(self.sems[e], 1)
                        elif e in ("gpsimd", "tensor"):
                            assert last is not None, f"{e} group must return last inst"
                            last.then_inc(self.sems[e], 1)
                        else:
                            E.nop().then_inc(self.sems[e], 1)

                getattr(blk, e)(lane)


def build_nc(debug=False):
    nc = bass.Bass("TRN2", target_bir_lowering=False)
    env = {"debug": debug, "dbg": {}}
    env["x_h"] = nc.dram_tensor("x", [S, D], F32, kind="ExternalInput").ap()
    env["xT_h"] = nc.dram_tensor("xT", [D, S], F32, kind="ExternalInput").ap()
    env["wall_h"] = nc.dram_tensor("wall", [D, D + 2 * HD], F32, kind="ExternalInput").ap()
    env["woT_h"] = nc.dram_tensor("woT", [D, D], F32, kind="ExternalInput").ap()
    env["wsrc_h"] = {
        "g": nc.dram_tensor("wgT", [IC, 128, NC, 128], F32, kind="ExternalInput").ap(),
        "u": nc.dram_tensor("wuT", [IC, 128, NC, 128], F32, kind="ExternalInput").ap(),
        "d": nc.dram_tensor("wdT", [I, D], F32, kind="ExternalInput").ap(),
    }
    env["y_h"] = nc.dram_tensor("y", [S, D], F32, kind="ExternalOutput").ap()
    env["xsp_h"] = nc.dram_tensor("x_spill", [S, D], F32, kind="Internal").ap()
    env["qsp_h"] = nc.dram_tensor("q_spill", [T, 128, NC * 128], BF16, kind="Internal").ap()
    env["tern_h"] = {
        "g": nc.dram_tensor("tern_g", [IC * D * 128], FP8, kind="Internal").ap(),
        "u": nc.dram_tensor("tern_u", [IC * D * 128], FP8, kind="Internal").ap(),
        "d": nc.dram_tensor("tern_d", [I * D], FP8, kind="Internal").ap(),
    }
    if debug:
        for name, shape in [
            ("dbg_qfm", [128, NC * S]), ("dbg_kdup", [128, S]),
            ("dbg_v1", [128, T * 66]), ("dbg_x1", [S, D]),
            ("dbg_negb", [128, H]), ("dbg_qint", [128, NC * S]),
            ("dbg_pt", [128, T * 512]), ("dbg_otm", [128, QB * D]),
            ("dbg_tern", [IC * D * 128]),
        ]:
            env["dbg"][name] = nc.dram_tensor(name, shape, F32, kind="ExternalOutput").ap()

    with ExitStack() as top:
        def alloc(n, shape, dt, stk=top):
            t = stk.enter_context(nc.sbuf_tensor(n, shape, dt))
            env[n] = t
            return t

        alloc("identb", [128, 128], BF16)
        alloc("identf", [128, 128], F32)
        alloc("ones_col_b", [128, 1], BF16)
        alloc("ones_row_f", [1, 128], F32)
        alloc("onerow16", [1, 16], F32)
        alloc("delta", [128, 1], F32)
        alloc("sin_t", [128, T * 32], F32)
        alloc("cos_t", [128, T * 32], F32)
        alloc("negBtab", [128, H], F32)
        alloc("mxq", [128, T], F32)
        for w in "gud":
            alloc(f"sw_{w}", [128, 1], F32)
        alloc("beta_gu2", [128, 1], F32)
        alloc("beta_d", [128, 1], F32)
        alloc("wsum", [128, 48], F32)
        alloc("t1r", [1, 32], F32)
        alloc("t16r", [16, 4], F32)
        alloc("qn2acc", [128, H], F32)
        alloc("kn2acc", [128, 1], F32)
        for j in range(2):
            alloc(f"st6_{j}", [128, 12], F32)
            alloc(f"mv_{j}", [128, 2], F32)
            alloc(f"rstd_{j}", [128, 1], F32)
            alloc(f"negmr_{j}", [128, 1], F32)
            alloc(f"tny_{j}", [128, 16], F32)

        sX = ExitStack()
        alloc("x_tm", [128, T * D], F32, sX)

        with ExitStack() as sB:
            alloc("q_fm", [128, NC * S], BF16, sB)
            for j in range(2):
                alloc(f"wch_{j}", [128, 2048], F32, sB)
            alloc("kdup_fm", [128, S], BF16, sB)
            alloc("v1_tm", [128, T * 66], BF16, sB)

            with ExitStack() as sA:
                alloc("x_fm", [128, NC * S], BF16, sA)
                alloc("wall_sb", [128, NC * 1152], BF16, sA)
                build_p0(nc, sA, env)
                build_p1(nc, sA, env)

            with ExitStack() as sC:
                alloc("woT_sb", [128, NC * D], BF16, sC)
                alloc("pt_sb", [128, T * 512], BF16, sC)
                for j in range(2):
                    alloc(f"o_tm{j}", [128, QB * D], BF16, sC)
                alloc("o_fmt", [128, NC * 128], BF16, sC)
                alloc("tch_0", [128, 2048], FP8, sC)
                for j in range(2):
                    alloc(f"osum_{j}", [128, 4], F32, sC)
                alloc("zch2", [128, 2048], F32, sC)
                alloc("pch", [128, 2048], FP8, sC)
                alloc("ybuf", [128, D], F32, sC)
                alloc("oraw", [128, QB * 65], F32, sC)
                alloc("qst_0", [128, NC * 128], BF16, sC)
                for j in range(2):
                    alloc(f"ln2_{j}", [128, D], F32, sC)
                    alloc(f"sgq_{j}", [128, D], FP8, sC)
                alloc("i16_0", [128, D], I16, sC)
                alloc("qtm_0", [128, D], BF16, sC)
                build_p2(nc, sC, env)

        sX.close()

        with ExitStack() as sD:
            for j in range(2):
                alloc(f"qb_{j}", [128, 4 * 1024], BF16, sD)
                alloc(f"tg_{j}", [128, NC * 128], FP8, sD)
                alloc(f"tg_{2 + j}", [128, NC * 128], FP8, sD)
                alloc(f"tu_{j}", [128, NC * 128], FP8, sD)
                alloc(f"tu_{2 + j}", [128, NC * 128], FP8, sD)
                alloc(f"rb_{j}", [128, TB], F32, sD)
                alloc(f"ub_{j}", [128, TB], F32, sD)
                alloc(f"xrl_{j}", [128, D], F32, sD)
            for j in range(3):
                alloc(f"tdc_{j}", [128, CH * 512], FP8, sD)
            alloc("h_sb", [128, IC * TB], F32, sD)
            alloc("hq_sb", [128, IC * TB], BF16, sD)
            alloc("mxacc", [128, TB], F32, sD)
            for j in range(2):
                alloc(f"habs_{j}", [128, TB], F32, sD)
            alloc("csrow", [1, TB], F32, sD)
            alloc("csb", [128, TB], F32, sD)
            for j in range(2):
                alloc(f"m1b_{j}", [128, TB], F32, sD)
                alloc(f"sgb_{j}", [128, TB], FP8, sD)
                alloc(f"i16f_{j}", [128, TB], I16, sD)
            alloc("s4", [128, 32], F32, sD)
            alloc("d_tm", [128, 4 * D], F32, sD)
            alloc("ybuf3", [128, D], F32, sD)
            build_p3b(nc, sD, env)

    return nc, env["dbg"]


def build_p0(nc, stack, env):
    st = Sched(nc, stack, "p0")
    Ez = stack.enter_context
    s_dg = Ez(nc.semaphore("p0dg"))
    s_ds = Ez(nc.semaphore("p0ds"))
    ps_a = Ez(nc.psum_tensor("p0psa", [128, 512], F32))
    trows = Ez(nc.sbuf_tensor("trows", [1, T * 128], F32))
    jrow = Ez(nc.sbuf_tensor("jrow", [1, 32], F32))
    freqs = Ez(nc.sbuf_tensor("freqs", [1, 32], F32))
    rr = Ez(nc.sbuf_tensor("rr", [128, 32], F32))
    rk = Ez(nc.sbuf_tensor("rk", [128, 32], I32))
    rf = Ez(nc.sbuf_tensor("rf", [128, 32], F32))
    rf2 = Ez(nc.sbuf_tensor("rf2", [128, 32], F32))
    rs1 = Ez(nc.sbuf_tensor("rs1", [128, 32], F32))
    rs2 = Ez(nc.sbuf_tensor("rs2", [128, 32], F32))
    d15 = Ez(nc.sbuf_tensor("d15", [128, 1], F32))
    d15i = Ez(nc.sbuf_tensor("d15i", [128, 1], I16))
    g = env.__getitem__

    st.emit("sync", lambda E: E.dma_start(
        g("x_tm")[:], g("x_h").rearrange("(t p) d -> p t d", p=128)).then_inc(s_ds, 16))

    def gdmas(E):
        E.dma_start(g("x_fm")[:],
                    g("xT_h").rearrange("(c p) t -> p c t", p=128)).then_inc(s_dg, 16)
        E.dma_start(g("wall_sb")[:],
                    g("wall_h").rearrange("(c p) n -> p c n", p=128)).then_inc(s_dg, 16)
        return E.nop()

    st.emit("gpsimd", gdmas)

    def memsets(E):
        for n in ("identb", "identf", "ones_col_b", "ones_row_f", "onerow16"):
            E.memset(g(n)[:], 1.0)
        for n in ("qn2acc", "kn2acc", "wsum"):
            E.memset(g(n)[:], 0.0)
        E.memset(g("v1_tm")[:], 1.0)
        return E.memset(d15[:], 1.5)

    t_ms = st.emit("vector", memsets)

    def mkdelta(E):
        E.tensor_scalar(out=d15i[:], in0=d15[:], scalar1=1.0, scalar2=None, op0=Op.mult)
        E.drain()
        return E.tensor_scalar(out=g("delta")[:], in0=d15i[:], scalar1=-0.5,
                               scalar2=1.0, op0=Op.mult, op1=Op.add)

    st.emit("vector", mkdelta, deps=[t_ms])

    t_idb = st.emit("gpsimd", lambda E: E.affine_select(
        g("identb")[:], g("identb")[:], [[1, 128]], Op.is_equal, 0.0,
        base=0, channel_multiplier=-1), deps=[t_ms])
    t_idf = st.emit("gpsimd", lambda E: E.affine_select(
        g("identf")[:], g("identf")[:], [[1, 128]], Op.is_equal, 0.0,
        base=0, channel_multiplier=-1), deps=[t_ms])
    env["tok_ident"] = t_idf
    env["tok_identb"] = t_idb

    t_io1 = st.emit("gpsimd", lambda E: E.iota(
        trows[:], [[128, T], [1, 128]], base=0, channel_multiplier=0,
        allow_small_or_imprecise_dtypes=True))
    t_io2 = st.emit("gpsimd", lambda E: E.iota(
        jrow[:], [[1, 32]], base=0, channel_multiplier=0,
        allow_small_or_imprecise_dtypes=True))
    t_fr = st.emit("scalar", lambda E: E.activation(
        out=freqs[:], in_=jrow[:], func=AF.Exp, scale=np.float32(-LN1000 / 32.0)), deps=[t_io2])

    t_prev = None
    for i in range(T):
        t_an = st.emit("tensor", lambda E, i=i: E.matmul(
            ps_a[:, 0:32], trows[:, i * 128:(i + 1) * 128], freqs[:],
            start=True, stop=True), deps=[t_fr, t_io1, t_prev])

        def frac(E):
            E.tensor_scalar(out=rr[:], in0=ps_a[:, 0:32], scalar1=1.0 / TWO_PI,
                            scalar2=None, op0=Op.mult)
            E.drain()
            E.tensor_scalar(out=rk[:], in0=rr[:], scalar1=1.0, scalar2=None, op0=Op.mult)
            E.drain()
            E.tensor_tensor(out=rf[:], in0=rr[:], in1=rk[:], op=Op.subtract)
            E.drain()
            E.scalar_tensor_tensor(out=rf[:], in0=rf[:], scalar=0.0, in1=rf[:],
                                   op0=Op.is_lt, op1=Op.add)
            E.drain()
            E.tensor_scalar(out=rf2[:], in0=rf[:], scalar1=0.25, scalar2=None, op0=Op.add)
            E.drain()
            return E.scalar_tensor_tensor(out=rf2[:], in0=rf2[:], scalar=1.0,
                                          in1=rf2[:], op0=Op.is_lt, op1=Op.add)

        t_vf = st.emit("vector", frac, deps=[t_an])

        def sins(E):
            E.activation(out=rs1[:], in_=rf[:], func=AF.Sin, scale=np.float32(TWO_PI), bias=np.float32(-math.pi))
            E.drain()
            return E.activation(out=rs2[:], in_=rf2[:], func=AF.Sin,
                                scale=np.float32(TWO_PI),
                                bias=np.float32(-3.0 * math.pi))

        t_sn = st.emit("scalar", sins, deps=[t_vf])

        def neg(E, i=i):
            E.tensor_scalar(out=g("sin_t")[:, i * 32:(i + 1) * 32], in0=rs1[:],
                            scalar1=-1.0, scalar2=None, op0=Op.mult)
            return E.tensor_scalar(out=g("cos_t")[:, i * 32:(i + 1) * 32],
                                   in0=rs2[:], scalar1=-1.0, scalar2=None,
                                   op0=Op.mult)

        t_prev = st.emit("vector", neg, deps=[t_sn])

    def waitall(E):
        E.wait_ge(s_ds, 16)
        E.wait_ge(s_dg, 32)

    st.emit("sync", waitall, deps=[t_prev, env["tok_ident"]])
    st.build()


def build_p1(nc, stack, env):
    st = Sched(nc, stack, "p1")
    Ez = stack.enter_context
    ps_qkv = Ez(nc.psum_tensor("p1qkv", [128, 1536], F32))
    ps_tr = Ez(nc.psum_tensor("p1tr", [128, 1152], BF16))
    ps_nm = Ez(nc.psum_tensor("p1nm", [128, 512], F32))
    call = Ez(nc.sbuf_tensor("call", [1, 1152], BF16))
    negmrow_all = Ez(nc.sbuf_tensor("negmrow_all", [1, T * 128], BF16))
    rstd_all = Ez(nc.sbuf_tensor("rstd_all", [128, T], F32))
    qtq = [Ez(nc.sbuf_tensor(f"qtq{j}", [128, 1024], BF16)) for j in range(2)]
    qtkv = [Ez(nc.sbuf_tensor(f"qtkv{j}", [128, 128], F32)) for j in range(2)]
    qro = [Ez(nc.sbuf_tensor(f"qro{j}", [128, 1024], BF16)) for j in range(2)]
    kro = [Ez(nc.sbuf_tensor(f"kro{j}", [128, 128], BF16)) for j in range(2)]
    sc1 = Ez(nc.sbuf_tensor("sc1", [128, 1024], F32))
    g = env.__getitem__
    x_fm, wall_sb = g("x_fm"), g("wall_sb")
    st6 = [g("st6_0"), g("st6_1")]
    mvs = [g("mv_0"), g("mv_1")]
    tny = [g("tny_0"), g("tny_1")]

    def colsum(E):
        for c in range(NC):
            wc = wall_sb[:, c * 1152:(c + 1) * 1152]
            for lo, hi in ((0, 512), (512, 1024), (1024, 1152)):
                last = E.matmul(ps_qkv[:1, lo:hi], g("ones_col_b")[:], wc[:, lo:hi],
                                start=(c == 0), stop=(c == NC - 1))
        return last

    t_cs = st.emit("tensor", colsum)
    t_cse = st.emit("vector", lambda E: E.tensor_copy(call[:], ps_qkv[:1, 0:1152]),
                    deps=[t_cs])

    s_dwA = [Ez(nc.semaphore(f"p1wA{j}")) for j in range(2)]
    ps_fa = Ez(nc.psum_tensor("p1fa", [128, 512], F32))
    wsrc = env["wsrc_h"]
    passA = [(w, c) for w in "gud" for c in range(16)]
    tokA = {}
    t_abs = {}
    t_finred = {}
    t_findone = {}

    def wflat(w):
        a = wsrc[w]
        if len(a.shape) == 4:
            return a.rearrange("a b c d -> (a b c d)")
        return a.rearrange("a b -> (a b)")

    def emit_dmaA(k):
        if k < 0 or k >= len(passA):
            return
        w, c = passA[k]
        j = k % 2
        view = wflat(w).rearrange("(c p n) -> p c n", p=128, n=2048)[:, c, :]
        st.emit("sync", lambda E: E.dma_start(
            g(f"wch_{j}")[:], view).then_inc(s_dwA[j], 16),
            deps=[tokA.get(k - 2)])

    def emit_passA(k):
        j = k % 2
        col = k

        def red(E, j=j, col=col, k=k):
            E.wait_ge(s_dwA[j], 16 * (k // 2 + 1))
            return E.tensor_reduce(out=g("wsum")[:, col:col + 1],
                                   in_=g(f"wch_{j}")[:], axis=AX.X, op=Op.add,
                                   apply_absolute_value=True)

        tokA[k] = st.emit("vector", red)
        emit_dmaA(k + 2)

    def emit_finishA(w):
        wi = "gud".index(w)
        f1 = st.emit("vector", lambda E: E.tensor_reduce(
            out=g("tny_0")[:, 14:15], in_=g("wsum")[:, 16 * wi:16 * wi + 16],
            axis=AX.X, op=Op.add), deps=[tokA[16 * wi + c] for c in range(16)])
        f2 = st.emit("tensor", lambda E: E.transpose(
            ps_fa[0:1, 0:128], g("tny_0")[:, 14:15], g("identf")[:]),
            deps=[f1, env["tok_ident"]])

        def finv(E):
            E.tensor_reduce(out=g("t1r")[:, 0:1], in_=ps_fa[0:1, 0:128], axis=AX.X,
                            op=Op.add)
            E.drain()
            E.tensor_scalar(out=g("t1r")[:, 1:2], in0=g("t1r")[:, 0:1],
                            scalar1=1.0 / (I * D), scalar2=EPS, op0=Op.mult,
                            op1=Op.max)
            E.drain()
            return E.reciprocal(g("t1r")[:, 2:3], g("t1r")[:, 1:2])

        t_finred[w] = f3 = st.emit("vector", finv, deps=[f2])
        f4 = st.emit("tensor", lambda E: (
            E.matmul(ps_fa[:, 300:301], g("ones_row_f")[:], g("t1r")[:, 2:3],
                     start=True, stop=True),
            E.matmul(ps_fa[:, 301:302], g("ones_row_f")[:], g("t1r")[:, 1:2],
                     start=True, stop=True))[-1], deps=[f3])

        def fin5(E, w=w, wi=wi):
            E.tensor_copy(g(f"sw_{w}")[:], ps_fa[:, 300:301])
            E.drain()
            if w == "d":
                return E.tensor_copy(g("beta_d")[:], ps_fa[:, 301:302])
            return E.tensor_copy(g("tny_1")[:, 5 + wi:6 + wi], ps_fa[:, 301:302])

        t_findone[w] = st.emit("vector", fin5, deps=[f4])

    def emit_comb():
        def comb(E):
            E.tensor_tensor(out=g("beta_gu2")[:], in0=g("tny_1")[:, 5:6],
                            in1=g("tny_1")[:, 5:6], op=Op.mult)
            E.drain()
            return E.tensor_tensor(out=g("beta_gu2")[:], in0=g("beta_gu2")[:],
                                   in1=g("tny_1")[:, 6:7], op=Op.mult)

        st.emit("vector", comb, deps=[t_findone["g"], t_findone["u"]])

    emit_dmaA(0)
    emit_dmaA(1)
    pA_items = ([("A", k) for k in range(16)] + [("F", "g")]
                + [("A", k) for k in range(16, 32)] + [("F", "u")]
                + [("A", k) for k in range(32, 48)] + [("F", "d"), ("C", None)])
    pA_next = [0]

    def pA_quantum(n):
        for _ in range(n):
            if pA_next[0] < len(pA_items):
                kind, a = pA_items[pA_next[0]]
                pA_next[0] += 1
                if kind == "A":
                    emit_passA(a)
                elif kind == "F":
                    emit_finishA(a)
                else:
                    emit_comb()

    t_stats = {}
    t_ne = None
    t_nm = {}
    t_sq = {}
    for i in range(T):
        j = i % 2

        def stats(E, i=i, j=j):
            xx = g("x_tm")[:, i * D:(i + 1) * D]
            E.bn_stats(st6[j][:, 0:6], xx[:, 0:512])
            E.bn_stats(st6[j][:, 6:12], xx[:, 512:1024])
            E.drain()
            E.bn_aggr(mvs[j][:], st6[j][:])
            E.drain()
            E.tensor_scalar(out=tny[j][:, 0:1], in0=mvs[j][:, 1:2], scalar1=EPS,
                            scalar2=None, op0=Op.add)
            E.drain()
            E.reciprocal(tny[j][:, 1:2], tny[j][:, 0:1])
            E.drain()
            return E.tensor_scalar(out=tny[j][:, 2:3], in0=mvs[j][:, 0:1],
                                   scalar1=-1.0, scalar2=None, op0=Op.mult)

        t_stats[i] = st.emit("vector", stats, deps=[t_nm.get(i - 2),
                                                    t_sq.get(i - 2)])
        t_sq[i] = st.emit("scalar", lambda E, i=i, j=j: E.activation(
            out=rstd_all[:, i:i + 1], in_=tny[j][:, 1:2], func=AF.Sqrt,
            scale=np.float32(1.0)), deps=[t_stats[i]])
        t_nm[i] = st.emit("tensor", lambda E, j=j: E.transpose(
            ps_nm[:1, 0:128], tny[j][:, 2:3], g("identf")[:]),
            deps=[t_stats[i], t_ne, env["tok_ident"]])
        t_ne = st.emit("vector", lambda E, i=i: E.tensor_copy(
            negmrow_all[:, i * 128:(i + 1) * 128], ps_nm[:1, 0:128]), deps=[t_nm[i]])

    t_scal_last = (st, "scalar", st.cnt["scalar"])

    t_ev = {}
    t_rope = {}
    t_tre = None
    for i in range(T):
        j = i % 2

        def qkv(E, i=i):
            for c in range(NC):
                stat = x_fm[:, c * S + i * 128: c * S + (i + 1) * 128]
                wc = wall_sb[:, c * 1152:(c + 1) * 1152]
                for lo, hi in ((0, 512), (512, 1024), (1024, 1152)):
                    E.matmul(ps_qkv[:, lo:hi], stat, wc[:, lo:hi],
                             start=(c == 0), stop=False)
            nrow = negmrow_all[:, i * 128:(i + 1) * 128]
            for lo, hi in ((0, 512), (512, 1024), (1024, 1152)):
                last = E.matmul(ps_qkv[:, lo:hi], nrow, call[:, lo:hi],
                                start=False, stop=True)
            return last

        dq = [t_cse, t_ne, t_ev.get(i - 1)]
        t_mm = st.emit("tensor", qkv, deps=dq)

        def evac(E, i=i, j=j):
            for lo, hi in ((0, 512), (512, 1024)):
                E.activation(out=qtq[j][:, lo:hi], in_=ps_qkv[:, lo:hi],
                             func=AF.Identity, scale=rstd_all[:, i:i + 1])
            E.activation(out=qtkv[j][:], in_=ps_qkv[:, 1024:1152],
                         func=AF.Identity, scale=rstd_all[:, i:i + 1])

        t_ev[i] = st.emit("scalar", evac,
                          deps=[t_mm, t_scal_last if i == 0 else None,
                                t_rope.get(i - 2)])

        def rope(E, i=i, j=j):
            kv = qtkv[j]
            sin_b = g("sin_t")[:, i * 32:(i + 1) * 32].rearrange(
                "p (o k) -> p o k", o=1).broadcast_to((128, H, 32))
            cos_b = g("cos_t")[:, i * 32:(i + 1) * 32].rearrange(
                "p (o k) -> p o k", o=1).broadcast_to((128, H, 32))
            qq = qtq[j][:, 0:1024].rearrange("p (h k) -> p h k", k=64)
            qa, qb = qq[:, :, 0:32], qq[:, :, 32:64]
            ro = qro[j][:, 0:1024].rearrange("p (h k) -> p h k", k=64)
            ra, rb2 = ro[:, :, 0:32], ro[:, :, 32:64]
            w1 = sc1[:, 0:512].rearrange("p (h k) -> p h k", k=32)
            w2 = sc1[:, 512:1024].rearrange("p (h k) -> p h k", k=32)
            E.tensor_tensor(out=w1, in0=qa, in1=cos_b, op=Op.mult)
            E.tensor_tensor(out=w2, in0=qb, in1=sin_b, op=Op.mult)
            E.drain()
            E.tensor_tensor(out=ra, in0=w1, in1=w2, op=Op.subtract)
            E.drain()
            E.tensor_tensor(out=w1, in0=qb, in1=cos_b, op=Op.mult)
            E.tensor_tensor(out=w2, in0=qa, in1=sin_b, op=Op.mult)
            E.drain()
            E.tensor_tensor(out=rb2, in0=w1, in1=w2, op=Op.add)
            E.drain()
            ka, kb = kv[:, 0:32], kv[:, 32:64]
            s1 = g("sin_t")[:, i * 32:(i + 1) * 32]
            c1 = g("cos_t")[:, i * 32:(i + 1) * 32]
            E.tensor_tensor(out=sc1[:, 0:32], in0=ka, in1=c1, op=Op.mult)
            E.tensor_tensor(out=sc1[:, 32:64], in0=kb, in1=s1, op=Op.mult)
            E.tensor_tensor(out=sc1[:, 64:96], in0=kb, in1=c1, op=Op.mult)
            E.tensor_tensor(out=sc1[:, 96:128], in0=ka, in1=s1, op=Op.mult)
            E.drain()
            E.tensor_tensor(out=kro[j][:, 0:32], in0=sc1[:, 0:32], in1=sc1[:, 32:64],
                            op=Op.subtract)
            E.tensor_tensor(out=kro[j][:, 32:64], in0=sc1[:, 64:96],
                            in1=sc1[:, 96:128], op=Op.add)
            E.drain()
            E.tensor_copy(kro[j][:, 64:128], kro[j][:, 0:64])
            E.tensor_copy(g("v1_tm")[:, i * 66:i * 66 + 64], kv[:, 64:128])
            E.drain()
            E.tensor_tensor(out=sc1[:], in0=qro[j][:], in1=qro[j][:], op=Op.mult)
            E.tensor_tensor(out=kv[:, 0:64], in0=kro[j][:, 0:64],
                            in1=kro[j][:, 0:64], op=Op.mult)
            E.drain()
            E.tensor_reduce(out=kv[:, 64:65], in_=kv[:, 0:64], axis=AX.X, op=Op.add)
            E.tensor_reduce(out=kv[:, 80:88],
                            in_=sc1[:, 0:512].rearrange("p (h k) -> p h k", k=64),
                            axis=AX.X, op=Op.add)
            E.tensor_reduce(out=kv[:, 88:96],
                            in_=sc1[:, 512:1024].rearrange("p (h k) -> p h k", k=64),
                            axis=AX.X, op=Op.add)
            E.drain()
            E.tensor_tensor(out=g("qn2acc")[:], in0=g("qn2acc")[:],
                            in1=kv[:, 80:96], op=Op.max)
            return E.tensor_tensor(out=g("kn2acc")[:], in0=g("kn2acc")[:],
                                   in1=kv[:, 64:65], op=Op.max)

        t_rope[i] = st.emit("vector", rope, deps=[t_ev[i]])

        def tposes(E, j=j):
            for c in range(NC):
                E.transpose(ps_tr[:, c * 128:(c + 1) * 128],
                            qro[j][:, c * 128:(c + 1) * 128], g("identb")[:])
            return E.transpose(ps_tr[:, 1024:1152], kro[j][:], g("identb")[:])

        t_tp = st.emit("tensor", tposes, deps=[t_rope[i], env["tok_identb"], t_tre])

        def trevac(E, i=i):
            for c in range(NC):
                E.activation(out=g("q_fm")[:, c * S + i * 128: c * S + (i + 1) * 128],
                             in_=ps_tr[:, c * 128:(c + 1) * 128], func=AF.Identity)
            return E.activation(out=g("kdup_fm")[:, i * 128:(i + 1) * 128],
                                in_=ps_tr[:, 1024:1152], func=AF.Identity)

        t_tre = st.emit("scalar", trevac, deps=[t_tp])
        pA_quantum(4)

    pA_quantum(len(pA_items))

    if env["debug"]:
        dbg = env["dbg"]
        s_dbg = Ez(nc.semaphore("p1dbg"))

        def dbgdma(E):
            E.dma_start(dbg["dbg_qfm"], g("q_fm")[:]).then_inc(s_dbg, 16)
            E.dma_start(dbg["dbg_kdup"], g("kdup_fm")[:]).then_inc(s_dbg, 16)
            E.dma_start(dbg["dbg_v1"], g("v1_tm")[:]).then_inc(s_dbg, 16)
            E.wait_ge(s_dbg, 48)
            return E.nop()

        st.emit("gpsimd", dbgdma, deps=[t_tre, t_rope[T - 1], t_rope[T - 2]])
    st.build()


def build_p2(nc, stack, env):
    st = Sched(nc, stack, "p2")
    Ez = stack.enter_context
    s_dwo = Ez(nc.semaphore("p2dwo"))
    s_dq = [Ez(nc.semaphore(f"p2q{j}")) for j in range(2)]
    s_dxs = Ez(nc.semaphore("p2xs"))
    s_dwA = [Ez(nc.semaphore(f"p2dA{j}")) for j in range(2)]
    s_dwB = [Ez(nc.semaphore(f"p2dB{j}")) for j in range(2)]
    ps_st = [Ez(nc.psum_tensor(f"p2st{j}", [128, 512], F32)) for j in range(2)]
    ps_o = [Ez(nc.psum_tensor(f"p2o{j}", [128, 512], F32)) for j in range(2)]
    ps_wo = Ez(nc.psum_tensor("p2wo", [128, 1024], F32))
    ps_tr = Ez(nc.psum_tensor("p2tr", [128, 1024], BF16))
    ps_bc = Ez(nc.psum_tensor("p2bc", [128, 512], F32))
    g = env.__getitem__
    st6 = [g("st6_0"), g("st6_1")]
    mvs = [g("mv_0"), g("mv_1")]
    rstd = [g("rstd_0"), g("rstd_1")]
    negmr = [g("negmr_0"), g("negmr_1")]
    tny = [g("tny_0"), g("tny_1")]
    t1r, t16r = g("t1r"), g("t16r")
    osum = [g("osum_0"), g("osum_1")]

    def _woT(E):
        E.dma_start(g("woT_sb")[:],
                    g("woT_h").rearrange("(c p) n -> p c n", p=128)).then_inc(s_dwo, 16)
        return E.nop()

    st.emit("gpsimd", _woT)
    t_ptms = st.emit("vector", lambda E: E.memset(g("pt_sb")[:], 0.0))

    tk = st.emit("tensor", lambda E: (
        E.transpose(ps_bc[0:16, 0:128], g("qn2acc")[:], g("identf")[:]),
        E.transpose(ps_bc[0:1, 128:256], g("kn2acc")[:], g("identf")[:]))[-1],
        deps=[env["tok_ident"]])

    def nb_v2(E):
        E.tensor_reduce(out=t16r[:, 0:1], in_=ps_bc[0:16, 0:128], axis=AX.X, op=Op.max)
        E.drain()
        return E.tensor_reduce(out=t1r[:, 0:1], in_=ps_bc[0:1, 128:256], axis=AX.X,
                               op=Op.max)

    tk = st.emit("vector", nb_v2, deps=[tk])
    tk2 = st.emit("tensor", lambda E: E.matmul(
        ps_bc[0:16, 256:257], g("onerow16")[:], t1r[:, 0:1], start=True, stop=True),
        deps=[tk])
    tk3 = st.emit("vector", lambda E: E.tensor_tensor(
        out=t16r[:, 1:2], in0=t16r[:, 0:1], in1=ps_bc[0:16, 256:257], op=Op.mult),
        deps=[tk2])
    tk4 = st.emit("scalar", lambda E: E.activation(
        out=t16r[:, 2:3], in_=t16r[:, 1:2], func=AF.Sqrt, scale=np.float32(1.0 / 64.0), bias=np.float32(1e-30)),
        deps=[tk3])
    tk5 = st.emit("vector", lambda E: E.tensor_scalar(
        out=t16r[:, 3:4], in0=t16r[:, 2:3], scalar1=-1.0, scalar2=None, op0=Op.mult),
        deps=[tk4])
    tk6 = st.emit("tensor", lambda E: E.transpose(
        ps_bc[0:1, 256:272], t16r[:, 3:4], g("identf")[0:16, 0:16]), deps=[tk5])
    tk7 = st.emit("vector", lambda E: E.tensor_copy(t1r[:, 8:24], ps_bc[0:1, 256:272]),
                  deps=[tk6])
    tk8 = st.emit("tensor", lambda E: E.matmul(
        ps_bc[:, 256:272], g("ones_row_f")[:], t1r[:, 8:24], start=True, stop=True),
        deps=[tk7])
    t_negb = st.emit("vector", lambda E: E.tensor_copy(g("negBtab")[:], ps_bc[:, 256:272]),
                     deps=[tk8])

    wsrc, tern = env["wsrc_h"], env["tern_h"]
    passA = [(w, c) for w in "gud" for c in range(16)]
    tokB = {}
    tokBv = {}

    def wflat(w):
        a = wsrc[w]
        if len(a.shape) == 4:
            return a.rearrange("a b c d -> (a b c d)")
        return a.rearrange("a b -> (a b)")

    def emit_passB(k):
        w, c = passA[k]
        j = k % 2
        view = wflat(w).rearrange("(c p n) -> p c n", p=128, n=2048)[:, c, :]
        tB = st.emit("sync", lambda E: E.dma_start(
            g(f"wch_{j}")[:], view).then_inc(s_dwA[j], 16),
            deps=[tokBv.get(k - 2)])

        def tzg(E, w=w, j=j, k=k):
            E.wait_ge(s_dwA[j], 16 * (k // 2 + 1))
            return E.tensor_scalar(out=g("zch2")[:], in0=g(f"wch_{j}")[:],
                                   scalar1=g(f"sw_{w}")[:], scalar2=1.0,
                                   op0=Op.mult, op1=Op.mult)

        tzg_tok = st.emit("gpsimd", tzg, deps=[tokBv.get(k - 2)])

        def tz(E, j=j, k=k):
            E.wait_ge(s_dwB[0], 16 * k)
            E.tensor_scalar(out=g("pch")[:], in0=g("zch2")[:], scalar1=0.5,
                            scalar2=None, op0=Op.is_ge)
            E.drain()
            return E.scalar_tensor_tensor(out=g("tch_0")[:], in0=g("zch2")[:],
                                          scalar=-0.5, in1=g("pch")[:], op0=Op.is_le,
                                          op1=Op.subtract)

        tokBv[k] = st.emit("vector", tz, deps=[tzg_tok])
        tern_view = tern[w].rearrange("(c p n) -> p c n", p=128, n=2048)[:, c, :]
        tokB[k] = st.emit("sync", lambda E: E.dma_start(
            tern_view, g("tch_0")[:]).then_inc(s_dwB[0], 16), deps=[tokBv[k]])

    woT_sb, pt_sb, o_fmt = g("woT_sb"), g("pt_sb"), g("o_fmt")
    o_tm = [g("o_tm0"), g("o_tm1")]
    q_fm, kdup_fm, v1_tm = g("q_fm"), g("kdup_fm"), g("v1_tm")
    x_tm, ybuf, oraw = g("x_tm"), g("ybuf"), g("oraw")

    prepseq = [("B", k) for k in range(len(passA))]
    prep_state = {"B": 0}

    def prep_quantum():
        if prep_state["B"] < len(prepseq):
            kind, a = prepseq[prep_state["B"]]
            prep_state["B"] += 1
            emit_passB(a)

    pend2 = deque()

    def make_wo_quanta(B):
        qs = []
        for q in range(QB):
            i = B * QB + q
            j = i % 2

            def qA(E=None, B=B, q=q, i=i, pb=B % 2):
                def otp(E, q=q, pb=pb):
                    for c in range(NC):
                        last = E.transpose(ps_tr[:, c * 128:(c + 1) * 128],
                                           o_tm[pb][:, q * D + c * 128:
                                                     q * D + (c + 1) * 128],
                                           g("identb")[:])
                    return last

                t_ot = st.emit("tensor", otp,
                               deps=[t_oev[(B, h, q)] for h in range(H)]
                                    + [env["tok_identb"], chain2["ps_tr_r"]])
                t_otp[(B, q)] = t_ot

                def otrev(E):
                    for c in range(NC):
                        last = E.tensor_copy(o_fmt[:, c * 128:(c + 1) * 128],
                                             ps_tr[:, c * 128:(c + 1) * 128])
                    return last

                chain2["trev"] = chain2["ps_tr_r"] = st.emit(
                    "vector", otrev, deps=[t_ot, chain2["wo"]])

            def qB_(E=None, B=B, q=q, i=i, j=j):
                def womm(E):
                    E.wait_ge(s_dwo, 16)
                    for c in range(NC):
                        stat = o_fmt[:, c * 128:(c + 1) * 128]
                        wc = woT_sb[:, c * D:(c + 1) * D]
                        E.matmul(ps_wo[:, 0:512], stat, wc[:, 0:512],
                                 start=(c == 0), stop=(c == NC - 1))
                        last = E.matmul(ps_wo[:, 512:1024], stat, wc[:, 512:1024],
                                        start=(c == 0), stop=(c == NC - 1))
                    return last

                t_wo = st.emit("tensor", womm, deps=[chain2["trev"], chain2["ye"]])
                chain2["wo"] = t_wo

                def lnstats(E, j=j):
                    E.bn_stats(st6[j][:, 0:6], ps_wo[:, 0:512])
                    E.bn_stats(st6[j][:, 6:12], ps_wo[:, 512:1024])
                    E.drain()
                    E.bn_aggr(mvs[j][:], st6[j][:])
                    E.drain()
                    E.tensor_scalar(out=tny[j][:, 0:1], in0=mvs[j][:, 1:2],
                                    scalar1=EPS, scalar2=None, op0=Op.add)
                    E.drain()
                    return E.reciprocal(tny[j][:, 1:2], tny[j][:, 0:1])

                t_ls = st.emit("vector", lnstats, deps=[t_wo, t_as3.get(i - 2)])
                chain2["rs"] = st.emit("scalar", lambda E, j=j: E.activation(
                    out=rstd[j][:], in_=tny[j][:, 1:2], func=AF.Sqrt,
                    scale=np.float32(1.0)), deps=[t_ls])

            def qC_(E=None, B=B, q=q, i=i, j=j):
                t_n2 = st.emit("vector", lambda E, j=j: E.tensor_scalar(
                    out=negmr[j][:], in0=mvs[j][:, 0:1], scalar1=rstd[j][:],
                    scalar2=-1.0, op0=Op.mult, op1=Op.mult), deps=[chain2["rs"]])

                def yev(E, j=j):
                    return E.activation(out=ybuf[:], in_=ps_wo[:, 0:1024],
                                        func=AF.Identity,
                                        scale=rstd[j][:], bias=negmr[j][:])

                chain2["ye"] = t_ye = st.emit("scalar", yev,
                                              deps=[t_n2, chain2["resid"]])

                def resid(E, i=i):
                    return E.tensor_tensor(out=x_tm[:, i * D:(i + 1) * D],
                                           in0=x_tm[:, i * D:(i + 1) * D],
                                           in1=ybuf[:], op=Op.add)

                t_resid[i] = st.emit("vector", resid, deps=[t_ye])
                chain2["resid"] = t_resid[i]

                def xspill(E, i=i):
                    E.dma_start(g("xsp_h").rearrange("(t p) d -> p t d", p=128)[:, i, :],
                                x_tm[:, i * D:(i + 1) * D]).then_inc(s_dxs, 16)

                st.emit("sync", xspill, deps=[t_resid[i]])

            def qD_(E=None, B=B, q=q, i=i, j=j):
                def lnq_stats(E, i=i, j=j):
                    xx = x_tm[:, i * D:(i + 1) * D]
                    E.bn_stats(st6[j][:, 0:6], xx[:, 0:512])
                    E.bn_stats(st6[j][:, 6:12], xx[:, 512:1024])
                    E.drain()
                    E.bn_aggr(mvs[j][:], st6[j][:])
                    E.drain()
                    E.tensor_scalar(out=tny[j][:, 0:1], in0=mvs[j][:, 1:2],
                                    scalar1=EPS, scalar2=None, op0=Op.add)
                    E.drain()
                    return E.reciprocal(tny[j][:, 1:2], tny[j][:, 0:1])

                t_q_ls = st.emit("vector", lnq_stats,
                                 deps=[t_resid[i], t_as3.get(i - 2)])
                t_q_rs = st.emit("scalar", lambda E, j=j: E.activation(
                    out=rstd[j][:], in_=tny[j][:, 1:2], func=AF.Sqrt,
                    scale=np.float32(1.0)), deps=[t_q_ls])
                chain2["qnm"] = st.emit("vector", lambda E, j=j: E.tensor_scalar(
                    out=negmr[j][:], in0=mvs[j][:, 0:1], scalar1=rstd[j][:],
                    scalar2=-1.0, op0=Op.mult, op1=Op.mult), deps=[t_q_rs])

            def qE_(E=None, B=B, q=q, i=i, j=j):
                t_q_l2 = st.emit("scalar", lambda E, i=i, j=j: E.activation(
                    out=g(f"ln2_{j}")[:], in_=x_tm[:, i * D:(i + 1) * D],
                    func=AF.Identity, scale=rstd[j][:], bias=negmr[j][:]),
                    deps=[chain2["qnm"]])

                def lnq_scale(E, i=i, j=j):
                    E.tensor_reduce(out=tny[j][:, 2:3], in_=g(f"ln2_{j}")[:],
                                    axis=AX.X, op=Op.max,
                                    apply_absolute_value=True)
                    E.drain()
                    E.tensor_scalar(out=g("mxq")[:, i:i + 1], in0=tny[j][:, 2:3],
                                    scalar1=EPS, scalar2=None, op0=Op.max)
                    E.drain()
                    E.reciprocal(tny[j][:, 3:4], g("mxq")[:, i:i + 1])
                    E.drain()
                    return E.tensor_scalar(out=tny[j][:, 4:5], in0=tny[j][:, 3:4],
                                           scalar1=127.0, scalar2=None,
                                           op0=Op.mult)

                chain2["qqs"] = st.emit("vector", lnq_scale, deps=[t_q_l2])

            def qF_(E=None, B=B, q=q, i=i, j=j):
                def lnq_absign(E, j=j):
                    E.activation(out=g(f"sgq_{j}")[:], in_=g(f"ln2_{j}")[:],
                                 func=AF.Sign)
                    E.drain()
                    return E.activation(out=g(f"ln2_{j}")[:], in_=g(f"ln2_{j}")[:],
                                        func=AF.Abs, scale=tny[j][:, 4:5])

                t_as3[i] = st.emit("scalar", lnq_absign,
                                   deps=[chain2["qqs"], t_rq3.get(i - 2)])

                def lnq_round(E, j=j):
                    E.tensor_scalar(out=g("i16_0")[:], in0=g(f"ln2_{j}")[:],
                                    scalar1=g("delta")[:], scalar2=None, op0=Op.add)
                    E.drain()
                    return E.tensor_tensor(out=g("qtm_0")[:], in0=g("i16_0")[:],
                                           in1=g(f"sgq_{j}")[:], op=Op.mult)

                t_rq3[i] = st.emit("vector", lnq_round,
                                   deps=[t_as3[i], t_qtp3.get(i - 1)])

            def qG_(E=None, B=B, q=q, i=i, j=j):
                def lnq_tp(E):
                    for c in range(NC):
                        last = E.transpose(ps_tr[:, c * 128:(c + 1) * 128],
                                           g("qtm_0")[:, c * 128:(c + 1) * 128],
                                           g("identb")[:])
                    return last

                t_qtp3[i] = st.emit("tensor", lnq_tp,
                                    deps=[t_rq3[i], chain2["ps_tr_r"],
                                          env["tok_identb"]])

                def lnq_trev(E, i=i):
                    if i >= 1:
                        E.wait_ge(s_dq[0], 16 * i)
                    for c in range(NC):
                        last = E.tensor_copy(g("qst_0")[:, c * 128:(c + 1) * 128],
                                             ps_tr[:, c * 128:(c + 1) * 128])
                    return last

                t_qtrev3[i] = st.emit("vector", lnq_trev, deps=[t_qtp3[i]])
                chain2["ps_tr_r"] = t_qtrev3[i]

                st.emit("sync", lambda E, i=i: E.dma_start(
                    g("qsp_h")[i], g("qst_0")[:]).then_inc(s_dq[0], 16),
                    deps=[t_qtrev3[i]])

            qs.append([qA, qB_, qC_, qD_, qE_, qF_, qG_])
        pat = [(0, 0), (0, 1), (0, 2), (1, 0), (0, 3), (1, 1), (0, 4), (1, 2),
               (0, 5), (1, 3), (0, 6), (1, 4), (2, 0), (1, 5), (2, 1), (1, 6),
               (2, 2), (3, 0), (2, 3), (3, 1), (2, 4), (3, 2), (2, 5), (3, 3),
               (2, 6), (3, 4), (3, 5), (3, 6)]
        return [qs[q][k] for q, k in pat]

    t_exp = {}
    t_pv = {}
    t_oev = {}
    t_cpd = {}
    t_as3 = {}
    t_rq3 = {}
    t_qtp3 = {}
    t_qtrev3 = {}
    t_otp = {}
    t_resid = {}
    last_pv_block = {}
    chain2 = {"trev": None, "ye": None, "resid": None, "rs": None,
              "qnm": None, "qqs": None, "wo": None, "ps_tr_r": None}

    for B in range(NB):
        nkt = QB * (B + 1)
        for h in range(H):
            hb = 64 * (h % 2)
            chunk = h // 2
            for kt in range(nkt):
                lo = max(kt * 128, B * 512)
                hi = (B + 1) * 512
                segw = hi - lo
                par = kt % 3

                def strip(E, kt=kt, lo=lo, hi=hi, segw=segw, par=par, hb=hb, chunk=chunk):
                    dst = ps_bc if par == 2 else ps_st[par]
                    return E.matmul(dst[:, 0:segw],
                                    kdup_fm[hb:hb + 64, kt * 128:(kt + 1) * 128],
                                    q_fm[hb:hb + 64, chunk * S + lo: chunk * S + hi],
                                    start=True, stop=True)

                dstrip = []
                if kt >= 3:
                    dstrip.append(t_exp[(B, h, kt - 3)])
                elif h > 0:
                    dstrip.append(t_exp[(B, h - 1, nkt - 3 + kt)])
                elif B > 0:
                    dstrip.append(t_exp[(B - 1, H - 1, QB * B - 3 + kt)])
                elif par == 2:
                    dstrip.append(t_negb)
                t_s = st.emit("tensor", strip, deps=dstrip)

                def expf(E, h=h, kt=kt, lo=lo, segw=segw, par=par, B=B):
                    off = kt * 512 + (lo - B * 512)
                    src_ps = ps_bc if par == 2 else ps_st[par]
                    return E.activation(out=pt_sb[:, off:off + segw],
                                        in_=src_ps[:, 0:segw],
                                        func=AF.Exp, scale=np.float32(0.125),
                                        bias=g("negBtab")[:, h:h + 1])

                dexp = [t_s]
                if B == 0 and h == 0 and kt == 0:
                    dexp.append(t_negb)
                    dexp.append(t_ptms)
                if h > 0:
                    dexp.append(t_pv[(B, h - 1, QB - 1)])
                elif B > 0:
                    dexp.append(t_pv[(B - 1, H - 1, QB - 1)])
                t_exp[(B, h, kt)] = st.emit("scalar", expf, deps=dexp)

            t_masks = []
            for qd in range(QB):
                def mask(E, B=B, qd=qd):
                    off0 = (B * QB) * 512 + (B * QB * 128 - B * 512) + 640 * qd
                    ap = bass.AP(pt_sb.tensor if hasattr(pt_sb, "tensor") else pt_sb,
                                 off0, [[T * 512, 128], [1, 128]])
                    return E.affine_select(ap, ap, [[1, 128]], Op.is_ge, 0.0,
                                           base=0, channel_multiplier=-1)

                t_masks.append(st.emit("gpsimd", mask,
                                       deps=[t_exp[(B, h, B * QB + qd)]]))

            for q in range(QB):
                qt = B * QB + q
                par = q % 2

                def pv(E, B=B, qt=qt, par=par):
                    for kt in range(qt + 1):
                        off = kt * 512 + (qt * 128 - B * 512)
                        last = E.matmul(ps_o[par][:, 0:65], pt_sb[:, off:off + 128],
                                        v1_tm[:, kt * 66:kt * 66 + 65],
                                        start=(kt == 0), stop=(kt == qt))
                    return last

                dpv = t_masks[:q + 1] + [t_exp[(B, h, qt)]]
                if q >= 2:
                    dpv.append(t_cpd.get((B, h, q - 2)))
                elif h > 0:
                    dpv.append(t_cpd.get((B, h - 1, q + 2)))
                elif B > 0:
                    dpv.append(t_cpd.get((B - 1, H - 1, q + 2)))
                t_pv[(B, h, q)] = st.emit("tensor", pv, deps=dpv)

                dcp = [t_pv[(B, h, q)]]
                if h > 0:
                    dcp.append(t_oev.get((B, h - 1, q)))
                elif B > 0:
                    dcp.append(t_oev.get((B - 1, H - 1, q)))
                t_cpd[(B, h, q)] = st.emit("vector", lambda E, q=q, par=par:
                    E.tensor_copy(oraw[:, q * 65:(q + 1) * 65], ps_o[par][:, 0:65]),
                    deps=dcp)

            t_rc = st.emit("vector", lambda E: E.reciprocal(
                osum[1][:, 0:4], oraw[:].rearrange("p (q c) -> p q c", c=65)[:, :, 64]),
                deps=[t_cpd[(B, h, q)] for q in range(QB)])

            for q in range(QB):
                def oev(E, h=h, q=q, pb=B % 2):
                    return E.tensor_scalar(
                        out=o_tm[pb][:, q * D + h * 64: q * D + (h + 1) * 64],
                        in0=oraw[:, q * 65:q * 65 + 64],
                        scalar1=osum[1][:, q:q + 1], scalar2=None, op0=Op.mult)

                doev = [t_rc]
                if B > 1 and h == 0:
                    doev.append(t_otp.get((B - 2, q)))
                t_oev[(B, h, q)] = st.emit("vector", oev, deps=doev)

            prep_quantum()
            for _ in range(2 if h < 13 else 1):
                if pend2:
                    pend2.popleft()()

        last_pv_block[B] = t_pv[(B, H - 1, QB - 1)]

        pend2.extend(make_wo_quanta(B))

    while pend2:
        pend2.popleft()()
    while prep_state["B"] < len(prepseq):
        prep_quantum()

    def waitB(E):
        E.wait_ge(s_dwB[0], 16 * 48)
        E.wait_ge(s_dxs, 16 * T)
        E.wait_ge(s_dq[0], 16 * T)

    st.emit("sync", waitB, deps=[tokB[len(passA) - 1], tokB[len(passA) - 2]])

    if env["debug"]:
        dbg = env["dbg"]
        s_dbg = Ez(nc.semaphore("p2dbg"))

        def dbgdma(E):
            E.dma_start(dbg["dbg_x1"].rearrange("(t p) d -> p t d", p=128),
                        x_tm[:].rearrange("p (t d) -> p t d", d=D)).then_inc(s_dbg, 16)
            E.dma_start(dbg["dbg_negb"], g("negBtab")[:]).then_inc(s_dbg, 16)
            E.dma_start(dbg["dbg_pt"], pt_sb[:]).then_inc(s_dbg, 16)
            E.dma_start(dbg["dbg_otm"], o_tm[1][:]).then_inc(s_dbg, 16)
            E.dma_start(dbg["dbg_tern"], tern["g"]).then_inc(s_dbg, 16)
            E.wait_ge(s_dbg, 80)
            return E.nop()

        st.emit("gpsimd", dbgdma,
                deps=[t_resid[T - 1], t_negb, tokB[len(passA) - 1]])

    env["tok_resid_p2"] = dict(t_resid)
    st.build()


def build_p3b(nc, stack, env):
    st = Sched(nc, stack, "p3b")
    Ez = stack.enter_context
    s_qb = [Ez(nc.semaphore(f"p3bq{j}")) for j in range(2)]
    s_dt = [Ez(nc.semaphore(f"p3bt{j}")) for j in range(4)]
    s_tdc = [Ez(nc.semaphore(f"p3bd{j}")) for j in range(3)]
    s_dx = [Ez(nc.semaphore(f"p3bx{j}")) for j in range(2)]
    s_dy = Ez(nc.semaphore("p3by"))
    ps_gu = [Ez(nc.psum_tensor(f"p3gu{j}", [128, 512], F32)) for j in range(4)]
    ps_d = [Ez(nc.psum_tensor(f"p3d{j}", [128, 512], F32)) for j in range(2)]
    ps_sc = [Ez(nc.psum_tensor(f"p3sc{j}", [128, 512], F32)) for j in range(2)]
    g = env.__getitem__
    st6 = [g("st6_0"), g("st6_1")]
    mvs = [g("mv_0"), g("mv_1")]
    rstd = [g("rstd_0"), g("rstd_1")]
    negmr = [g("negmr_0"), g("negmr_1")]
    tny = [g("tny_0"), g("tny_1")]
    tern = env["tern_h"]
    s4 = g("s4")

    t_qdma = {}
    t_tdma = {}
    t_mm_gu = {}
    t_ev = {}
    t_h1 = {}
    t_h = {}
    t_mx = {}
    t_habs = {}
    t_mt = {}
    t_cs4 = {}
    t_csrow = {}
    t_bcm = {}
    t_bce = {}
    t_m1 = {}
    t_sg = {}
    t_rds = {}
    t_fill = {}
    t_fev = {}
    t_fev2 = {}
    t_y3 = {}
    t_store = {}
    t_xr = {}
    chain = {"r3b": None}
    nqb = [0, 0]
    ndt = [0] * 4
    ndc = [0] * 3
    pend = deque()

    def emit_gu(b3, ic):
        par = ic % 4
        par_b = b3 % 2
        jj = ic % 2

        def dma_gu(E, ic=ic, par=par):
            gv = tern["g"].rearrange("(a p n) -> a p n", p=128, n=NC * 128)[ic]
            uv = tern["u"].rearrange("(a p n) -> a p n", p=128, n=NC * 128)[ic]
            E.dma_start(g(f"tg_{par}")[:], gv).then_inc(s_dt[par], 16)
            E.dma_start(g(f"tu_{par}")[:], uv).then_inc(s_dt[par], 16)

        t_tdma[(b3, ic)] = st.emit("sync", dma_gu,
                                   deps=[t_mm_gu.get((b3, ic - 4)),
                                         t_mm_gu.get((b3 - 1, IC - 4 + ic))
                                         if ic < 4 else None])
        ndt[par] += 2

        def mm_gu(E, par=par, par_b=par_b, dtt=16 * ndt[par], qbt=16 * nqb[par_b]):
            pp = par % 2
            E.wait_ge(s_dt[par], dtt)
            E.wait_ge(s_qb[par_b], qbt)
            qb = g(f"qb_{par_b}")[:].rearrange("p (t n) -> p t n", n=1024)
            for c2 in range(NC):
                mov = qb[:, :, c2 * 128:(c2 + 1) * 128]
                E.matmul(ps_gu[pp][:, 0:TB], g(f"tg_{par}")[:, c2 * 128:(c2 + 1) * 128],
                         mov, start=(c2 == 0), stop=(c2 == NC - 1))
                last = E.matmul(ps_gu[2 + pp][:, 0:TB],
                                g(f"tu_{par}")[:, c2 * 128:(c2 + 1) * 128],
                                mov, start=(c2 == 0), stop=(c2 == NC - 1))
            return last

        dmm = list(t_ev.get((b3, ic - 2)) or ())
        if ic < 2:
            dmm += list(t_ev.get((b3 - 1, IC - 2 + ic)) or ())
        t_mm_gu[(b3, ic)] = st.emit("tensor", mm_gu, deps=dmm)

        dev = [t_mm_gu[(b3, ic)], t_h.get((b3, ic - 2))]
        if ic < 2:
            dev.append(t_h.get((b3 - 1, IC - 2 + ic)))
        t_eg = st.emit("scalar", lambda E, pp=jj: E.activation(
            out=g(f"rb_{pp}")[:], in_=ps_gu[pp][:, 0:TB], func=AF.Relu,
            scale=np.float32(-1.0)), deps=dev)
        t_eu = st.emit("scalar", lambda E, pp=jj: E.activation(
            out=g(f"ub_{pp}")[:], in_=ps_gu[2 + pp][:, 0:TB], func=AF.Identity),
            deps=dev)
        t_ev[(b3, ic)] = (t_eg, t_eu)

        t_h1[(b3, ic)] = st.emit("gpsimd", lambda E, pp=jj: E.tensor_tensor(
            out=g(f"ub_{pp}")[:], in0=g(f"rb_{pp}")[:], in1=g(f"ub_{pp}")[:],
            op=Op.mult), deps=[t_eg, t_eu, t_m1.get((b3 - 1, ic))])
        t_h[(b3, ic)] = st.emit("gpsimd", lambda E, ic=ic, pp=jj: E.tensor_tensor(
            out=g("h_sb")[:, ic * TB:(ic + 1) * TB], in0=g(f"ub_{pp}")[:],
            in1=g(f"rb_{pp}")[:], op=Op.mult), deps=[t_h1[(b3, ic)]])

        dab = [t_h[(b3, ic)], t_mx.get((b3, ic - 2))]
        if ic < 2:
            dab.append(t_mx.get((b3 - 1, IC - 2 + ic)))
        t_habs[(b3, ic)] = st.emit("scalar", lambda E, ic=ic: E.activation(
            out=g(f"habs_{ic % 2}")[:], in_=g("h_sb")[:, ic * TB:(ic + 1) * TB],
            func=AF.Abs), deps=dab)

    def emit_mxup(b3, ic):
        dmx = [t_habs[(b3, ic)], t_mx.get((b3, ic - 1))]
        pr = ic % 2
        if ic == 0:
            dmx.append(t_mt.get(b3 - 1))
            fn = lambda E, pr=pr: E.tensor_copy(g("mxacc")[:], g(f"habs_{pr}")[:])
        else:
            fn = lambda E, pr=pr: E.tensor_tensor(out=g("mxacc")[:], in0=g("mxacc")[:],
                                                  in1=g(f"habs_{pr}")[:], op=Op.max)
        t_mx[(b3, ic)] = st.emit("vector", fn, deps=dmx)

    def emit_scales(b3):
        def mxtp(E):
            for q4 in range(4):
                last = E.transpose(ps_sc[0][:, q4 * 128:(q4 + 1) * 128],
                                   g("mxacc")[:, q4 * 128:(q4 + 1) * 128],
                                   g("identf")[:])
            return last

        t_mt[b3] = st.emit("tensor", mxtp,
                           deps=[t_mx[(b3, IC - 1)], env["tok_ident"],
                                 t_bce.get(b3 - 1)] + list(t_ev[(b3, IC - 1)]))

        def mx4(E):
            for tt in range(4):
                last = E.tensor_reduce(out=s4[:, tt:tt + 1],
                                       in_=ps_sc[0][:, tt * 128:(tt + 1) * 128],
                                       axis=AX.X, op=Op.max)
            return last

        t_m4 = st.emit("vector", mx4, deps=[t_mt[b3], t_cs4.get(b3 - 1)])

        def csc4(E, b3=b3):
            mq = g("mxq")[:, 4 * b3:4 * b3 + 4]
            E.tensor_scalar(out=s4[:, 4:8], in0=mq, scalar1=1.0 / 127.0,
                            scalar2=None, op0=Op.mult)
            E.drain()
            E.tensor_tensor(out=s4[:, 8:12], in0=s4[:, 4:8], in1=s4[:, 4:8], op=Op.mult)
            E.drain()
            E.tensor_tensor(out=s4[:, 8:12], in0=s4[:, 8:12], in1=s4[:, 4:8], op=Op.mult)
            E.drain()
            E.tensor_scalar(out=s4[:, 8:12], in0=s4[:, 8:12], scalar1=g("beta_gu2")[:],
                            scalar2=None, op0=Op.mult)
            E.drain()
            E.tensor_tensor(out=s4[:, 4:8], in0=s4[:, 8:12], in1=s4[:, 0:4], op=Op.mult)
            E.drain()
            E.tensor_scalar(out=s4[:, 4:8], in0=s4[:, 4:8], scalar1=EPS,
                            scalar2=None, op0=Op.max)
            E.drain()
            E.reciprocal(s4[:, 0:4], s4[:, 4:8])
            E.drain()
            E.tensor_scalar(out=s4[:, 8:12], in0=s4[:, 8:12], scalar1=127.0,
                            scalar2=None, op0=Op.mult)
            E.drain()
            E.tensor_tensor(out=s4[:, 12:16], in0=s4[:, 8:12], in1=s4[:, 0:4],
                            op=Op.mult)
            E.drain()
            PA = 16 + 8 * (b3 % 2)
            E.tensor_scalar(out=s4[:, PA:PA + 4], in0=s4[:, 4:8],
                            scalar1=g("beta_d")[:], scalar2=1.0 / 127.0,
                            op0=Op.mult, op1=Op.mult)
            E.drain()
            return E.tensor_tensor(out=s4[:, PA + 4:PA + 8], in0=s4[:, PA:PA + 4],
                                   in1=s4[:, PA:PA + 4], op=Op.mult)

        dcs = [t_m4, t_csrow.get(b3 - 1), t_y3.get((b3 - 2, 3))]
        t_cs4[b3] = st.emit("vector", csc4, deps=dcs)

        def cstp(E):
            for tt in range(4):
                last = E.transpose(ps_sc[1][:1, tt * 128:(tt + 1) * 128],
                                   s4[:, 12 + tt:13 + tt], g("identf")[:])
            return last

        t_ctp = st.emit("tensor", cstp, deps=[t_cs4[b3], t_csrow.get(b3 - 1)])
        t_csrow[b3] = st.emit("vector", lambda E: E.tensor_copy(
            g("csrow")[:], ps_sc[1][:1, 0:512]), deps=[t_ctp, t_bcm.get(b3 - 1)])
        t_bcm[b3] = st.emit("tensor", lambda E: E.matmul(
            ps_sc[0][:, 0:TB], g("ones_row_f")[:], g("csrow")[:],
            start=True, stop=True), deps=[t_csrow[b3], t_m4])
        t_bce[b3] = st.emit("vector", lambda E: E.tensor_copy(
            g("csb")[:], ps_sc[0][:, 0:TB]),
            deps=[t_bcm[b3], t_m1.get((b3 - 1, IC - 1))])

    def emit_m1_sg(b3, ic):
        pr = ic % 2
        dm1 = [t_bce[b3], t_rds.get((b3, ic - 2))]
        if ic < 2:
            dm1.append(t_rds.get((b3 - 1, IC - 2 + ic)))
        t_m1[(b3, ic)] = st.emit("vector", lambda E, ic=ic, pr=pr: E.tensor_tensor(
            out=g(f"m1b_{pr}")[:], in0=g("h_sb")[:, ic * TB:(ic + 1) * TB],
            in1=g("csb")[:], op=Op.mult), deps=dm1)

        def sgabs(E, pr=pr):
            E.activation(out=g(f"sgb_{pr}")[:], in_=g(f"m1b_{pr}")[:], func=AF.Sign)
            E.drain()
            return E.activation(out=g(f"m1b_{pr}")[:], in_=g(f"m1b_{pr}")[:],
                                func=AF.Abs)

        t_sg[(b3, ic)] = st.emit("scalar", sgabs, deps=[t_m1[(b3, ic)]])

    def emit_rnd2(b3, ic):
        pr = ic % 2

        def rnd2(E, ic=ic, pr=pr):
            E.tensor_scalar(out=g(f"i16f_{pr}")[:], in0=g(f"m1b_{pr}")[:],
                            scalar1=g("delta")[:], scalar2=None, op0=Op.add)
            E.drain()
            return E.tensor_tensor(out=g("hq_sb")[:, ic * TB:(ic + 1) * TB],
                                   in0=g(f"i16f_{pr}")[:], in1=g(f"sgb_{pr}")[:],
                                   op=Op.mult)

        t_rds[(b3, ic)] = st.emit("vector", rnd2,
                                  deps=[t_m1[(b3, ic)], t_sg[(b3, ic)],
                                        t_fill.get((b3 - 1, ic // CH, 3, 1))])

    def emit_tdc_dma(j):
        sl = j % 3
        dprev = None
        if j >= 3:
            bp, rp = divmod(j - 3, 8)
            cp, dhp = divmod(rp, 2)
            dprev = t_fill.get((bp, cp, 3, dhp))
        bj, rj = divmod(j, 8)
        c, dh = divmod(rj, 2)

        def dmaf(E, c=c, dh=dh, sl=sl):
            dv = tern["d"].rearrange("(h c p n) -> h c p n",
                                     c=NCH, p=128, n=CH * 512)[dh, c]
            E.dma_start(g(f"tdc_{sl}")[:], dv).then_inc(s_tdc[sl], 16)

        st.emit("sync", dmaf, deps=[dprev])
        ndc[sl] += 1

    def emit_fill(b3, c, tg, dh):
        j = b3 * 8 + c * 2 + dh
        sl = j % 3
        f = c * 8 + tg * 2 + dh
        bank = f % 2

        def mm(E, c=c, tg=tg, sl=sl, bank=bank, dct=16 * ndc[sl]):
            E.wait_ge(s_tdc[sl], dct)
            for icl in range(CH):
                ic = c * CH + icl
                last = E.matmul(
                    ps_d[bank][:, 0:TB],
                    g("hq_sb")[:, ic * TB + tg * 128: ic * TB + (tg + 1) * 128],
                    g(f"tdc_{sl}")[:, icl * 512:(icl + 1) * 512],
                    start=(icl == 0), stop=(icl == CH - 1))
            return last

        dmm = [t_rds[(b3, c * CH + CH - 1)], t_fev.get((b3, f - 2))]
        if f < 2:
            dmm.append(t_fev.get((b3 - 1, 30 + f)))
        t_fill[(b3, c, tg, dh)] = tok = st.emit("tensor", mm, deps=dmm)

        dev = [tok]
        if c == 0:
            def ev(E, tg=tg, dh=dh, bank=bank):
                return E.activation(
                    out=g("d_tm")[:, tg * D + dh * 512: tg * D + dh * 512 + 512],
                    in_=ps_d[bank][:, 0:TB], func=AF.Identity)

            dev.append(t_y3.get((b3 - 1, tg)))
            eng = "scalar"
        else:
            def ev(E, c=c, tg=tg, dh=dh, bank=bank):
                dst = g("d_tm")[:, tg * D + dh * 512: tg * D + dh * 512 + 512]
                return E.tensor_tensor(out=dst, in0=dst, in1=ps_d[bank][:, 0:TB],
                                       op=Op.add)

            dev.append(t_fev2.get((b3, c - 1, tg, dh)))
            eng = "vector"
        t_fev[(b3, f)] = t_fev2[(b3, c, tg, dh)] = st.emit(eng, ev, deps=dev)

        if tg == 3 and j + 3 < NB3 * 8:
            emit_tdc_dma(j + 3)

    def make_tail_quanta(b3):
        qs = []
        for tt in range(4):
            i = 4 * b3 + tt
            j = i % 2
            hold = {}

            def qa(b3=b3, tt=tt, i=i, j=j, hold=hold):
                def xdma(E, i=i, j=j):
                    if i >= 2:
                        E.wait_ge(s_dy, 16 * i)
                    E.dma_start(g(f"xrl_{j}")[:],
                                g("xsp_h").rearrange("(t p) d -> p t d", p=128)[:, i, :]
                                ).then_inc(s_dx[j], 16)

                t_xr[(b3, tt)] = st.emit("sync", xdma, deps=[t_store.get(i - 2)])

                def lnst3(E, tt=tt, j=j, PA=16 + 8 * (b3 % 2)):
                    dd = g("d_tm")[:, tt * D:(tt + 1) * D]
                    E.bn_stats(st6[j][:, 0:6], dd[:, 0:512])
                    E.bn_stats(st6[j][:, 6:12], dd[:, 512:1024])
                    E.drain()
                    E.bn_aggr(mvs[j][:], st6[j][:])
                    E.drain()
                    E.tensor_scalar(out=tny[j][:, 0:1], in0=mvs[j][:, 1:2],
                                    scalar1=s4[:, PA + 4 + tt:PA + 5 + tt],
                                    scalar2=EPS, op0=Op.mult, op1=Op.add)
                    E.drain()
                    return E.reciprocal(tny[j][:, 1:2], tny[j][:, 0:1])

                t_l3 = st.emit("vector", lnst3,
                               deps=[t_fev2[(b3, NCH - 1, tt, 0)],
                                     t_fev2[(b3, NCH - 1, tt, 1)], t_cs4[b3]])
                t_r3 = st.emit("scalar", lambda E, j=j: E.activation(
                    out=rstd[j][:], in_=tny[j][:, 1:2], func=AF.Sqrt,
                    scale=np.float32(1.0)), deps=[t_l3])
                hold["toks"] = (t_l3, t_r3)

            def qb_(b3=b3, tt=tt, i=i, j=j, hold=hold):
                t_l3, t_r3 = hold["toks"]

                def n3f(E, j=j, tt=tt, PA=16 + 8 * (b3 % 2)):
                    E.tensor_tensor(out=tny[j][:, 2:3], in0=rstd[j][:],
                                    in1=s4[:, PA + tt:PA + tt + 1], op=Op.mult)
                    E.drain()
                    return E.tensor_scalar(out=negmr[j][:], in0=mvs[j][:, 0:1],
                                           scalar1=tny[j][:, 2:3], scalar2=-1.0,
                                           op0=Op.mult, op1=Op.mult)

                t_n3 = st.emit("vector", n3f, deps=[t_r3])
                t_y3[(b3, tt)] = st.emit("scalar", lambda E, tt=tt, j=j: E.activation(
                    out=g("ybuf3")[:], in_=g("d_tm")[:, tt * D:(tt + 1) * D],
                    func=AF.Identity, scale=tny[j][:, 2:3], bias=negmr[j][:]),
                    deps=[t_n3, chain["r3b"]])

                def resid3(E, j=j, xrt=16 * ((i // 2) + 1)):
                    E.wait_ge(s_dx[j], xrt)
                    return E.tensor_tensor(out=g(f"xrl_{j}")[:], in0=g(f"xrl_{j}")[:],
                                           in1=g("ybuf3")[:], op=Op.add)

                t_r3b = st.emit("vector", resid3, deps=[t_y3[(b3, tt)], t_xr[(b3, tt)]])
                chain["r3b"] = t_r3b

                def store(E, i=i, j=j):
                    E.dma_start(env["y_h"].rearrange("(t p) d -> p t d", p=128)[:, i, :],
                                g(f"xrl_{j}")[:]).then_inc(s_dy, 16)

                t_store[i] = st.emit("sync", store, deps=[t_r3b])

            qs.append(qa)
            qs.append(qb_)
        return qs

    for j in range(3):
        emit_tdc_dma(j)

    for b3 in range(NB3):
        par_b = b3 % 2

        def qdma(E, b3=b3, par_b=par_b):
            for tt in range(4):
                E.dma_start(g(f"qb_{par_b}")[:, tt * 1024:(tt + 1) * 1024],
                            g("qsp_h")[4 * b3 + tt]).then_inc(s_qb[par_b], 16)

        t_qdma[b3] = st.emit("sync", qdma, deps=[t_mm_gu.get((b3 - 2, IC - 1))])
        nqb[par_b] += 4

        fq = deque((b3 - 1, c, tg, dh) for c in range(NCH - 1)
                   for tg in range(4) for dh in range(2)) if b3 > 0 else deque()
        for ic in range(IC):
            if b3 > 0:
                emit_m1_sg(b3 - 1, ic)
            emit_gu(b3, ic)
            if b3 > 0 and ic > 0:
                emit_rnd2(b3 - 1, ic - 1)
            if ic > 0:
                emit_mxup(b3, ic - 1)
            if pend:
                pend.popleft()()
            if fq and fq[0][1] * CH + CH <= ic:
                emit_fill(*fq.popleft())
        emit_mxup(b3, IC - 1)
        while pend:
            pend.popleft()()
        if b3 > 0:
            emit_rnd2(b3 - 1, IC - 1)
            while fq:
                emit_fill(*fq.popleft())
            for tg in range(2):
                for dh in range(2):
                    emit_fill(b3 - 1, NCH - 1, tg, dh)
            emit_scales(b3)
            for tg in range(2, 4):
                for dh in range(2):
                    emit_fill(b3 - 1, NCH - 1, tg, dh)
            pend.extend(make_tail_quanta(b3 - 1))
        else:
            emit_scales(b3)

    b3 = NB3 - 1
    fq = deque((b3, c, tg, dh) for c in range(NCH - 1)
               for tg in range(4) for dh in range(2))
    for ic in range(IC):
        emit_m1_sg(b3, ic)
        if ic > 0:
            emit_rnd2(b3, ic - 1)
        if pend:
            pend.popleft()()
        if fq and fq[0][1] * CH + CH <= ic:
            emit_fill(*fq.popleft())
    emit_rnd2(b3, IC - 1)
    while fq:
        emit_fill(*fq.popleft())
    while pend:
        pend.popleft()()
    for tg in range(4):
        for dh in range(2):
            emit_fill(b3, NCH - 1, tg, dh)
    for q in make_tail_quanta(b3):
        q()

    st.emit("sync", lambda E: E.wait_ge(s_dy, 16 * T))
    st.build()


_CACHE = {}


def _build_cached(debug=False):
    key = ("nc", debug)
    if key not in _CACHE:
        _CACHE[key] = build_nc(debug=debug)
    return _CACHE[key]


def _host_layout(wq, wk, wv, wo, wg, wu, wd):
    perm = np.concatenate([np.arange(0, 64, 2), np.arange(1, 64, 2)])
    rows = np.concatenate([h * 64 + perm for h in range(H)])
    wall = np.empty((D, D + 2 * HD), np.float32)
    wall[:, 0:D] = wq[rows, :].T
    wall[:, D:D + HD] = wk[:, perm]
    wall[:, D + HD:] = wv
    woT = np.ascontiguousarray(wo.T)
    wgT = np.ascontiguousarray(
        wg.T.reshape(NC, 128, IC, 128).transpose(2, 1, 0, 3))
    wuT = np.ascontiguousarray(
        wu.T.reshape(NC, 128, IC, 128).transpose(2, 1, 0, 3))
    wdT = np.ascontiguousarray(
        wd.T.reshape(NCH, CH, 128, 2, 512).transpose(3, 0, 2, 1, 4)).reshape(I, D)
    return wall, woT, wgT, wuT, wdT


def make_in_maps(x, wq, wk, wv, wo, wg, wu, wd):
    wall, woT, wgT, wuT, wdT = _host_layout(wq, wk, wv, wo, wg, wu, wd)
    in_maps = []
    for c in range(x.shape[0]):
        in_maps.append({
            "x": np.ascontiguousarray(x[c]),
            "xT": np.ascontiguousarray(x[c].T),
            "wall": wall, "woT": woT, "wgT": wgT, "wuT": wuT, "wdT": wdT,
        })
    return in_maps


def kernel(L0_attn_x, L0_attn_wq, L0_attn_wk, L0_attn_wv, L0_attn_wo,
           L0_ffn_w_gate, L0_ffn_w_up, L0_ffn_w_down):
    x = np.ascontiguousarray(np.asarray(L0_attn_x, np.float32))
    in_maps = make_in_maps(
        x, np.asarray(L0_attn_wq, np.float32), np.asarray(L0_attn_wk, np.float32),
        np.asarray(L0_attn_wv, np.float32), np.asarray(L0_attn_wo, np.float32),
        np.asarray(L0_ffn_w_gate, np.float32), np.asarray(L0_ffn_w_up, np.float32),
        np.asarray(L0_ffn_w_down, np.float32))
    nc, _ = _build_cached(debug=False)
    res = run_bass_kernel_spmd(nc, in_maps, core_ids=list(range(8)))
    return np.stack([res.results[c]["y"] for c in range(8)], axis=0)
